# version 03
# baseline (speedup 1.0000x reference)
import numpy as np
from contextlib import ExitStack

import concourse.bacc as bacc
import concourse.mybir as mybir
from concourse.bass_utils import run_bass_kernel_spmd
from concourse.library_config import mlp as _LIB_MLP

F32 = mybir.dt.float32
BF16 = mybir.dt.bfloat16
I16 = mybir.dt.int16
AF = mybir.ActivationFunctionType
OP = mybir.AluOpType
AXX = mybir.AxisListType.X

D = 1024
T = 2048
NE = 8
CAP = 640
TRASH = NE * CAP
EPS = 1e-5
BIG = 1.0e9
ENGS = ("pe", "ve", "se", "gp", "sy")
RG_PAIRS = [[0, 1], [2, 3], [4, 5], [6, 7]]


class B:
    def __init__(self, nc, es):
        self.nc = nc
        self.es = es
        self.q = {e: [] for e in ENGS}
        self.sems = {}
        self.cnt = {}
        self._lw = {e: {} for e in ENGS}

    def sem(self, name):
        self.sems[name] = self.es.enter_context(self.nc.semaphore(name))
        self.cnt[name] = 0

    AUTO_SER = ("ve", "se", "gp")

    def emit(self, eng, fn, wait=(), inc=(), async_op=False):
        w = []
        lw = self._lw[eng]
        if eng in self.AUTO_SER and self.cnt[eng] > 0:
            wait = ((eng, self.cnt[eng]),) + tuple(wait)
        for s, v in wait:
            if v > 0 and lw.get(s, -1) < v:
                lw[s] = v
                w.append((s, v))
        if eng in self.AUTO_SER and not async_op:
            inc = tuple((s, n) for s, n in inc if s != eng) + ((eng, 1),)
        for s, n in inc:
            self.cnt[s] += n
        self.q[eng].append((tuple(w), fn, tuple(inc)))

    def now(self, s):
        return self.cnt[s]

    def dma(self, eng, sem, out, in_, wait=(), slow=False):
        self.emit(eng, lambda e, o=out, i=in_, sl=slow:
                  e.dma_start(out=o, in_=i, allow_slow_non_contiguous=sl),
                  wait=wait, inc=((sem, 16),), async_op=True)
        return self.cnt[sem]

    def build(self):
        block = self.es.enter_context(self.nc.Block())

        def mk(name):
            def body(e):
                for waits, fn, incs in self.q[name]:
                    for s, v in waits:
                        e.wait_ge(self.sems[s], v)
                    inst = fn(e)
                    for s, n in incs:
                        if inst is not None:
                            inst.then_inc(self.sems[s], n)
            return body

        block.tensor(mk("pe"))
        block.vector(mk("ve"))
        block.scalar(mk("se"))
        block.gpsimd(mk("gp"))
        block.sync(mk("sy"))


class Ring:

    def __init__(self, b, slots):
        self.b = b
        self.slots = slots
        self.free_at = [None] * len(slots)
        self.i = 0

    def acquire(self):
        i = self.i
        self.i = (self.i + 1) % len(self.slots)
        w = (self.free_at[i],) if self.free_at[i] else ()
        return i, self.slots[i], w

    def release(self, i, eng):
        self.free_at[i] = (eng, self.b.now(eng))

    def drain_waits(self):
        return tuple(fa for fa in self.free_at if fa)


SKIP = set()


def build_program(dbg=False, upto='F'):
    nc = bacc.Bacc("TRN2", target_bir_lowering=False, debug=False)
    es = ExitStack()
    b = B(nc, es)
    dbg_t = {}

    def din(name, shape):
        return nc.dram_tensor(name, list(shape), F32, kind="ExternalInput").ap()

    x_fm = din("x_fm", [D, 2051])
    x_tm = din("x_tm", [T, D])
    wproj = din("wproj", [24, 128, 8, 128])
    convw = din("convw", [128, 12, 4])
    convb = din("convb", [128, 12])
    fl_in = din("fl", [128, 12])
    wgates = din("wgates", [12, 128, 2, 12, 128])
    gpb = din("gpb", [128, 12])
    ipb = din("ipb", [128, 12])
    wout = din("wout", [8, 128, 12, 128])
    wr1 = din("wr1", [32, 128, 8, 128])
    rb1 = din("rb1", [128, 32])
    wr2 = din("wr2", [32, 128, 32, 128])
    rb2 = din("rb2", [128, 32])
    wr3 = din("wr3", [128, 32, 8])
    rb3 = din("rb3", [128, 8])
    we1 = din("we1", [NE, 8, 128, 4, 8, 128])
    eb1 = din("eb1", [NE, 128, 32])
    we2 = din("we2", [NE, 128, 32, D])
    eb2 = din("eb2", [NE, 128, D])
    bflag = din("bflag", [128, 1])
    out_t = nc.dram_tensor("out", [T, D], F32, kind="ExternalOutput").ap()

    rw2bf = nc.dram_tensor("rw2bf", [32, 128, 32, 128], BF16).ap()
    rw1bf = nc.dram_tensor("rw1bf", [32, 128, 8, 128], BF16).ap()
    disp_d = nc.dram_tensor("disp_d", [TRASH + 2, D], BF16).ap()
    eo_d = nc.dram_tensor("eo_d", [TRASH + 1, D], BF16).ap()
    xln_d = nc.dram_tensor("xln_d", [T, D], F32).ap()
    idx_d = nc.dram_tensor("idx_d", [6, 128, 128], I16).ap()
    cs_car = nc.dram_tensor("cs_car", [128, 12], F32)
    cg_car = nc.dram_tensor("cg_car", [2, 128, 12], F32)
    cs_cnt = nc.dram_tensor("cs_cnt", [8], F32)
    cg_cnt = nc.dram_tensor("cg_cnt", [2, 8], F32)
    if dbg:
        for nm, shp in [("d_gate", [128, 12, T]), ("d_xc", [128, 12, T]),
                        ("d_z", [128, 12, T]), ("d_attn", [128, 8, T]),
                        ("d_xln", [T, D]), ("d_sel", [128, 16, 8]),
                        ("d_pem", [8, T]), ("d_disp", [TRASH + 2, D]),
                        ("d_eo", [TRASH + 1, D]), ("d_g01", [2, 128, 16, D]),
                        ("dbh1", [128, 32, 512]), ("dbh2", [128, 32, 512]),
                        ("dbpp", [128, 16, 8])]:
            dbg_t[nm] = nc.dram_tensor(nm, shp, F32, kind="ExternalOutput").ap()

    for s in ["Wmisc", "Wpj0", "Wpj1", "Wpjf", "Wg0", "Wg1", "Wgf", "Wo0", "Wo1", "Wof",
              "Wr1a", "Wr1b", "Wr1c", "Wr1f", "Wr2c", "Wr2a", "Wr2b", "Wr2f", "We1a", "We1b", "We1f", "We2",
              "Web2", "Xf", "Xt0", "Xt1", "Xtf", "Spl", "Rl0", "Rl1", "Rlf",
              "Zer", "Dl0", "Dl1", "Eo0", "Eo1", "Out0", "Out1", "Idx", "Sca",
              "Gat", "CC", "CCc", "Dbg", "pe", "ve", "se", "gp", "sy"]:
        b.sem(s)

    SEM_OUT = ("Out0", "Out1")
    SEM_EO = ("Eo0", "Eo1")
    SEM_XT = ("Xt0", "Xt1")
    SEM_RL = ("Rl0", "Rl1")
    SEM_DL = ("Dl0", "Dl1")

    PA = lambda name, shape, dt=F32: es.enter_context(nc.sbuf_tensor(name, shape, dt))
    sb_convw = PA("c_convw", [128, 12, 4])
    sb_convb = PA("c_convb", [128, 12])
    sb_fl = PA("c_fl", [128, 12])
    sb_sfl = PA("c_sfl", [128, 12])
    sb_gpb = PA("c_gpb", [128, 12])
    sb_ipb = PA("c_ipb", [128, 12])
    sb_rb1 = PA("c_rb1", [128, 32])
    sb_rb2 = PA("c_rb2", [128, 32])
    sb_rb3 = PA("c_rb3", [128, 8])
    sb_eb1 = PA("c_eb1", [128, NE, 32])
    sb_bf = PA("c_bf", [128, 1])
    sb_io8 = PA("c_io8", [128, 8])
    sb_idb = PA("c_idb", [128, 128], BF16)
    sb_idf = PA("c_idf", [128, 128])
    sb_ones = PA("c_ones", [128, 128])
    sb_carry = PA("c_carry", [128, 12])
    sb_carin = PA("c_carin", [128, 12])
    sb_cnt = PA("c_cnt", [8, 1])
    sb_crow = PA("c_crow", [1, 8])
    sb_cntbc = PA("c_cntbc", [128, 8])
    sb_s0 = PA("c_s0", [128, 16])
    sb_s1 = PA("c_s1", [128, 16])
    sb_e0 = PA("c_e0", [128, 16])
    sb_e1 = PA("c_e1", [128, 16])
    sb_ps0 = PA("c_ps0", [128, 16])
    sb_ps1 = PA("c_ps1", [128, 16])
    sb_sl0 = PA("c_sl0", [128, 16])
    sb_sl1 = PA("c_sl1", [128, 16])
    sb_oh0 = PA("c_oh0", [128, 16, 8])
    sb_oh1 = PA("c_oh1", [128, 16, 8])
    sb_w0 = PA("c_w0", [128, 16])
    sb_w1 = PA("c_w1", [128, 16])
    sb_pin = PA("c_pin", [8, 1])
    sb_trash = PA("c_trash", [128, 1])
    sb_beps = PA("c_beps", [128, 1])
    sb_zero = PA("c_zero", [128, D], BF16)
    sb_t16 = PA("c_t16", [16, 128], I16)
    sb_i16a = PA("c_i16a", [128, 128], I16)
    sb_i16c = PA("c_i16c", [128, 128], I16)
    sb_i16d = PA("c_i16d", [128, 128], I16)
    sb_i16b = PA("c_i16b", [128, 128], I16)
    sb_wr3 = PA("c_wr3", [128, 32, 8], BF16)
    COL = [PA(f"c_col{i}", [128, 1]) for i in range(6)]
    ROW8 = [PA(f"c_row8{i}", [128, 8]) for i in range(4)]

    pA = es.enter_context(nc.psum_tensor("pA", [128, 1024], F32))
    pB = es.enter_context(nc.psum_tensor("pB", [128, 1024], F32))
    pC = es.enter_context(nc.psum_tensor("pC", [128, 1024], F32))
    pT1 = es.enter_context(nc.psum_tensor("pT1", [128, 1024], BF16))
    pT2 = es.enter_context(nc.psum_tensor("pT2", [128, 1024], BF16))
    ring = Ring(b, [pA[:, 0:512], pA[:, 512:1024], pB[:, 0:512],
                    pB[:, 512:1024], pC[:, 0:512], pC[:, 512:1024]])


    _stacks = []

    def _finish_early():
        for s_ in reversed(_stacks):
            s_.close()
        _stacks.clear()
        waits = []
        for snm in ["Wmisc", "Wpj0", "Wpj1", "Wg0", "Wg1", "Wo0", "Wo1", "Wr1c",
                    "Wr2c", "Wr2a", "Wr2b", "Wr1a", "Wr1b", "We1a", "We1b", "We2",
                    "Web2", "Xf", "Xt0", "Xt1", "Spl", "Rl0", "Rl1", "Zer",
                    "Dl0", "Dl1", "Eo0", "Eo1", "Out0", "Out1", "Idx", "Sca",
                    "Gat", "CC", "Dbg"]:
            v = b.now(snm)
            if v > 0:
                waits.append((snm, v))
        b.emit("sy", lambda e: e.nop(), wait=tuple(waits))
        b.build()
        es.close()
        nc.finalize()
        return nc
    def ve(fn, wait=(), inc=()):
        b.emit("ve", fn, wait=wait, inc=inc)

    def se(fn, wait=(), inc=()):
        b.emit("se", fn, wait=wait, inc=inc)

    def pe(fn, wait=(), inc=()):
        b.emit("pe", fn, wait=wait, inc=inc)

    def gp(fn, wait=(), inc=()):
        b.emit("gp", fn, wait=wait, inc=inc)

    if "iota" not in SKIP:
        gp(lambda e: e.iota(sb_io8[:, :], [[1, 8]], base=0, channel_multiplier=0,
                            allow_small_or_imprecise_dtypes=True), inc=(("gp", 1),))
    else:
        gp(lambda e: e.memset(sb_io8[:, :], 0.0), inc=(("gp", 1),))
    gp(lambda e: e.memset(sb_ones[:, :], 1.0), inc=(("gp", 1),))
    gp(lambda e: e.memset(sb_trash[:, :], float(TRASH)), inc=(("gp", 1),))
    gp(lambda e: e.memset(sb_beps[:, :], 1.0 + 1e-6), inc=(("gp", 1),))
    gp(lambda e: e.memset(sb_zero[:, :], 0.0), inc=(("gp", 1),))
    gp(lambda e: e.affine_select(sb_idf[:, :], sb_ones[:, :], [[1, 128]],
                                 OP.is_equal, 0.0, base=0, channel_multiplier=-1),
       wait=(("gp", 5),), inc=(("gp", 1),))
    gp(lambda e: e.tensor_copy(sb_idb[:, :], sb_idf[:, :]),
       wait=(("gp", 6),), inc=(("gp", 1),))
    g_pro = b.now("gp")
    if "lib" not in SKIP:
        b.emit("gp", lambda e: e.load_library(_LIB_MLP), async_op=True)
    if "misc" not in SKIP:
        _miscl = [(sb_convw[:, :, :], convw[:, :, :]), (sb_convb[:, :], convb[:, :]),
                  (sb_fl[:, :], fl_in[:, :]), (sb_gpb[:, :], gpb[:, :]),
                  (sb_ipb[:, :], ipb[:, :]), (sb_rb1[:, :], rb1[:, :]),
                  (sb_rb2[:, :], rb2[:, :]), (sb_rb3[:, :], rb3[:, :]),
                  (sb_bf[:, :], bflag[:, :]), (sb_wr3[:, :, :], wr3[:, :, :])]
        for _mi, (dst, src) in enumerate(_miscl):
            if f"m{_mi}" in SKIP:
                continue
            b.dma("gp", "Wmisc", dst, src)
    if "eb1" not in SKIP:
        for e_ in range(NE):
            b.dma("gp", "Wmisc", sb_eb1[:, e_, :], eb1[e_, :, :])
    wmisc = b.now("Wmisc")
    if "sfl" not in SKIP:
        se(lambda e: e.activation(sb_sfl[:, :], sb_fl[:, :], AF.Exp),
           wait=(("Wmisc", wmisc),))
        se(lambda e: e.activation(sb_sfl[:, :], sb_sfl[:, :], AF.Ln, bias=1.0))
        se(lambda e: e.mul(sb_sfl[:, :], sb_sfl[:, :], -8.0), inc=(("se", 1),))
    se_sfl = b.now("se")
    if upto == 'P0':
        return _finish_early()
    for base in (disp_d, eo_d):
        nr = TRASH + 2 if base is disp_d else TRASH + 1
        for r0 in range(0, nr, 128):
            n = min(128, nr - r0)
            b.dma("sy", "Zer", base[r0:r0 + n, :], sb_zero[0:n, :],
                  wait=(("gp", g_pro),))
    zer_tot = b.now("Zer")
    if upto == 'P1':
        return _finish_early()
    for m in range(32):
        b.dma("gp", "Wr2c", rw2bf[m, :, :, :], wr2[m, :, :, :])
    wr2c = b.now("Wr2c")
    for m in range(32):
        b.dma("gp", "Wr1c", rw1bf[m, :, :, :], wr1[m, :, :, :])
    wr1c = b.now("Wr1c")

    if upto == 'P':
        return _finish_early()

    sGATE = ExitStack()
    _stacks.append(sGATE)
    sXC = ExitStack()
    _stacks.append(sXC)
    sb_gate = sGATE.enter_context(nc.sbuf_tensor("ab_gate", [128, 12, T], BF16))
    sb_xc = sXC.enter_context(nc.sbuf_tensor("ab_xc", [128, 12, T], BF16))
    sA = ExitStack()
    _stacks.append(sA)
    sb_xbf = sA.enter_context(nc.sbuf_tensor("a_xbf", [128, 8, 2051], BF16))
    sb_wpj = [sA.enter_context(nc.sbuf_tensor(f"a_wpj{i}", [128, 8, 128], BF16))
              for i in range(2)]
    sb_xh = sA.enter_context(nc.sbuf_tensor("a_xh", [128, 2051], F32))
    sb_cvt = sA.enter_context(nc.sbuf_tensor("a_cvt", [128, T], F32))

    for k in range(8):
        b.dma("gp", "Xf", sb_xbf[:, k, :], x_fm[k * 128:(k + 1) * 128, :])
    xf_tot = b.now("Xf")

    pj_tot = {}
    pj_fin = {}

    def issue_proj(m):
        p = m % 2
        w = (("pe", pj_fin[m - 2]),) if m >= 2 else ()
        pj_tot[m] = b.dma("gp", ("Wpj0", "Wpj1")[p], sb_wpj[p][:, :, :],
                          wproj[m, :, :, :], wait=w)

    issue_proj(0)
    issue_proj(1)
    for m in range(24):
        p = m % 2
        sem = ("Wpj0", "Wpj1")[p]
        is_gate = m < 12
        cts = [(3 + 512 * j, 512) for j in range(4)] if is_gate else \
              [(512 * j, 512) for j in range(4)] + [(2048, 3)]
        for ci, (c0, cw) in enumerate(cts):
            si, ps, pw = ring.acquire()
            for k in range(8):
                w = pw + ((sem, pj_tot[m]), ("Xf", xf_tot)) if k == 0 else ()
                pe(lambda e, o=ps[:, 0:cw], wt=sb_wpj[p][:, k, :],
                   rh=sb_xbf[:, k, c0:c0 + cw], st=(k == 0), sp=(k == 7):
                   e.matmul(o, wt, rh, start=st, stop=sp),
                   wait=w, inc=((("pe", 1),) if k == 7 else ()))
            mm_done = b.now("pe")
            if is_gate:
                se(lambda e, d=sb_gate[:, m, c0 - 3:c0 - 3 + cw], s=ps[:, 0:cw]:
                   e.activation(d, s, AF.Gelu),
                   wait=(("pe", mm_done),))
                ring.release(si, "se")
            else:
                ve(lambda e, d=sb_xh[:, c0:c0 + cw], s=ps[:, 0:cw]: e.tensor_copy(d, s),
                   wait=(("pe", mm_done),))
                ring.release(si, "ve")
        pj_fin[m] = b.now("pe")
        if m + 2 < 24:
            issue_proj(m + 2)
        if not is_gate:
            f = m - 12
            ve(lambda e, f=f: e.tensor_scalar(
                sb_cvt[:, :], sb_xh[:, 0:T], sb_convw[:, f, 0:1],
                sb_convb[:, f:f + 1], OP.mult, OP.add),
               wait=(("Wmisc", wmisc),))
            for kk in range(1, 4):
                ve(lambda e, f=f, kk=kk: e.scalar_tensor_tensor(
                    sb_cvt[:, :], sb_xh[:, kk:kk + T], sb_convw[:, f, kk:kk + 1],
                    sb_cvt[:, :], OP.mult, OP.add))
            ve(lambda e, f=f: e.tensor_copy(sb_xc[:, f, :], sb_cvt[:, :]))
    ve_xc_done = b.now("ve")
    se_gate_done = b.now("se")
    dbg_tot = 0
    if dbg:
        for f in range(12):
            b.dma("gp", "Dbg", dbg_t["d_gate"][:, f, :], sb_gate[:, f, :],
                  wait=(("se", se_gate_done), ("ve", ve_xc_done)))
            dbg_tot = b.dma("gp", "Dbg", dbg_t["d_xc"][:, f, :], sb_xc[:, f, :])
    _stacks.remove(sA)
    sA.close()
    if upto == 'A':
        return _finish_early()

    sB = ExitStack()
    _stacks.append(sB)
    sb_wg = [sB.enter_context(nc.sbuf_tensor(f"b_wg{i}", [128, 2, 12, 128], BF16))
             for i in range(2)]
    sb_al = sB.enter_context(nc.sbuf_tensor("b_al", [128, T], F32))
    sb_t1 = sB.enter_context(nc.sbuf_tensor("b_t1", [128, T], F32))
    sb_t2 = sB.enter_context(nc.sbuf_tensor("b_t2", [128, T], F32))
    sGAQ = ExitStack()
    _stacks.append(sGAQ)
    sb_gaq = sGAQ.enter_context(nc.sbuf_tensor("bc_gaq", [128, 12, T], BF16,
                                               side="right"))

    wg_tot = {}
    wg_fin = {}

    def issue_gw(f):
        p = f % 2
        w = (("pe", wg_fin[f - 2]),) if f >= 2 else             (("pe", pe_A_done), ("ve", ve_A_done))
        wg_tot[f] = b.dma("gp", ("Wg0", "Wg1")[p], sb_wg[p][:, :, :, :],
                          wgates[f, :, :, :, :], wait=w)

    pe_A_done = b.now("pe")
    ve_A_done = b.now("ve")

    issue_gw(0)
    issue_gw(1)
    ve_chain = 0
    for f in range(12):
        p = f % 2
        sem = ("Wg0", "Wg1")[p]
        for gi in range(2):
            for ct in range(4):
                si, ps, pw = ring.acquire()
                for k in range(12):
                    w = pw + ((sem, wg_tot[f]), ("ve", ve_xc_done),
                              ("Dbg", dbg_tot)) if k == 0 else ()
                    pe(lambda e, o=ps[:, 0:512], wt=sb_wg[p][:, gi, k, :],
                       rh=sb_xc[:, k, ct * 512:(ct + 1) * 512],
                       st=(k == 0), sp=(k == 11): e.matmul(o, wt, rh, start=st, stop=sp),
                       wait=w, inc=((("pe", 1),) if k == 11 else ()))
                mm_done = b.now("pe")
                sl = slice(ct * 512, (ct + 1) * 512)
                if gi == 0:
                    se(lambda e, d=sb_t2[:, sl], s=ps[:, 0:512], bi=sb_gpb[:, f:f + 1]:
                       e.activation(d, s, AF.Sigmoid, bias=bi),
                       wait=(("pe", mm_done), ("se", se_sfl), ("ve", ve_chain)),
                       inc=(("se", 1),))
                    se(lambda e, d=sb_al[:, sl], s=sb_t2[:, sl], sc=sb_sfl[:, f:f + 1]:
                       e.activation(d, s, AF.Exp, scale=sc), inc=(("se", 1),))
                else:
                    se(lambda e, d=sb_t1[:, sl], s=ps[:, 0:512], bi=sb_ipb[:, f:f + 1]:
                       e.activation(d, s, AF.Sigmoid, bias=bi),
                       wait=(("pe", mm_done), ("ve", ve_chain)), inc=(("se", 1),))
                ring.release(si, "se")
        wg_fin[f] = b.now("pe")
        se_g = b.now("se")
        ve(lambda e: e.tensor_mul(sb_t2[:, :], sb_al[:, :], sb_al[:, :]),
           wait=(("se", se_g),), inc=(("ve", 1),))
        v_sq = b.now("ve")
        se(lambda e: e.activation(sb_t2[:, :], sb_t2[:, :], AF.Sqrt,
                                  bias=sb_beps[:, :], scale=-1.0),
           wait=(("ve", v_sq), ("gp", g_pro)), inc=(("se", 1),))
        se_b = b.now("se")
        ve(lambda e: e.tensor_mul(sb_t1[:, :], sb_t1[:, :], sb_t2[:, :]),
           wait=(("se", se_b),))
        ve(lambda e, f=f: e.tensor_mul(sb_t1[:, :], sb_t1[:, :], sb_xc[:, f, :]))
        ve(lambda e: e.tensor_tensor_scan(sb_t2[:, :], sb_al[:, :], sb_t1[:, :],
                                          0.0, OP.mult, OP.add))
        ve(lambda e: e.tensor_tensor_scan(sb_t1[:, :], sb_al[:, :], sb_al[:, :],
                                          1.0, OP.mult, OP.bypass))
        ve(lambda e, f=f: e.tensor_copy(sb_carry[:, f:f + 1], sb_t2[:, T - 1:T]))
        ve(lambda e, f=f: e.tensor_mul(sb_gaq[:, f, :], sb_gate[:, f, :], sb_t1[:, :]),
           wait=(("se", se_gate_done),))
        ve(lambda e, f=f: e.tensor_mul(sb_gate[:, f, :], sb_gate[:, f, :], sb_t2[:, :]))
        ve_chain = b.now("ve")
        if f + 2 < 12:
            issue_gw(f + 2)
    ve_scan_done = b.now("ve")

    cc0 = b.dma("gp", "CC", cs_car.ap()[:, :], sb_carry[:, :],
                wait=(("ve", ve_scan_done),))
    b.emit("gp", lambda e: e.collective_compute(
        "AllGather", OP.bypass, replica_groups=RG_PAIRS,
        ins=[cs_car.ap().opt()], outs=[cg_car.ap().opt()]),
        wait=(("CC", cc0),), inc=(("CCc", 1),), async_op=True)
    cc1 = b.now("CCc")
    cc2 = b.dma("gp", "CC", sb_carin[:, :], cg_car.ap()[0, :, :],
                wait=(("CCc", cc1),))
    ve(lambda e: e.tensor_scalar_mul(sb_carin[:, :], sb_carin[:, :], sb_bf[:, 0:1]),
       wait=(("CC", cc2),), inc=(("ve", 1),))
    ve_car = b.now("ve")
    _stacks.remove(sB)
    sB.close()
    _stacks.remove(sXC)
    sXC.close()
    if upto == 'B':
        return _finish_early()

    for f in range(12):
        ve(lambda e, f=f: e.scalar_tensor_tensor(
            sb_gate[:, f, :], sb_gaq[:, f, :], sb_carin[:, f:f + 1],
            sb_gate[:, f, :], OP.mult, OP.add),
           wait=(("ve", ve_car),), inc=((("ve", 1),) if f == 11 else ()))
    ve_z = b.now("ve")
    if dbg:
        for f in range(12):
            dbg_tot = b.dma("gp", "Dbg", dbg_t["d_z"][:, f, :], sb_gate[:, f, :],
                            wait=(("ve", ve_z),))

    _stacks.remove(sGAQ)
    sGAQ.close()
    sATT = ExitStack()
    _stacks.append(sATT)
    sb_attn = sATT.enter_context(nc.sbuf_tensor("c_attn", [128, 8, T], BF16,
                                                side="right"))
    sWO = ExitStack()
    _stacks.append(sWO)
    sb_wo = [sWO.enter_context(nc.sbuf_tensor(f"cw{i}", [128, 12, 128], BF16))
             for i in range(2)]
    wo_tot = {}

    wo_fin = {}

    def issue_wo(m):
        p = m % 2
        w = (("pe", wo_fin[m - 2]),) if m >= 2 else             (("pe", wg_fin[11]), ("ve", ve_scan_done))
        wo_tot[m] = b.dma("gp", ("Wo0", "Wo1")[p], sb_wo[p][:, :, :],
                          wout[m, :, :, :], wait=w)

    issue_wo(0)
    issue_wo(1)
    for m in range(8):
        p = m % 2
        sem = ("Wo0", "Wo1")[p]
        for ct in range(4):
            si, ps, pw = ring.acquire()
            for k in range(12):
                w = pw + ((sem, wo_tot[m]), ("ve", ve_z)) if k == 0 else ()
                pe(lambda e, o=ps[:, 0:512], wt=sb_wo[p][:, k, :],
                   rh=sb_gate[:, k, ct * 512:(ct + 1) * 512],
                   st=(k == 0), sp=(k == 11): e.matmul(o, wt, rh, start=st, stop=sp),
                   wait=w, inc=((("pe", 1),) if k == 11 else ()))
            mm_done = b.now("pe")
            se(lambda e, d=sb_attn[:, m, ct * 512:(ct + 1) * 512], s=ps[:, 0:512]:
               e.copy(d, s), wait=(("pe", mm_done),))
            ring.release(si, "se")
        wo_fin[m] = b.now("pe")
        if m + 2 < 8:
            issue_wo(m + 2)
    se_attn = b.now("se")
    _stacks.remove(sWO)
    sWO.close()
    _stacks.remove(sGATE)
    sGATE.close()
    sCD = ExitStack()
    _stacks.append(sCD)
    sb_xtmb = sCD.enter_context(nc.sbuf_tensor("cd_xtmb", [128, 16, D], BF16))
    sC = ExitStack()
    _stacks.append(sC)
    sb_xtc = [sC.enter_context(nc.sbuf_tensor(f"c_xt{i}", [128, D], F32))
              for i in range(2)]
    sb_ytc = sC.enter_context(nc.sbuf_tensor("c_ytc", [128, D], F32))
    sb_sq = sC.enter_context(nc.sbuf_tensor("c_sq", [128, D], F32))
    sb_xlc = [sC.enter_context(nc.sbuf_tensor(f"c_xl{i}", [128, D], F32))
              for i in range(2)]
    if dbg:
        for m in range(8):
            dbg_tot = b.dma("gp", "Dbg", dbg_t["d_attn"][:, m, :], sb_attn[:, m, :],
                            wait=(("se", se_attn),))

    pe_attn_done = b.now("pe")
    xt_tot = {}
    xt_fin = {}

    def issue_xt(i):
        p = i % 2
        w = (("ve", xt_fin[i - 2]),) if i >= 2 else \
            (("pe", pe_attn_done), ("Dbg", dbg_tot))
        xt_tot[i] = b.dma("sy", SEM_XT[p], sb_xtc[p][:, :],
                          x_tm[i * 128:(i + 1) * 128, :], wait=w)

    issue_xt(0)
    issue_xt(1)
    ve_pt_free = {0: 0, 1: 0}
    spl_tot = 0
    se_sq_prev = 0
    for i in range(16):
        p = i % 2
        pt = (pT1, pT2)[p]
        for c in range(8):
            w = (("se", se_attn), ("ve", ve_pt_free[p]), ("gp", g_pro)) if c == 0 else ()
            pe(lambda e, o=pt[:, c * 128:(c + 1) * 128],
               s=sb_attn[:, c, i * 128:(i + 1) * 128]: e.transpose(o, s, sb_idb[:, :]),
               wait=w, inc=((("pe", 1),) if c == 7 else ()))
        pe_tr = b.now("pe")
        ve(lambda e, p=p, pt=pt: e.tensor_add(sb_ytc[:, :], sb_xtc[p][:, :], pt[:, :]),
           wait=(("pe", pe_tr), (SEM_XT[p], xt_tot[i]), ("Spl", spl_tot),
                 ("se", se_sq_prev)))
        ve_pt_free[p] = b.now("ve")
        xt_fin[i] = b.now("ve")
        ve(lambda e: e.reduce_sum(COL[0][:, :], sb_ytc[:, :], axis=AXX, negate=True))
        ve(lambda e: e.tensor_scalar_mul(COL[0][:, :], COL[0][:, :], 1.0 / D),
           inc=(("ve", 1),))
        v_mu = b.now("ve")
        se(lambda e: e.activation(sb_sq[:, :], sb_ytc[:, :], AF.Square,
                                  accum_out=COL[1][:, :]),
           wait=(("ve", v_mu),), inc=(("se", 1),))
        se_sq_prev = b.now("se")
        ve(lambda e: e.tensor_scalar(COL[1][:, :], COL[1][:, :], 1.0 / D, EPS,
                                     OP.mult, OP.add), wait=(("se", se_sq_prev),))
        ve(lambda e: e.tensor_mul(COL[2][:, :], COL[0][:, :], COL[0][:, :]))
        ve(lambda e: e.tensor_sub(COL[1][:, :], COL[1][:, :], COL[2][:, :]))
        ve(lambda e: e.reciprocal(COL[2][:, :], COL[1][:, :]), inc=(("ve", 1),))
        v_rc = b.now("ve")
        se(lambda e: e.activation(COL[3][:, :], COL[2][:, :], AF.Sqrt),
           wait=(("ve", v_rc),), inc=(("se", 1),))
        se_inv = b.now("se")
        ve(lambda e, p=p: e.tensor_scalar(sb_xlc[p][:, :], sb_ytc[:, :],
                                          COL[0][:, :], COL[3][:, :], OP.add, OP.mult),
           wait=(("se", se_inv),), inc=(("ve", 1),))
        v_xl = b.now("ve")
        spl_tot = b.dma("sy", "Spl", xln_d[i * 128:(i + 1) * 128, :], sb_xlc[p][:, :],
                        wait=(("ve", v_xl),))
        ve(lambda e, i=i, p=p: e.tensor_copy(sb_xtmb[:, i, :], sb_xlc[p][:, :]),
           inc=(("ve", 1),))
        if i + 2 < 16:
            issue_xt(i + 2)
    ve_xln = b.now("ve")
    if dbg:
        dbg_tot = b.dma("gp", "Dbg", dbg_t["d_xln"][:, :], xln_d[:, :],
                        wait=(("Spl", spl_tot),))
    _stacks.remove(sC)
    sC.close()
    _stacks.remove(sATT)
    sATT.close()
    if upto == 'C':
        return _finish_early()

    sD = ExitStack()
    _stacks.append(sD)
    sb_o8 = sD.enter_context(nc.sbuf_tensor("s_o8", [8, T], F32))
    sb_mem = sD.enter_context(nc.sbuf_tensor("s_mem", [8, T], F32))
    sb_pincl = sD.enter_context(nc.sbuf_tensor("s_pincl", [8, T], F32))
    sb_pem = sD.enter_context(nc.sbuf_tensor("s_pem", [8, T], F32))
    sb_wr1s = [sD.enter_context(nc.sbuf_tensor(f"d_wr1{i}", [128, 8, 128], BF16))
               for i in range(2)]
    sb_xfm = sD.enter_context(nc.sbuf_tensor("d_xfm", [128, 8, 512], BF16))
    sb_h1 = sD.enter_context(nc.sbuf_tensor("d_h1", [128, 32, 512], BF16))
    sb_h2 = sD.enter_context(nc.sbuf_tensor("d_h2", [128, 32, 512], BF16))
    sb_wr2 = [sD.enter_context(nc.sbuf_tensor(f"d_wr2{i}", [128, 32, 128], BF16))
              for i in range(2)]
    sb_lg = sD.enter_context(nc.sbuf_tensor("d_lg", [128, 8], F32))
    sb_pp = sD.enter_context(nc.sbuf_tensor("d_pp", [128, 8], F32))
    sb_ptok = sD.enter_context(nc.sbuf_tensor("d_ptok", [128, 8], F32))

    gp(lambda e: e.memset(sb_o8[:, :], 1.0),
       wait=(("ve", ve_xln), ("Spl", spl_tot)))
    g_o8 = b.now("gp")
    r1_tot = {}
    r1_fin = {}

    def issue_r1(mg):
        p = mg % 2
        w = [("Wr1c", wr1c)]
        if mg >= 2:
            w.append(("pe", r1_fin[mg - 2]))
        else:
            w += [("ve", ve_xln), ("Spl", spl_tot)]
        r1_tot[mg] = b.dma("sy", ("Wr1a", "Wr1b")[p], sb_wr1s[p][:, :, :],
                           rw1bf[mg % 32, :, :, :], wait=tuple(w))

    issue_r1(0)
    issue_r1(1)
    r2_tot = {}
    r2_fin = {}

    def issue_r2(mg):
        p = mg % 2
        w = [("Wr2c", wr2c)]
        if mg >= 2:
            w.append(("pe", r2_fin[mg - 2]))
        else:
            w += [("ve", ve_xln), ("Spl", spl_tot)]
        r2_tot[mg] = b.dma("sy", ("Wr2a", "Wr2b")[p], sb_wr2[p][:, :, :],
                           rw2bf[mg % 32, :, :, :], wait=tuple(w))

    issue_r2(0)
    issue_r2(1)
    for blk in range(4):
        for j in range(4):
            tc = blk * 4 + j
            p = j % 2
            pt = (pT1, pT2)[p]
            for c in range(8):
                w = (("ve", ve_xln), ("ve", ve_pt_free[p])) if c == 0 else ()
                pe(lambda e, o=pt[:, c * 128:(c + 1) * 128],
                   s=sb_xtmb[:, tc, c * 128:(c + 1) * 128]:
                   e.transpose(o, s, sb_idb[:, :]),
                   wait=w, inc=((("pe", 1),) if c == 7 else ()))
            pe_tr = b.now("pe")
            ve(lambda e, j=j, pt=pt: e.tensor_copy(
                sb_xfm[:, :, j * 128:(j + 1) * 128],
                pt[:, :].rearrange("p (c t) -> p c t", c=8)),
               wait=(("pe", pe_tr),), inc=(("ve", 1),))
            ve_pt_free[p] = b.now("ve")
        ve_xfm = b.now("ve")
        for m in range(32):
            mg1 = blk * 32 + m
            p1 = mg1 % 2
            sem1 = ("Wr1a", "Wr1b")[p1]
            si, ps, pw = ring.acquire()
            for k in range(8):
                w = pw + ((sem1, r1_tot[mg1]), ("ve", ve_xfm),
                          ("Dbg", dbg_tot)) if k == 0 else ()
                pe(lambda e, o=ps[:, 0:512], wt=sb_wr1s[p1][:, k, :],
                   rh=sb_xfm[:, k, :], st=(k == 0), sp=(k == 7):
                   e.matmul(o, wt, rh, start=st, stop=sp),
                   wait=w, inc=((("pe", 1),) if k == 7 else ()))
            mm_done = b.now("pe")
            r1_fin[mg1] = mm_done
            se(lambda e, m=m, s=ps[:, 0:512], bi=sb_rb1[:, m:m + 1]:
               e.activation(sb_h1[:, m, :], s, AF.Relu, bias=bi),
               wait=(("pe", mm_done),))
            ring.release(si, "se")
            if mg1 + 2 < 128:
                issue_r1(mg1 + 2)
        se_h1 = b.now("se")
        for m in range(32):
            mg = blk * 32 + m
            p = mg % 2
            sem = ("Wr2a", "Wr2b")[p]
            si, ps, pw = ring.acquire()
            for k in range(32):
                w = pw + ((sem, r2_tot[mg]), ("se", se_h1)) if k == 0 else ()
                pe(lambda e, o=ps[:, 0:512], wt=sb_wr2[p][:, k, :],
                   rh=sb_h1[:, k, :], st=(k == 0), sp=(k == 31):
                   e.matmul(o, wt, rh, start=st, stop=sp),
                   wait=w, inc=((("pe", 1),) if k == 31 else ()))
            mm_done = b.now("pe")
            r2_fin[mg] = mm_done
            se(lambda e, m=m, s=ps[:, 0:512], bi=sb_rb2[:, m:m + 1]:
               e.activation(sb_h2[:, m, :], s, AF.Relu, bias=bi),
               wait=(("pe", mm_done),))
            ring.release(si, "se")
            if mg + 2 < 128:
                issue_r2(mg + 2)
        se_h2 = b.now("se")
        if dbg and blk == 0:
            for m_ in range(32):
                dbg_tot = b.dma("gp", "Dbg", dbg_t["dbh1"][:, m_, :], sb_h1[:, m_, :],
                                wait=(("se", se_h2),))
                dbg_tot = b.dma("gp", "Dbg", dbg_t["dbh2"][:, m_, :], sb_h2[:, m_, :])
        for j in range(4):
            tc = blk * 4 + j
            si, ps, pw = ring.acquire()
            for k in range(32):
                w = pw + (("se", se_h2), ("Wmisc", wmisc)) if k == 0 else ()
                pe(lambda e, o=ps[:, 0:8], wt=sb_h2[:, k, j * 128:(j + 1) * 128],
                   rh=sb_wr3[:, k, :], st=(k == 0), sp=(k == 31):
                   e.matmul(o, wt, rh, start=st, stop=sp),
                   wait=w, inc=((("pe", 1),) if k == 31 else ()))
            mm_done = b.now("pe")
            ve(lambda e, s=ps[:, 0:8]: e.tensor_add(sb_lg[:, :], s, sb_rb3[:, :]),
               wait=(("pe", mm_done), ("gp", g_pro)), inc=(("ve", 1),))
            ring.release(si, "ve")
            ve(lambda e: e.reduce_max(COL[0][:, :], sb_lg[:, :], axis=AXX, negate=True),
               inc=(("ve", 1),))
            v_nm = b.now("ve")
            se(lambda e: e.activation(sb_pp[:, :], sb_lg[:, :], AF.Exp,
                                      bias=COL[0][:, :]),
               wait=(("ve", v_nm), ("Dbg", dbg_tot)))
            se_ex = b.now("se")
            ve(lambda e: e.reduce_sum(COL[1][:, :], sb_pp[:, :], axis=AXX),
               wait=(("se", se_ex),))
            ve(lambda e: e.reciprocal(COL[2][:, :], COL[1][:, :]))
            ve(lambda e: e.tensor_scalar_mul(sb_pp[:, :], sb_pp[:, :], COL[2][:, :]))
            if dbg:
                dbg_tot = b.dma("gp", "Dbg", dbg_t["dbpp"][:, tc, :], sb_pp[:, :],
                                wait=(("ve", b.now("ve")),))
            ve(lambda e, tc=tc: e.reduce_max(sb_s0[:, tc:tc + 1], sb_pp[:, :], axis=AXX))
            ve(lambda e, tc=tc: e.tensor_scalar(ROW8[0][:, :], sb_pp[:, :],
                                                sb_s0[:, tc:tc + 1], None, OP.is_equal))
            ve(lambda e: e.tensor_scalar(ROW8[1][:, :], ROW8[0][:, :], -BIG, BIG,
                                         OP.mult, OP.add))
            ve(lambda e: e.tensor_add(ROW8[1][:, :], ROW8[1][:, :], sb_io8[:, :]))
            ve(lambda e, tc=tc: e.tensor_reduce(sb_e0[:, tc:tc + 1], ROW8[1][:, :],
                                                AXX, OP.min))
            ve(lambda e, tc=tc: e.tensor_scalar(sb_oh0[:, tc, :], sb_io8[:, :],
                                                sb_e0[:, tc:tc + 1], None, OP.is_equal))
            ve(lambda e, tc=tc: e.tensor_scalar(ROW8[1][:, :], sb_oh0[:, tc, :],
                                                -BIG, None, OP.mult))
            ve(lambda e: e.tensor_add(ROW8[1][:, :], ROW8[1][:, :], sb_pp[:, :]))
            ve(lambda e, tc=tc: e.reduce_max(sb_s1[:, tc:tc + 1], ROW8[1][:, :],
                                             axis=AXX))
            ve(lambda e, tc=tc: e.tensor_scalar(ROW8[2][:, :], ROW8[1][:, :],
                                                sb_s1[:, tc:tc + 1], None, OP.is_equal))
            ve(lambda e: e.tensor_scalar(ROW8[2][:, :], ROW8[2][:, :], -BIG, BIG,
                                         OP.mult, OP.add))
            ve(lambda e: e.tensor_add(ROW8[2][:, :], ROW8[2][:, :], sb_io8[:, :]))
            ve(lambda e, tc=tc: e.tensor_reduce(sb_e1[:, tc:tc + 1], ROW8[2][:, :],
                                                AXX, OP.min))
            ve(lambda e, tc=tc: e.tensor_scalar(sb_oh1[:, tc, :], sb_io8[:, :],
                                                sb_e1[:, tc:tc + 1], None, OP.is_equal))
            ve(lambda e, tc=tc: e.tensor_add(ROW8[3][:, :], sb_oh0[:, tc, :],
                                             sb_oh1[:, tc, :]), inc=(("ve", 1),))
            v_m = b.now("ve")
            si2, ps2, pw2 = ring.acquire()
            pe(lambda e, o=ps2[0:8, 0:128], s=ROW8[3][:, :]:
               e.transpose(o, s, sb_idf[:, :]),
               wait=pw2 + (("ve", v_m),), inc=(("pe", 1),))
            pe_mt = b.now("pe")
            ve(lambda e, tc=tc, s=ps2[0:8, 0:128]:
               e.tensor_copy(sb_mem[0:8, tc * 128:(tc + 1) * 128], s),
               wait=(("pe", pe_mt),), inc=(("ve", 1),))
            ring.release(si2, "ve")
        ve_mem = b.now("ve")
        seg = slice(blk * 512, (blk + 1) * 512)
        init = 0.0 if blk == 0 else sb_pin[0:8, 0:1]
        ve(lambda e, seg=seg, init=init: e.tensor_tensor_scan(
            sb_pincl[0:8, seg], sb_o8[0:8, seg], sb_mem[0:8, seg],
            init, OP.mult, OP.add), wait=(("ve", ve_mem), ("gp", g_o8)))
        ve(lambda e, seg=seg: e.tensor_sub(sb_pem[0:8, seg], sb_pincl[0:8, seg],
                                           sb_mem[0:8, seg]))
        ve(lambda e, blk=blk: e.tensor_copy(
            sb_pin[0:8, 0:1], sb_pincl[0:8, (blk + 1) * 512 - 1:(blk + 1) * 512]),
           inc=(("ve", 1),))
        ve_pscan = b.now("ve")
        for j in range(4):
            tc = blk * 4 + j
            si2, ps2, pw2 = ring.acquire()
            pe(lambda e, o=ps2[:, 0:8], s=sb_pem[0:8, tc * 128:(tc + 1) * 128]:
               e.transpose(o, s, sb_idf[0:8, 0:8]),
               wait=pw2 + (("ve", ve_pscan),), inc=(("pe", 1),))
            pe_pt = b.now("pe")
            ve(lambda e, s=ps2[:, 0:8]: e.tensor_copy(sb_ptok[:, :], s),
               wait=(("pe", pe_pt),), inc=(("ve", 1),))
            ring.release(si2, "ve")
            for oh, pssel, esel, slsel, tval in [
                    (sb_oh0, sb_ps0, sb_e0, sb_sl0, float(TRASH)),
                    (sb_oh1, sb_ps1, sb_e1, sb_sl1, float(TRASH + 1))]:
                ve(lambda e, oh=oh, tc=tc: e.tensor_mul(ROW8[0][:, :],
                                                        sb_ptok[:, :], oh[:, tc, :]))
                ve(lambda e, p_=pssel, tc=tc: e.reduce_sum(p_[:, tc:tc + 1],
                                                           ROW8[0][:, :], axis=AXX))
                ve(lambda e, p_=pssel, e_=esel, s_=slsel, tc=tc:
                   e.scalar_tensor_tensor(s_[:, tc:tc + 1], e_[:, tc:tc + 1],
                                          float(CAP), p_[:, tc:tc + 1],
                                          OP.mult, OP.add))
                ve(lambda e, p_=pssel, tc=tc: e.tensor_scalar(
                    COL[4][:, :], p_[:, tc:tc + 1], float(CAP), None, OP.is_lt))
                ve(lambda e, s_=slsel, tc=tc, tv=tval: e.scalar_tensor_tensor(
                    COL[5][:, :], s_[:, tc:tc + 1], tv, COL[4][:, :],
                    OP.subtract, OP.mult))
                ve(lambda e, s_=slsel, tc=tc, tv=tval: e.tensor_scalar(
                    s_[:, tc:tc + 1], COL[5][:, :], tv, None, OP.add))
    ve_route = b.now("ve")
    if dbg:
        dbg_tot = b.dma("gp", "Dbg", dbg_t["d_pem"][:, :], sb_pem[0:8, :],
                        wait=(("ve", ve_route),))
        for j, tile in enumerate([sb_s0, sb_s1, sb_e0, sb_e1, sb_ps0, sb_ps1,
                                  sb_sl0, sb_sl1]):
            dbg_tot = b.dma("gp", "Dbg", dbg_t["d_sel"][:, :, j], tile[:, :],
                            wait=(("ve", ve_route),), slow=True)

    cc3 = b.dma("gp", "CC", cs_cnt.ap()[:], sb_pincl[0:8, T - 1:T],
                wait=(("ve", ve_route),))
    b.emit("gp", lambda e: e.collective_compute(
        "AllGather", OP.bypass, replica_groups=RG_PAIRS,
        ins=[cs_cnt.ap().opt()], outs=[cg_cnt.ap().opt()]),
        wait=(("CC", cc3),), inc=(("CCc", 1),), async_op=True)
    cc4 = b.now("CCc")
    cc5 = b.dma("gp", "CC", sb_cnt[0:8, 0:1], cg_cnt.ap()[0, :], wait=(("CCc", cc4),))
    ve(lambda e: e.tensor_scalar_mul(sb_cnt[0:8, 0:1], sb_cnt[0:8, 0:1],
                                     sb_bf[0:8, 0:1]),
       wait=(("CC", cc5),), inc=(("ve", 1),))
    v_cnt = b.now("ve")
    si2, ps2, pw2 = ring.acquire()
    pe(lambda e, o=ps2[0:1, 0:8]: e.transpose(o, sb_cnt[0:8, 0:1], sb_idf[0:8, 0:8]),
       wait=pw2 + (("ve", v_cnt),), inc=(("pe", 1),))
    pe_ct = b.now("pe")
    ve(lambda e, s=ps2[0:1, 0:8]: e.tensor_copy(sb_crow[0:1, :], s),
       wait=(("pe", pe_ct),), inc=(("ve", 1),))
    v_cr = b.now("ve")
    ring.release(si2, "ve")
    si2, ps2, pw2 = ring.acquire()
    pe(lambda e, o=ps2[:, 0:8]: e.matmul(o, sb_ones[0:1, :], sb_crow[0:1, :],
                                         start=True, stop=True),
       wait=pw2 + (("ve", v_cr), ("gp", g_pro)), inc=(("pe", 1),))
    pe_cb = b.now("pe")
    ve(lambda e, s=ps2[:, 0:8]: e.tensor_copy(sb_cntbc[:, :], s),
       wait=(("pe", pe_cb),), inc=(("ve", 1),))
    ring.release(si2, "ve")
    v_cbc = b.now("ve")


    def build_idx(slsel, i16t, region, base_wait):
        si_, ps_, pw_ = ring.acquire()
        pe(lambda e, o=ps_[0:16, 0:128], s=slsel[:, :]:
           e.transpose(o, s, sb_idf[:, :]),
           wait=pw_ + base_wait, inc=(("pe", 1),))
        pe_t = b.now("pe")
        ve(lambda e, s=ps_[0:16, 0:128]: e.tensor_copy(sb_t16[:, :], s),
           wait=(("pe", pe_t), ("Idx", b.now("Idx"))))
        v_t = b.now("ve")
        ring.release(si_, "ve")
        dst = idx_d[region, :, :].rearrange("(a s) (r q) -> a s r q", s=8, q=16)
        t_ = 0
        for r in range(8):
            t_ = b.dma("sy", "Idx",
                       dst[:, :, r, :].rearrange("a s q -> a s q") if False else
                       idx_d[region, :, :].rearrange("(a s) j -> a s j", s=8)[:, :, 16 * r:16 * (r + 1)],
                       sb_t16[:, :].rearrange("a (s q) -> a s q", q=16),
                       wait=(("ve", v_t),) if r == 0 else ())
        b.emit("sy", lambda e, d=i16t[:, :], s=idx_d[region, :, :]:
               e.dma_start_transpose(d, s),
               wait=(("Idx", t_),), inc=(("Idx", 16),), async_op=True)
        return b.now("Idx")

    idx_tot = 0
    for sl, i16t, reg in [(sb_sl0, sb_i16a, 0), (sb_sl1, sb_i16b, 1)]:
        idx_tot = build_idx(sl, i16t, reg,
                            (("ve", ve_route), ("Dbg", dbg_tot)))
    for i16t in (sb_i16a, sb_i16b):
        b.emit("gp", lambda e, t=i16t: e.dma_scatter_add(disp_d[:, :], sb_xtmb[:, :, :],
                                                         t[:, :], T, T, D),
               wait=(("Idx", idx_tot), ("Zer", zer_tot), ("ve", ve_xln)),
               inc=(("Sca", 16),), async_op=True)
    sca_tot = b.now("Sca")
    pe_selidx = b.now("pe")
    for tc in range(16):
        for oh, pssel, ssel, slsel, wsel in [(sb_oh0, sb_ps0, sb_s0, sb_sl0, sb_w0),
                                             (sb_oh1, sb_ps1, sb_s1, sb_sl1, sb_w1)]:
            ve(lambda e, oh=oh, tc=tc: e.tensor_mul(ROW8[0][:, :], sb_cntbc[:, :],
                                                    oh[:, tc, :]),
               wait=(("ve", v_cbc), ("pe", pe_selidx)))
            ve(lambda e: e.reduce_sum(COL[0][:, :], ROW8[0][:, :], axis=AXX))
            ve(lambda e, p_=pssel, tc=tc: e.tensor_add(COL[0][:, :], COL[0][:, :],
                                                       p_[:, tc:tc + 1]))
            ve(lambda e: e.tensor_scalar(COL[1][:, :], COL[0][:, :], float(CAP),
                                         None, OP.is_lt))
            ve(lambda e, s_=ssel, w_=wsel, tc=tc: e.tensor_mul(
                w_[:, tc:tc + 1], s_[:, tc:tc + 1], COL[1][:, :]))
            ve(lambda e, sl=slsel, tc=tc: e.scalar_tensor_tensor(
                COL[5][:, :], sl[:, tc:tc + 1], float(TRASH), COL[1][:, :],
                OP.subtract, OP.mult))
            ve(lambda e, sl=slsel, tc=tc: e.tensor_scalar(
                sl[:, tc:tc + 1], COL[5][:, :], float(TRASH), None, OP.add))
    v_keep = b.now("ve")
    idx_tot2 = 0
    for sl, i16t, reg in [(sb_sl0, sb_i16a, 2), (sb_sl1, sb_i16b, 3)]:
        idx_tot2 = build_idx(sl, i16t, reg,
                             (("ve", v_keep), ("Sca", sca_tot)))
    if upto == 'F0':
        return _finish_early()
    gidx_tot = 0
    for sl, i16t, reg in [(sb_sl0, sb_i16c, 2), (sb_sl1, sb_i16d, 3)]:
        gidx_tot = build_idx(sl, i16t, reg,
                             (("ve", v_keep), ("Sca", sca_tot)))
    if dbg:
        dbg_tot = b.dma("gp", "Dbg", dbg_t["d_disp"][:, :], disp_d[:, :],
                        wait=(("Sca", sca_tot),))
    pe_router_done = b.now("pe")
    _stacks.remove(sD)
    sD.close()
    _stacks.remove(sCD)
    sCD.close()
    if upto == 'D':
        return _finish_early()

    sE = ExitStack()
    _stacks.append(sE)
    sb_dtm = [sE.enter_context(nc.sbuf_tensor(f"e_dt{i}", [128, 5, D], BF16))
              for i in range(2)]
    sb_dsp = [sE.enter_context(nc.sbuf_tensor(f"e_d{i}", [128, 8, CAP], BF16))
              for i in range(2)]
    sb_we1 = [sE.enter_context(nc.sbuf_tensor(f"e_w1{i}", [128, 4, 8, 128], BF16))
              for i in range(2)]
    sb_ew2 = sE.enter_context(nc.sbuf_tensor("e_w2", [128, 32, D], BF16))
    sb_eh = sE.enter_context(nc.sbuf_tensor("e_eh", [128, 32, CAP], BF16))
    sb_eb2c = sE.enter_context(nc.sbuf_tensor("e_b2", [128, D], F32))
    sb_eob = [sE.enter_context(nc.sbuf_tensor(f"e_o{i}", [128, D], BF16))
              for i in range(2)]

    dl_tot = {}
    dl_free = {-2: 0, -1: 0}

    def issue_disp_t(e_):
        p = e_ % 2
        w = [("Sca", sca_tot)]
        if e_ >= 2:
            w.append(("pe", dl_free[e_ - 2]))
        dl_tot[e_] = b.dma(
            "sy", SEM_DL[p], sb_dtm[p][:, :, :],
            disp_d[e_ * CAP:(e_ + 1) * CAP, :].rearrange("(sc p) c -> p sc c", p=128),
            wait=tuple(w))

    fm_free = {-2: 0, -1: 0}

    def disp_transpose(e_):
        p = e_ % 2
        dsem = SEM_DL[p]
        for sc in range(5):
            tp_ = (pT1, pT2)[sc % 2]
            for c in range(8):
                w = ((dsem, dl_tot[e_]), ("ve", ve_pt_free[sc % 2]),
                     ("pe", 0)) if c == 0 else ()
                pe(lambda e, o=tp_[:, c * 128:(c + 1) * 128],
                   s=sb_dtm[p][:, sc, c * 128:(c + 1) * 128]:
                   e.transpose(o, s, sb_idb[:, :]),
                   wait=w, inc=((("pe", 1),) if c == 7 else ()))
            pe_tr = b.now("pe")
            ve(lambda e, sc=sc, tp_=tp_, p=p: e.tensor_copy(
                sb_dsp[p][:, :, sc * 128:(sc + 1) * 128],
                tp_[:, :].rearrange("q (c t) -> q c t", c=8)),
               wait=(("pe", pe_tr), ("pe", fm_free[e_ - 2])))
            ve_pt_free[sc % 2] = b.now("ve")
        dl_free[e_] = b.now("pe")

    we1_tot = {}

    we1_fin = {}

    def issue_we1(g):
        p = g % 2
        w = (("pe", we1_fin[g - 2]),) if g >= 2 else (("pe", pe_router_done),)
        we1_tot[g] = b.dma("gp", ("We1a", "We1b")[p], sb_we1[p][:, :, :, :],
                           we1[g // 8, g % 8, :, :, :, :], wait=w)

    issue_disp_t(0)
    issue_we1(0)
    issue_we1(1)
    we2_tot = {}
    eo_pe_done = {-1: pe_router_done}
    eo_ve_last = {-1: 0}
    eh_free = {0: 0, 1: 0}
    eo_ve_free = 0
    eob_free = {0: 0, 1: 0}
    ring_drain = ring.drain_waits()
    eo_tot = 0
    for e_ in range(NE):
        p = e_ % 2
        dsem = SEM_DL[p]
        we2_tot[e_] = b.dma("gp", "We2", sb_ew2[:, :, :], we2[e_, :, :, :],
                            wait=(("pe", eo_pe_done[e_ - 1]),))
        eb2_tot = b.dma("sy", "Web2", sb_eb2c[:, :], eb2[e_, :, :],
                        wait=(("pe", eo_pe_done[e_ - 1]), ("ve", eo_ve_last[e_ - 1])))
        if e_ + 1 < NE:
            issue_disp_t(e_ + 1)
        disp_transpose(e_)
        ve_dsp = b.now("ve")
        for mb in range(8):
            g = e_ * 8 + mb
            wp = g % 2
            wsem = ("We1a", "We1b")[wp]
            for m4 in range(4):
                m = mb * 4 + m4
                pp = (pA, pB)[m % 2]
                first_w = ((wsem, we1_tot[g]), ("ve", ve_dsp),
                           ("se", eh_free[m % 2])) + \
                    (ring_drain if (e_ == 0 and mb == 0 and m4 == 0) else ())
                first = True
                for k in range(8):
                    for (lo, wdt) in ((0, 512), (512, 128)):
                        pe(lambda e, o=pp[:, lo:lo + wdt], wt=sb_we1[wp][:, m4, k, :],
                           rh=sb_dsp[p][:, k, lo:lo + wdt], st=(k == 0), sp=(k == 7):
                           e.matmul(o, wt, rh, start=st, stop=sp),
                           wait=first_w if first else (),
                           inc=((("pe", 1),) if (k == 7 and wdt == 128) else ()))
                        first = False
                mm_done = b.now("pe")
                se(lambda e, m=m, ee=e_, s=pp[:, 0:CAP]:
                   e.activation(sb_eh[:, m, :], s, AF.Relu,
                                bias=sb_eb1[:, ee, m:m + 1]),
                   wait=(("pe", mm_done),))
                eh_free[m % 2] = b.now("se")
            we1_fin[g] = b.now("pe")
            if g + 2 < NE * 8:
                issue_we1(g + 2)
        se_eh = b.now("se")
        fm_free[e_] = b.now("pe")
        for n_c in range(5):
            first = True
            for m in range(32):
                for (lo, wdt) in ((0, 512), (512, 512)):
                    w = (("We2", we2_tot[e_]), ("se", se_eh),
                         ("ve", eo_ve_free)) if first else ()
                    pe(lambda e, o=pC[:, lo:lo + wdt],
                       wt=sb_eh[:, m, n_c * 128:(n_c + 1) * 128],
                       rh=sb_ew2[:, m, lo:lo + wdt], st=(m == 0), sp=(m == 31):
                       e.matmul(o, wt, rh, start=st, stop=sp),
                       wait=w,
                       inc=((("pe", 1),) if (m == 31 and lo == 512) else ()))
                    first = False
            mm_done = b.now("pe")
            op = n_c % 2
            ve(lambda e, o=sb_eob[op][:, :]: e.tensor_add(o, pC[:, :], sb_eb2c[:, :]),
               wait=(("pe", mm_done), ("Web2", eb2_tot), (SEM_EO[op], eob_free[op])),
               inc=(("ve", 1),))
            eo_ve_free = b.now("ve")
            eo_tot = b.dma("sy", SEM_EO[op],
                           eo_d[e_ * CAP + n_c * 128:e_ * CAP + (n_c + 1) * 128, :],
                           sb_eob[op][:, :], wait=(("ve", eo_ve_free),))
            eob_free[op] = eo_tot
        eo_pe_done[e_] = b.now("pe")
        eo_ve_last[e_] = b.now("ve")
    eo_all = (("Eo0", b.now("Eo0")), ("Eo1", b.now("Eo1")))
    if dbg:
        dbg_tot = b.dma("gp", "Dbg", dbg_t["d_eo"][:, :], eo_d[:, :], wait=eo_all)
    _stacks.remove(sE)
    sE.close()
    if upto == 'E':
        return _finish_early()

    sF = ExitStack()
    _stacks.append(sF)
    sb_g0 = sF.enter_context(nc.sbuf_tensor("f_g0", [128, 16, D], BF16))
    sb_g1 = sF.enter_context(nc.sbuf_tensor("f_g1", [128, 16, D], BF16))
    sb_xr = [sF.enter_context(nc.sbuf_tensor(f"f_xr{i}", [128, D], F32))
             for i in range(2)]
    sb_y2 = sF.enter_context(nc.sbuf_tensor("f_y2", [128, D], F32))
    sb_sq2 = sF.enter_context(nc.sbuf_tensor("f_sq", [128, D], F32))
    sb_yo = [sF.enter_context(nc.sbuf_tensor(f"f_yo{i}", [128, D], F32))
             for i in range(2)]

    for i16t, gdst in ((sb_i16c, sb_g0), (sb_i16d, sb_g1)):
        for j in range(4):
            b.emit("gp", lambda e, t=i16t[:, 32 * j:32 * (j + 1)],
                   d=gdst[:, 4 * j:4 * (j + 1), :]:
                   e.dma_gather(d, eo_d[:, :], t, 512, 512, D),
                   wait=(("Idx", gidx_tot),) + eo_all +
                        ((("Dbg", dbg_tot),) if dbg else ()),
                   inc=(("Gat", 16),), async_op=True)
    gat_tot = b.now("Gat")
    if dbg:
        for tc in range(16):
            b.dma("gp", "Dbg", dbg_t["d_g01"][0, :, tc, :], sb_g0[:, tc, :],
                  wait=(("Gat", gat_tot),))
            dbg_tot = b.dma("gp", "Dbg", dbg_t["d_g01"][1, :, tc, :], sb_g1[:, tc, :])

    pe_E_done = b.now("pe")
    if upto == 'F1':
        return _finish_early()
    xr_tot = {}

    xr_fin = {}

    def issue_xr(i):
        p = i % 2
        w = (("Spl", spl_tot), ("pe", pe_E_done)) if i < 2 else             (("ve", xr_fin[i - 2]),)
        xr_tot[i] = b.dma("sy", SEM_RL[p], sb_xr[p][:, :],
                          xln_d[i * 128:(i + 1) * 128, :], wait=w)

    issue_xr(0)
    issue_xr(1)
    se_sq_prev = b.now("se")
    for i in range(16):
        p = i % 2
        ve(lambda e, i=i: e.tensor_scalar_mul(sb_y2[:, :], sb_g1[:, i, :],
                                              sb_w1[:, i:i + 1]),
           wait=(("Gat", gat_tot), ("ve", v_keep), ("se", se_sq_prev)))
        ve(lambda e, i=i: e.scalar_tensor_tensor(sb_y2[:, :], sb_g0[:, i, :],
                                                 sb_w0[:, i:i + 1], sb_y2[:, :],
                                                 OP.mult, OP.add))
        ve(lambda e, p=p: e.tensor_add(sb_y2[:, :], sb_y2[:, :], sb_xr[p][:, :]),
           wait=((SEM_RL[p], xr_tot[i]),))
        xr_fin[i] = b.now("ve")
        ve(lambda e: e.reduce_sum(COL[0][:, :], sb_y2[:, :], axis=AXX, negate=True))
        ve(lambda e: e.tensor_scalar_mul(COL[0][:, :], COL[0][:, :], 1.0 / D),
           inc=(("ve", 1),))
        v_mu = b.now("ve")
        se(lambda e: e.activation(sb_sq2[:, :], sb_y2[:, :], AF.Square,
                                  accum_out=COL[1][:, :]),
           wait=(("ve", v_mu),), inc=(("se", 1),))
        se_sq_prev = b.now("se")
        ve(lambda e: e.tensor_scalar(COL[1][:, :], COL[1][:, :], 1.0 / D, EPS,
                                     OP.mult, OP.add), wait=(("se", se_sq_prev),))
        ve(lambda e: e.tensor_mul(COL[2][:, :], COL[0][:, :], COL[0][:, :]))
        ve(lambda e: e.tensor_sub(COL[1][:, :], COL[1][:, :], COL[2][:, :]))
        ve(lambda e: e.reciprocal(COL[2][:, :], COL[1][:, :]), inc=(("ve", 1),))
        v_rc = b.now("ve")
        se(lambda e: e.activation(COL[3][:, :], COL[2][:, :], AF.Sqrt),
           wait=(("ve", v_rc),), inc=(("se", 1),))
        se_inv = b.now("se")
        ve(lambda e, p=p: e.tensor_scalar(sb_yo[p][:, :], sb_y2[:, :],
                                          COL[0][:, :], COL[3][:, :], OP.add, OP.mult),
           wait=(("se", se_inv), (SEM_OUT[p], b.now(SEM_OUT[p]))), inc=(("ve", 1),))
        v_yo = b.now("ve")
        b.dma("sy", SEM_OUT[p], out_t[i * 128:(i + 1) * 128, :], sb_yo[p][:, :],
              wait=(("ve", v_yo),))
        if i + 2 < 16:
            issue_xr(i + 2)
    fw = [("Out0", b.now("Out0")), ("Out1", b.now("Out1"))]
    if dbg:
        fw.append(("Dbg", b.now("Dbg")))
    b.emit("sy", lambda e: e.nop(), wait=tuple(fw))
    _stacks.remove(sF)
    sF.close()

    b.build()
    es.close()
    nc.finalize()
    return nc


def stage_inputs(inputs):
    f32 = np.float32
    g = {k: np.asarray(v, f32) for k, v in inputs.items()}
    x = g["L0_attn_x"]
    cc = np.ascontiguousarray
    chunkcol = lambda v, n: cc(v.reshape(n, 128).T)
    com = {
        "wproj": cc(g["L0_attn_proj_w"].T.reshape(8, 128, 24, 128)
                    .transpose(2, 1, 0, 3)),
        "convw": cc(g["L0_attn_conv_w"][:, 0, :].reshape(12, 128, 4)
                    .transpose(1, 0, 2)),
        "convb": chunkcol(g["L0_attn_conv_b"], 12),
        "fl": chunkcol(g["L0_attn_forget_lambda"], 12),
        "wgates": cc(np.stack([g["L0_attn_gp_w"].T, g["L0_attn_ip_w"].T])
                     .reshape(2, 12, 128, 12, 128).transpose(3, 2, 0, 1, 4)),
        "gpb": chunkcol(g["L0_attn_gp_b"], 12),
        "ipb": chunkcol(g["L0_attn_ip_b"], 12),
        "wout": cc(g["L0_attn_out_w"].T.reshape(12, 128, 8, 128)
                   .transpose(2, 1, 0, 3)),
        "wr1": cc(g["L0_ffn_rW1"].reshape(8, 128, 32, 128).transpose(2, 1, 0, 3)),
        "rb1": chunkcol(g["L0_ffn_rb1"], 32),
        "wr2": cc(g["L0_ffn_rW2"].reshape(32, 128, 32, 128).transpose(2, 1, 0, 3)),
        "rb2": chunkcol(g["L0_ffn_rb2"], 32),
        "wr3": cc(g["L0_ffn_rW3"].reshape(32, 128, 8).transpose(1, 0, 2)),
        "rb3": cc(np.broadcast_to(g["L0_ffn_rb3"], (128, 8))),
        "we1": cc(g["L0_ffn_eW1"].reshape(NE, 8, 128, 8, 4, 128)
                  .transpose(0, 3, 2, 4, 1, 5)),
        "eb1": cc(g["L0_ffn_eb1"].reshape(NE, 32, 128).transpose(0, 2, 1)),
        "we2": cc(g["L0_ffn_eW2"].reshape(NE, 32, 128, D).transpose(0, 2, 1, 3)),
        "eb2": cc(np.broadcast_to(g["L0_ffn_eb2"][:, None, :], (NE, 128, D))),
    }
    in_maps = []
    for c in range(8):
        bb, h = c // 2, c % 2
        t0 = h * 2048
        xf = np.zeros((D, 2051), f32)
        if h == 0:
            xf[:, 3:] = x[bb, 0:2048].T
        else:
            xf[:, :] = x[bb, 2045:4096].T
        m = dict(com)
        m["x_fm"] = cc(xf)
        m["x_tm"] = cc(x[bb, t0:t0 + 2048])
        m["bflag"] = np.full((128, 1), float(h), f32)
        in_maps.append(m)
    return in_maps


def kernel(**inputs):
    in_maps = stage_inputs(inputs)
    nc = build_program(dbg=False)
    res = run_bass_kernel_spmd(nc, in_maps, core_ids=list(range(8)))
    out = np.empty((4, 4096, D), np.float32)
    for c in range(8):
        bb, h = c // 2, c % 2
        out[bb, h * 2048:(h + 1) * 2048, :] = res.results[c]["out"].reshape(T, D)
    return out


# version 04
# speedup vs baseline: 1.0420x; 1.0420x over previous
import numpy as np
from contextlib import ExitStack

import concourse.bacc as bacc
import concourse.mybir as mybir
from concourse.bass_utils import run_bass_kernel_spmd
from concourse.library_config import mlp as _LIB_MLP

F32 = mybir.dt.float32
BF16 = mybir.dt.bfloat16
I16 = mybir.dt.int16
AF = mybir.ActivationFunctionType
OP = mybir.AluOpType
AXX = mybir.AxisListType.X

D = 1024
T = 2048
NE = 8
CAP = 640
TRASH = NE * CAP
EPS = 1e-5
BIG = 1.0e9
ENGS = ("pe", "ve", "se", "gp", "sy")
RG_PAIRS = [[0, 1], [2, 3], [4, 5], [6, 7]]


class B:
    def __init__(self, nc, es):
        self.nc = nc
        self.es = es
        self.q = {e: [] for e in ENGS}
        self.sems = {}
        self.cnt = {}
        self._lw = {e: {} for e in ENGS}

    def sem(self, name):
        self.sems[name] = self.es.enter_context(self.nc.semaphore(name))
        self.cnt[name] = 0

    AUTO_SER = ("ve", "se", "gp")

    def emit(self, eng, fn, wait=(), inc=(), async_op=False):
        w = []
        lw = self._lw[eng]
        if eng in self.AUTO_SER and self.cnt[eng] > 0:
            wait = ((eng, self.cnt[eng]),) + tuple(wait)
        for s, v in wait:
            if v > 0 and lw.get(s, -1) < v:
                lw[s] = v
                w.append((s, v))
        if eng in self.AUTO_SER and not async_op:
            inc = tuple((s, n) for s, n in inc if s != eng) + ((eng, 1),)
        for s, n in inc:
            self.cnt[s] += n
        self.q[eng].append((tuple(w), fn, tuple(inc)))

    def now(self, s):
        return self.cnt[s]

    def dma(self, eng, sem, out, in_, wait=(), slow=False):
        self.emit(eng, lambda e, o=out, i=in_, sl=slow:
                  e.dma_start(out=o, in_=i, allow_slow_non_contiguous=sl),
                  wait=wait, inc=((sem, 16),), async_op=True)
        return self.cnt[sem]

    def build(self):
        block = self.es.enter_context(self.nc.Block())

        def mk(name):
            def body(e):
                for waits, fn, incs in self.q[name]:
                    for s, v in waits:
                        e.wait_ge(self.sems[s], v)
                    inst = fn(e)
                    for s, n in incs:
                        if inst is not None:
                            inst.then_inc(self.sems[s], n)
            return body

        block.tensor(mk("pe"))
        block.vector(mk("ve"))
        block.scalar(mk("se"))
        block.gpsimd(mk("gp"))
        block.sync(mk("sy"))


class Ring:

    def __init__(self, b, slots):
        self.b = b
        self.slots = slots
        self.free_at = [None] * len(slots)
        self.i = 0

    def acquire(self):
        i = self.i
        self.i = (self.i + 1) % len(self.slots)
        w = (self.free_at[i],) if self.free_at[i] else ()
        return i, self.slots[i], w

    def release(self, i, eng):
        self.free_at[i] = (eng, self.b.now(eng))

    def drain_waits(self):
        return tuple(fa for fa in self.free_at if fa)


SKIP = set()


def build_program(dbg=False, upto='F'):
    nc = bacc.Bacc("TRN2", target_bir_lowering=False, debug=False)
    es = ExitStack()
    b = B(nc, es)
    dbg_t = {}

    def din(name, shape):
        return nc.dram_tensor(name, list(shape), F32, kind="ExternalInput").ap()

    x_fm = din("x_fm", [D, 2051])
    x_tm = din("x_tm", [T, D])
    wproj = din("wproj", [24, 128, 8, 128])
    convw = din("convw", [128, 12, 4])
    convb = din("convb", [128, 12])
    fl_in = din("fl", [128, 12])
    wgates = din("wgates", [12, 128, 2, 12, 128])
    gpb = din("gpb", [128, 12])
    ipb = din("ipb", [128, 12])
    wout = din("wout", [8, 128, 12, 128])
    wr1 = din("wr1", [32, 128, 8, 128])
    rb1 = din("rb1", [128, 32])
    wr2 = din("wr2", [32, 128, 32, 128])
    rb2 = din("rb2", [128, 32])
    wr3 = din("wr3", [128, 32, 8])
    rb3 = din("rb3", [128, 8])
    we1 = din("we1", [NE, 8, 128, 4, 8, 128])
    eb1 = din("eb1", [NE, 128, 32])
    we2 = din("we2", [NE, 128, 32, D])
    eb2 = din("eb2", [NE, 128, D])
    bflag = din("bflag", [128, 1])
    out_t = nc.dram_tensor("out", [T, D], F32, kind="ExternalOutput").ap()

    disp_d = nc.dram_tensor("disp_d", [TRASH + 2, D], BF16).ap()
    eo_d = nc.dram_tensor("eo_d", [TRASH + 1, D], BF16).ap()
    xln_d = nc.dram_tensor("xln_d", [T, D], F32).ap()
    idx_d = nc.dram_tensor("idx_d", [6, 128, 128], I16).ap()
    cs_car = nc.dram_tensor("cs_car", [128, 12], F32)
    cg_car = nc.dram_tensor("cg_car", [2, 128, 12], F32)
    cs_cnt = nc.dram_tensor("cs_cnt", [8], F32)
    cg_cnt = nc.dram_tensor("cg_cnt", [2, 8], F32)
    if dbg:
        for nm, shp in [("d_gate", [128, 12, T]), ("d_xc", [128, 12, T]),
                        ("d_z", [128, 12, T]), ("d_attn", [128, 8, T]),
                        ("d_xln", [T, D]), ("d_sel", [128, 16, 8]),
                        ("d_pem", [8, T]), ("d_disp", [TRASH + 2, D]),
                        ("d_eo", [TRASH + 1, D]), ("d_g01", [2, 128, 16, D]),
                        ("dbh1", [128, 32, 512]), ("dbh2", [128, 32, 512]),
                        ("dbpp", [128, 16, 8])]:
            dbg_t[nm] = nc.dram_tensor(nm, shp, F32, kind="ExternalOutput").ap()

    for s in ["Wmisc", "Wpj0", "Wpj1", "Wpjf", "Wg0", "Wg1", "Wgf", "Wo0", "Wo1", "Wof",
              "Wr1a", "Wr1b", "Wr1c", "Wr1f", "Wr2c", "Wr2a", "Wr2b", "Wr2f", "We1a", "We1b", "We1f", "We2",
              "Web2", "Xf", "Xt0", "Xt1", "Xtf", "Spl", "Rl0", "Rl1", "Rlf",
              "Zer", "Dl0", "Dl1", "Eo0", "Eo1", "Out0", "Out1", "Idx", "Sca",
              "Gat", "CC", "CCc", "Dbg", "pe", "ve", "se", "gp", "sy"]:
        b.sem(s)

    SEM_OUT = ("Out0", "Out1")
    SEM_EO = ("Eo0", "Eo1")
    SEM_XT = ("Xt0", "Xt1")
    SEM_RL = ("Rl0", "Rl1")
    SEM_DL = ("Dl0", "Dl1")

    PA = lambda name, shape, dt=F32: es.enter_context(nc.sbuf_tensor(name, shape, dt))
    sb_convw = PA("c_convw", [128, 12, 4])
    sb_convb = PA("c_convb", [128, 12])
    sb_fl = PA("c_fl", [128, 12])
    sb_sfl = PA("c_sfl", [128, 12])
    sb_gpb = PA("c_gpb", [128, 12])
    sb_ipb = PA("c_ipb", [128, 12])
    sb_rb1 = PA("c_rb1", [128, 32])
    sb_rb2 = PA("c_rb2", [128, 32])
    sb_rb3 = PA("c_rb3", [128, 8])
    sb_eb1 = PA("c_eb1", [128, NE, 32])
    sb_bf = PA("c_bf", [128, 1])
    sb_io8 = PA("c_io8", [128, 8])
    sb_idb = PA("c_idb", [128, 128], BF16)
    sb_idf = PA("c_idf", [128, 128])
    sb_ones = PA("c_ones", [128, 128])
    sb_carry = PA("c_carry", [128, 12])
    sb_carin = PA("c_carin", [128, 12])
    sb_cnt = PA("c_cnt", [8, 1])
    sb_crow = PA("c_crow", [1, 8])
    sb_cntbc = PA("c_cntbc", [128, 8])
    sb_s0 = PA("c_s0", [128, 16])
    sb_s1 = PA("c_s1", [128, 16])
    sb_e0 = PA("c_e0", [128, 16])
    sb_e1 = PA("c_e1", [128, 16])
    sb_ps0 = PA("c_ps0", [128, 16])
    sb_ps1 = PA("c_ps1", [128, 16])
    sb_sl0 = PA("c_sl0", [128, 16])
    sb_sl1 = PA("c_sl1", [128, 16])
    sb_oh0 = PA("c_oh0", [128, 16, 8])
    sb_oh1 = PA("c_oh1", [128, 16, 8])
    sb_w0 = PA("c_w0", [128, 16])
    sb_w1 = PA("c_w1", [128, 16])
    sb_m16a = PA("c_m16a", [128, 16])
    sb_m16b = PA("c_m16b", [128, 16])
    sb_ln8 = PA("c_ln8", [128, 8])
    sb_pin = PA("c_pin", [8, 1])
    sb_trash = PA("c_trash", [128, 1])
    sb_beps = PA("c_beps", [128, 1])
    sb_zero = PA("c_zero", [128, D], BF16)
    sb_t16 = PA("c_t16", [16, 128], I16)
    sb_i16a = PA("c_i16a", [128, 128], I16)
    sb_i16c = PA("c_i16c", [128, 128], I16)
    sb_i16d = PA("c_i16d", [128, 128], I16)
    sb_i16b = PA("c_i16b", [128, 128], I16)
    sb_wr3 = PA("c_wr3", [128, 32, 8], BF16)
    COL = [PA(f"c_col{i}", [128, 1]) for i in range(6)]
    ROW8 = [PA(f"c_row8{i}", [128, 8]) for i in range(4)]

    pA = es.enter_context(nc.psum_tensor("pA", [128, 1024], F32))
    pB = es.enter_context(nc.psum_tensor("pB", [128, 1024], F32))
    pC = es.enter_context(nc.psum_tensor("pC", [128, 1024], F32))
    pT1 = es.enter_context(nc.psum_tensor("pT1", [128, 1024], BF16))
    pT2 = es.enter_context(nc.psum_tensor("pT2", [128, 1024], BF16))
    ring = Ring(b, [pA[:, 0:512], pA[:, 512:1024], pB[:, 0:512],
                    pB[:, 512:1024], pC[:, 0:512], pC[:, 512:1024]])


    _stacks = []

    def _finish_early():
        for s_ in reversed(_stacks):
            s_.close()
        _stacks.clear()
        waits = []
        for snm in ["Wmisc", "Wpj0", "Wpj1", "Wg0", "Wg1", "Wo0", "Wo1", "Wr1c",
                    "Wr2c", "Wr2a", "Wr2b", "Wr1a", "Wr1b", "We1a", "We1b", "We2",
                    "Web2", "Xf", "Xt0", "Xt1", "Spl", "Rl0", "Rl1", "Zer",
                    "Dl0", "Dl1", "Eo0", "Eo1", "Out0", "Out1", "Idx", "Sca",
                    "Gat", "CC", "Dbg"]:
            v = b.now(snm)
            if v > 0:
                waits.append((snm, v))
        b.emit("sy", lambda e: e.nop(), wait=tuple(waits))
        b.build()
        es.close()
        nc.finalize()
        return nc
    def ve(fn, wait=(), inc=()):
        b.emit("ve", fn, wait=wait, inc=inc)

    def se(fn, wait=(), inc=()):
        b.emit("se", fn, wait=wait, inc=inc)

    def pe(fn, wait=(), inc=()):
        b.emit("pe", fn, wait=wait, inc=inc)

    def gp(fn, wait=(), inc=()):
        b.emit("gp", fn, wait=wait, inc=inc)

    if "iota" not in SKIP:
        gp(lambda e: e.iota(sb_io8[:, :], [[1, 8]], base=0, channel_multiplier=0,
                            allow_small_or_imprecise_dtypes=True), inc=(("gp", 1),))
    else:
        gp(lambda e: e.memset(sb_io8[:, :], 0.0), inc=(("gp", 1),))
    gp(lambda e: e.memset(sb_ones[:, :], 1.0), inc=(("gp", 1),))
    gp(lambda e: e.memset(sb_trash[:, :], float(TRASH)), inc=(("gp", 1),))
    gp(lambda e: e.memset(sb_beps[:, :], 1.0 + 1e-6), inc=(("gp", 1),))
    gp(lambda e: e.memset(sb_zero[:, :], 0.0), inc=(("gp", 1),))
    gp(lambda e: e.affine_select(sb_idf[:, :], sb_ones[:, :], [[1, 128]],
                                 OP.is_equal, 0.0, base=0, channel_multiplier=-1),
       wait=(("gp", 5),), inc=(("gp", 1),))
    gp(lambda e: e.tensor_copy(sb_idb[:, :], sb_idf[:, :]),
       wait=(("gp", 6),), inc=(("gp", 1),))
    g_pro = b.now("gp")
    if "lib" not in SKIP:
        b.emit("gp", lambda e: e.load_library(_LIB_MLP), async_op=True)
    if "misc" not in SKIP:
        _miscl = [(sb_convw[:, :, :], convw[:, :, :]), (sb_convb[:, :], convb[:, :]),
                  (sb_fl[:, :], fl_in[:, :]), (sb_gpb[:, :], gpb[:, :]),
                  (sb_ipb[:, :], ipb[:, :]), (sb_rb1[:, :], rb1[:, :]),
                  (sb_rb2[:, :], rb2[:, :]), (sb_rb3[:, :], rb3[:, :]),
                  (sb_bf[:, :], bflag[:, :]), (sb_wr3[:, :, :], wr3[:, :, :])]
        for _mi, (dst, src) in enumerate(_miscl):
            if f"m{_mi}" in SKIP:
                continue
            b.dma("gp", "Wmisc", dst, src)
    if "eb1" not in SKIP:
        for e_ in range(NE):
            b.dma("gp", "Wmisc", sb_eb1[:, e_, :], eb1[e_, :, :])
    wmisc = b.now("Wmisc")
    if "sfl" not in SKIP:
        se(lambda e: e.activation(sb_sfl[:, :], sb_fl[:, :], AF.Exp),
           wait=(("Wmisc", wmisc),))
        se(lambda e: e.activation(sb_sfl[:, :], sb_sfl[:, :], AF.Ln, bias=1.0))
        se(lambda e: e.mul(sb_sfl[:, :], sb_sfl[:, :], -8.0), inc=(("se", 1),))
    se_sfl = b.now("se")
    if upto == 'P0':
        return _finish_early()
    for base in (disp_d, eo_d):
        nr = TRASH + 2 if base is disp_d else TRASH + 1
        for r0 in range(0, nr, 128):
            n = min(128, nr - r0)
            b.dma("sy", "Zer", base[r0:r0 + n, :], sb_zero[0:n, :],
                  wait=(("gp", g_pro),))
    zer_tot = b.now("Zer")
    if upto == 'P1':
        return _finish_early()

    if upto == 'P':
        return _finish_early()

    sGATE = ExitStack()
    _stacks.append(sGATE)
    sXC = ExitStack()
    _stacks.append(sXC)
    sb_gate = sGATE.enter_context(nc.sbuf_tensor("ab_gate", [128, 12, T], BF16))
    sb_xc = sXC.enter_context(nc.sbuf_tensor("ab_xc", [128, 12, T], BF16))
    sA = ExitStack()
    _stacks.append(sA)
    sb_xbf = sA.enter_context(nc.sbuf_tensor("a_xbf", [128, 8, 2051], BF16))
    sb_wpj = [sA.enter_context(nc.sbuf_tensor(f"a_wpj{i}", [128, 8, 128], BF16))
              for i in range(2)]
    sb_xh = sA.enter_context(nc.sbuf_tensor("a_xh", [128, 2051], F32))
    sb_cvt = sA.enter_context(nc.sbuf_tensor("a_cvt", [128, T], F32))

    for k in range(8):
        b.dma("gp", "Xf", sb_xbf[:, k, :], x_fm[k * 128:(k + 1) * 128, :])
    xf_tot = b.now("Xf")

    pj_tot = {}
    pj_fin = {}

    def issue_proj(m):
        p = m % 2
        w = (("pe", pj_fin[m - 2]),) if m >= 2 else ()
        pj_tot[m] = b.dma("gp", ("Wpj0", "Wpj1")[p], sb_wpj[p][:, :, :],
                          wproj[m, :, :, :], wait=w)

    issue_proj(0)
    issue_proj(1)
    for m in range(24):
        p = m % 2
        sem = ("Wpj0", "Wpj1")[p]
        is_gate = m < 12
        cts = [(3 + 512 * j, 512) for j in range(4)] if is_gate else \
              [(512 * j, 512) for j in range(4)] + [(2048, 3)]
        for ci, (c0, cw) in enumerate(cts):
            si, ps, pw = ring.acquire()
            for k in range(8):
                w = pw + ((sem, pj_tot[m]), ("Xf", xf_tot)) if k == 0 else ()
                pe(lambda e, o=ps[:, 0:cw], wt=sb_wpj[p][:, k, :],
                   rh=sb_xbf[:, k, c0:c0 + cw], st=(k == 0), sp=(k == 7):
                   e.matmul(o, wt, rh, start=st, stop=sp),
                   wait=w, inc=((("pe", 1),) if k == 7 else ()))
            mm_done = b.now("pe")
            if is_gate:
                se(lambda e, d=sb_gate[:, m, c0 - 3:c0 - 3 + cw], s=ps[:, 0:cw]:
                   e.activation(d, s, AF.Gelu),
                   wait=(("pe", mm_done),))
                ring.release(si, "se")
            else:
                ve(lambda e, d=sb_xh[:, c0:c0 + cw], s=ps[:, 0:cw]: e.tensor_copy(d, s),
                   wait=(("pe", mm_done),))
                ring.release(si, "ve")
        pj_fin[m] = b.now("pe")
        if m + 2 < 24:
            issue_proj(m + 2)
        if not is_gate:
            f = m - 12
            ve(lambda e, f=f: e.tensor_scalar(
                sb_cvt[:, :], sb_xh[:, 0:T], sb_convw[:, f, 0:1],
                sb_convb[:, f:f + 1], OP.mult, OP.add),
               wait=(("Wmisc", wmisc),))
            for kk in range(1, 4):
                ve(lambda e, f=f, kk=kk: e.scalar_tensor_tensor(
                    sb_cvt[:, :], sb_xh[:, kk:kk + T], sb_convw[:, f, kk:kk + 1],
                    sb_cvt[:, :], OP.mult, OP.add))
            ve(lambda e, f=f: e.tensor_copy(sb_xc[:, f, :], sb_cvt[:, :]))
    ve_xc_done = b.now("ve")
    se_gate_done = b.now("se")
    dbg_tot = 0
    if dbg:
        for f in range(12):
            b.dma("gp", "Dbg", dbg_t["d_gate"][:, f, :], sb_gate[:, f, :],
                  wait=(("se", se_gate_done), ("ve", ve_xc_done)))
            dbg_tot = b.dma("gp", "Dbg", dbg_t["d_xc"][:, f, :], sb_xc[:, f, :])
    _stacks.remove(sA)
    sA.close()
    if upto == 'A':
        return _finish_early()

    sB = ExitStack()
    _stacks.append(sB)
    sb_wg = [sB.enter_context(nc.sbuf_tensor(f"b_wg{i}", [128, 2, 12, 128], BF16))
             for i in range(2)]
    sb_al = sB.enter_context(nc.sbuf_tensor("b_al", [128, T], F32))
    sb_t1 = sB.enter_context(nc.sbuf_tensor("b_t1", [128, T], F32))
    sb_t2 = sB.enter_context(nc.sbuf_tensor("b_t2", [128, T], F32))
    sGAQ = ExitStack()
    _stacks.append(sGAQ)
    sb_gaq = sGAQ.enter_context(nc.sbuf_tensor("bc_gaq", [128, 12, T], BF16,
                                               side="right"))

    wg_tot = {}
    wg_fin = {}

    def issue_gw(f):
        p = f % 2
        w = (("pe", wg_fin[f - 2]),) if f >= 2 else             (("pe", pe_A_done), ("ve", ve_A_done))
        wg_tot[f] = b.dma("gp", ("Wg0", "Wg1")[p], sb_wg[p][:, :, :, :],
                          wgates[f, :, :, :, :], wait=w)

    pe_A_done = b.now("pe")
    ve_A_done = b.now("ve")

    issue_gw(0)
    issue_gw(1)
    ve_chain = 0
    for f in range(12):
        p = f % 2
        sem = ("Wg0", "Wg1")[p]
        for gi in range(2):
            for ct in range(4):
                si, ps, pw = ring.acquire()
                for k in range(12):
                    w = pw + ((sem, wg_tot[f]), ("ve", ve_xc_done),
                              ("Dbg", dbg_tot)) if k == 0 else ()
                    pe(lambda e, o=ps[:, 0:512], wt=sb_wg[p][:, gi, k, :],
                       rh=sb_xc[:, k, ct * 512:(ct + 1) * 512],
                       st=(k == 0), sp=(k == 11): e.matmul(o, wt, rh, start=st, stop=sp),
                       wait=w, inc=((("pe", 1),) if k == 11 else ()))
                mm_done = b.now("pe")
                sl = slice(ct * 512, (ct + 1) * 512)
                if gi == 0:
                    se(lambda e, d=sb_t2[:, sl], s=ps[:, 0:512], bi=sb_gpb[:, f:f + 1]:
                       e.activation(d, s, AF.Sigmoid, bias=bi),
                       wait=(("pe", mm_done), ("se", se_sfl), ("ve", ve_chain)),
                       inc=(("se", 1),))
                    se(lambda e, d=sb_al[:, sl], s=sb_t2[:, sl], sc=sb_sfl[:, f:f + 1]:
                       e.activation(d, s, AF.Exp, scale=sc), inc=(("se", 1),))
                else:
                    se(lambda e, d=sb_t1[:, sl], s=ps[:, 0:512], bi=sb_ipb[:, f:f + 1]:
                       e.activation(d, s, AF.Sigmoid, bias=bi),
                       wait=(("pe", mm_done), ("ve", ve_chain)), inc=(("se", 1),))
                ring.release(si, "se")
        wg_fin[f] = b.now("pe")
        se_g = b.now("se")
        ve(lambda e: e.tensor_mul(sb_t2[:, :], sb_al[:, :], sb_al[:, :]),
           wait=(("se", se_g),), inc=(("ve", 1),))
        v_sq = b.now("ve")
        se(lambda e: e.activation(sb_t2[:, :], sb_t2[:, :], AF.Sqrt,
                                  bias=sb_beps[:, :], scale=-1.0),
           wait=(("ve", v_sq), ("gp", g_pro)), inc=(("se", 1),))
        se_b = b.now("se")
        ve(lambda e: e.tensor_mul(sb_t1[:, :], sb_t1[:, :], sb_t2[:, :]),
           wait=(("se", se_b),))
        ve(lambda e, f=f: e.tensor_mul(sb_t1[:, :], sb_t1[:, :], sb_xc[:, f, :]))
        ve(lambda e: e.tensor_tensor_scan(sb_t2[:, :], sb_al[:, :], sb_t1[:, :],
                                          0.0, OP.mult, OP.add))
        ve(lambda e: e.tensor_tensor_scan(sb_t1[:, :], sb_al[:, :], sb_al[:, :],
                                          1.0, OP.mult, OP.bypass))
        ve(lambda e, f=f: e.tensor_copy(sb_carry[:, f:f + 1], sb_t2[:, T - 1:T]))
        ve(lambda e, f=f: e.tensor_mul(sb_gaq[:, f, :], sb_gate[:, f, :], sb_t1[:, :]),
           wait=(("se", se_gate_done),))
        ve(lambda e, f=f: e.tensor_mul(sb_gate[:, f, :], sb_gate[:, f, :], sb_t2[:, :]))
        ve_chain = b.now("ve")
        if f + 2 < 12:
            issue_gw(f + 2)
    ve_scan_done = b.now("ve")

    cc0 = b.dma("gp", "CC", cs_car.ap()[:, :], sb_carry[:, :],
                wait=(("ve", ve_scan_done),))
    b.emit("gp", lambda e: e.collective_compute(
        "AllGather", OP.bypass, replica_groups=RG_PAIRS,
        ins=[cs_car.ap().opt()], outs=[cg_car.ap().opt()]),
        wait=(("CC", cc0),), inc=(("CCc", 1),), async_op=True)
    cc1 = b.now("CCc")
    cc2 = b.dma("gp", "CC", sb_carin[:, :], cg_car.ap()[0, :, :],
                wait=(("CCc", cc1),))
    ve(lambda e: e.tensor_scalar_mul(sb_carin[:, :], sb_carin[:, :], sb_bf[:, 0:1]),
       wait=(("CC", cc2),), inc=(("ve", 1),))
    ve_car = b.now("ve")
    _stacks.remove(sB)
    sB.close()
    _stacks.remove(sXC)
    sXC.close()
    if upto == 'B':
        return _finish_early()

    for f in range(12):
        ve(lambda e, f=f: e.scalar_tensor_tensor(
            sb_gate[:, f, :], sb_gaq[:, f, :], sb_carin[:, f:f + 1],
            sb_gate[:, f, :], OP.mult, OP.add),
           wait=(("ve", ve_car),), inc=((("ve", 1),) if f == 11 else ()))
    ve_z = b.now("ve")
    if dbg:
        for f in range(12):
            dbg_tot = b.dma("gp", "Dbg", dbg_t["d_z"][:, f, :], sb_gate[:, f, :],
                            wait=(("ve", ve_z),))

    _stacks.remove(sGAQ)
    sGAQ.close()
    sATT = ExitStack()
    _stacks.append(sATT)
    sb_attn = sATT.enter_context(nc.sbuf_tensor("c_attn", [128, 8, T], BF16,
                                                side="right"))
    sWO = ExitStack()
    _stacks.append(sWO)
    sb_wo = [sWO.enter_context(nc.sbuf_tensor(f"cw{i}", [128, 12, 128], BF16))
             for i in range(2)]
    wo_tot = {}

    wo_fin = {}

    def issue_wo(m):
        p = m % 2
        w = (("pe", wo_fin[m - 2]),) if m >= 2 else             (("pe", wg_fin[11]), ("ve", ve_scan_done))
        wo_tot[m] = b.dma("gp", ("Wo0", "Wo1")[p], sb_wo[p][:, :, :],
                          wout[m, :, :, :], wait=w)

    issue_wo(0)
    issue_wo(1)
    for m in range(8):
        p = m % 2
        sem = ("Wo0", "Wo1")[p]
        for ct in range(4):
            si, ps, pw = ring.acquire()
            for k in range(12):
                w = pw + ((sem, wo_tot[m]), ("ve", ve_z)) if k == 0 else ()
                pe(lambda e, o=ps[:, 0:512], wt=sb_wo[p][:, k, :],
                   rh=sb_gate[:, k, ct * 512:(ct + 1) * 512],
                   st=(k == 0), sp=(k == 11): e.matmul(o, wt, rh, start=st, stop=sp),
                   wait=w, inc=((("pe", 1),) if k == 11 else ()))
            mm_done = b.now("pe")
            se(lambda e, d=sb_attn[:, m, ct * 512:(ct + 1) * 512], s=ps[:, 0:512]:
               e.copy(d, s), wait=(("pe", mm_done),))
            ring.release(si, "se")
        wo_fin[m] = b.now("pe")
        if m + 2 < 8:
            issue_wo(m + 2)
    se_attn = b.now("se")
    _stacks.remove(sWO)
    sWO.close()
    _stacks.remove(sGATE)
    sGATE.close()
    sCD = ExitStack()
    _stacks.append(sCD)
    sb_xtmb = sCD.enter_context(nc.sbuf_tensor("cd_xtmb", [128, 16, D], BF16))
    sC = ExitStack()
    _stacks.append(sC)
    sb_xtc = [sC.enter_context(nc.sbuf_tensor(f"c_xt{i}", [128, D], F32))
              for i in range(2)]
    sb_ytc = [sC.enter_context(nc.sbuf_tensor(f"c_ytc{i}", [128, D], F32))
              for i in range(2)]
    sb_sq = [sC.enter_context(nc.sbuf_tensor(f"c_sq{i}", [128, D], F32))
             for i in range(2)]
    sb_xlc = [sC.enter_context(nc.sbuf_tensor(f"c_xl{i}", [128, D], F32))
              for i in range(2)]
    if dbg:
        for m in range(8):
            dbg_tot = b.dma("gp", "Dbg", dbg_t["d_attn"][:, m, :], sb_attn[:, m, :],
                            wait=(("se", se_attn),))

    pe_attn_done = b.now("pe")
    xt_tot = {}
    xt_fin = {}

    def issue_xt(i):
        p = i % 2
        w = (("ve", xt_fin[i - 2]),) if i >= 2 else \
            (("pe", pe_attn_done), ("Dbg", dbg_tot))
        xt_tot[i] = b.dma("sy", SEM_XT[p], sb_xtc[p][:, :],
                          x_tm[i * 128:(i + 1) * 128, :], wait=w)

    issue_xt(0)
    issue_xt(1)
    ve_pt_free = {0: 0, 1: 0}
    spl_prev = {0: 0, 1: 0}
    se_sumsC = {}

    def LN8(p, k):
        return sb_ln8[:, p * 4 + k:p * 4 + k + 1]

    def c_phase1(i):
        p = i % 2
        pt = (pT1, pT2)[p]
        for c in range(8):
            w = (("se", se_attn), ("ve", ve_pt_free[p]), ("gp", g_pro)) if c == 0 else ()
            pe(lambda e, o=pt[:, c * 128:(c + 1) * 128],
               s=sb_attn[:, c, i * 128:(i + 1) * 128]: e.transpose(o, s, sb_idb[:, :]),
               wait=w, inc=((("pe", 1),) if c == 7 else ()))
        pe_tr = b.now("pe")
        yt = sb_ytc[p]
        ve(lambda e, p=p, pt=pt, yt=yt: e.tensor_add(yt[:, :], sb_xtc[p][:, :],
                                                     pt[:, :]),
           wait=(("pe", pe_tr), (SEM_XT[p], xt_tot[i]),
                 ("se", se_sumsC.get(i - 2, 0))), inc=(("ve", 1),))
        ve_pt_free[p] = b.now("ve")
        xt_fin[i] = b.now("ve")
        se(lambda e, yt=yt, sq=sb_sq[p], c=LN8(p, 0):
           e.activation(sq[:, :], yt[:, :], AF.Copy, accum_out=c),
           wait=(("ve", xt_fin[i]),))
        se(lambda e, yt=yt, sq=sb_sq[p], c=LN8(p, 1):
           e.activation(sq[:, :], yt[:, :], AF.Square, accum_out=c),
           inc=(("se", 1),))
        se_sumsC[i] = b.now("se")
        if i + 2 < 16:
            issue_xt(i + 2)

    def c_phase2(i):
        p = i % 2
        yt = sb_ytc[p]
        c0, c1, c2, c3 = LN8(p, 0), LN8(p, 1), LN8(p, 2), LN8(p, 3)
        ve(lambda e, c=c0: e.tensor_scalar_mul(c, c, -1.0 / D),
           wait=(("se", se_sumsC[i]),))
        ve(lambda e, c=c1: e.tensor_scalar(c, c, 1.0 / D, EPS, OP.mult, OP.add))
        ve(lambda e, a=c2, c=c0: e.tensor_mul(a, c, c))
        ve(lambda e, a=c1, bq=c2: e.tensor_sub(a, a, bq))
        ve(lambda e, a=c2, c=c1: e.reciprocal(a, c), inc=(("ve", 1),))
        v_s = b.now("ve")
        se(lambda e, a=c3, c=c2: e.activation(a, c, AF.Sqrt),
           wait=(("ve", v_s),), inc=(("se", 1),))
        se_i = b.now("se")
        ve(lambda e, p=p, yt=yt, c0=c0, c3=c3: e.tensor_scalar(
            sb_xlc[p][:, :], yt[:, :], c0, c3, OP.add, OP.mult),
           wait=(("se", se_i), ("Spl", spl_prev[p])), inc=(("ve", 1),))
        v_xl = b.now("ve")
        spl_prev[p] = b.dma("sy", "Spl", xln_d[i * 128:(i + 1) * 128, :],
                            sb_xlc[p][:, :], wait=(("ve", v_xl),))
        ve(lambda e, i=i, p=p: e.tensor_copy(sb_xtmb[:, i, :], sb_xlc[p][:, :]),
           inc=(("ve", 1),))

    c_phase1(0)
    for i in range(1, 16):
        c_phase1(i)
        c_phase2(i - 1)
    c_phase2(15)
    spl_tot = b.now("Spl")
    ve_xln = b.now("ve")
    if dbg:
        dbg_tot = b.dma("gp", "Dbg", dbg_t["d_xln"][:, :], xln_d[:, :],
                        wait=(("Spl", spl_tot),))
    _stacks.remove(sC)
    sC.close()
    _stacks.remove(sATT)
    sATT.close()
    if upto == 'C':
        return _finish_early()

    sD = ExitStack()
    _stacks.append(sD)
    sb_o8 = sD.enter_context(nc.sbuf_tensor("s_o8", [8, T], F32))
    sb_mem = sD.enter_context(nc.sbuf_tensor("s_mem", [8, T], F32))
    sb_pincl = sD.enter_context(nc.sbuf_tensor("s_pincl", [8, T], F32))
    sb_pem = sD.enter_context(nc.sbuf_tensor("s_pem", [8, T], F32))
    sb_wr1s = [sD.enter_context(nc.sbuf_tensor(f"d_wr1{i}", [128, 8, 128], BF16))
               for i in range(2)]
    sb_xfm = sD.enter_context(nc.sbuf_tensor("d_xfm", [128, 8, 512], BF16))
    sb_h1 = sD.enter_context(nc.sbuf_tensor("d_h1", [128, 32, 512], BF16))
    sb_h2 = sD.enter_context(nc.sbuf_tensor("d_h2", [128, 32, 512], BF16))
    sb_wr2 = [sD.enter_context(nc.sbuf_tensor(f"d_wr2{i}", [128, 32, 128], BF16))
              for i in range(2)]
    sb_lg = sD.enter_context(nc.sbuf_tensor("d_lg", [128, 8], F32))
    sb_pp = sD.enter_context(nc.sbuf_tensor("d_pp", [128, 8], F32))
    sb_ptok = sD.enter_context(nc.sbuf_tensor("d_ptok", [128, 8], F32))

    gp(lambda e: e.memset(sb_o8[:, :], 1.0),
       wait=(("ve", ve_xln), ("Spl", spl_tot)))
    g_o8 = b.now("gp")
    r1_tot = {}
    r1_fin = {}

    def issue_r1(mg):
        p = mg % 2
        w = []
        if mg >= 2:
            w.append(("pe", r1_fin[mg - 2]))
        else:
            w += [("ve", ve_xln), ("Spl", spl_tot)]
        r1_tot[mg] = b.dma("gp", ("Wr1a", "Wr1b")[p], sb_wr1s[p][:, :, :],
                           wr1[mg % 32, :, :, :], wait=tuple(w))

    issue_r1(0)
    issue_r1(1)
    r2_tot = {}
    r2_fin = {}

    def issue_r2(mg):
        p = mg % 2
        w = []
        if mg >= 2:
            w.append(("pe", r2_fin[mg - 2]))
        else:
            w += [("ve", ve_xln), ("Spl", spl_tot)]
        r2_tot[mg] = b.dma("gp", ("Wr2a", "Wr2b")[p], sb_wr2[p][:, :, :],
                           wr2[mg % 32, :, :, :], wait=tuple(w))

    issue_r2(0)
    issue_r2(1)
    for blk in range(4):
        for j in range(4):
            tc = blk * 4 + j
            p = j % 2
            pt = (pT1, pT2)[p]
            for c in range(8):
                w = (("ve", ve_xln), ("ve", ve_pt_free[p])) if c == 0 else ()
                pe(lambda e, o=pt[:, c * 128:(c + 1) * 128],
                   s=sb_xtmb[:, tc, c * 128:(c + 1) * 128]:
                   e.transpose(o, s, sb_idb[:, :]),
                   wait=w, inc=((("pe", 1),) if c == 7 else ()))
            pe_tr = b.now("pe")
            ve(lambda e, j=j, pt=pt: e.tensor_copy(
                sb_xfm[:, :, j * 128:(j + 1) * 128],
                pt[:, :].rearrange("p (c t) -> p c t", c=8)),
               wait=(("pe", pe_tr),), inc=(("ve", 1),))
            ve_pt_free[p] = b.now("ve")
        ve_xfm = b.now("ve")
        for m in range(32):
            mg1 = blk * 32 + m
            p1 = mg1 % 2
            sem1 = ("Wr1a", "Wr1b")[p1]
            si, ps, pw = ring.acquire()
            for k in range(8):
                w = pw + ((sem1, r1_tot[mg1]), ("ve", ve_xfm),
                          ("Dbg", dbg_tot)) if k == 0 else ()
                pe(lambda e, o=ps[:, 0:512], wt=sb_wr1s[p1][:, k, :],
                   rh=sb_xfm[:, k, :], st=(k == 0), sp=(k == 7):
                   e.matmul(o, wt, rh, start=st, stop=sp),
                   wait=w, inc=((("pe", 1),) if k == 7 else ()))
            mm_done = b.now("pe")
            r1_fin[mg1] = mm_done
            se(lambda e, m=m, s=ps[:, 0:512], bi=sb_rb1[:, m:m + 1]:
               e.activation(sb_h1[:, m, :], s, AF.Relu, bias=bi),
               wait=(("pe", mm_done),))
            ring.release(si, "se")
            if mg1 + 2 < 128:
                issue_r1(mg1 + 2)
        se_h1 = b.now("se")
        for m in range(32):
            mg = blk * 32 + m
            p = mg % 2
            sem = ("Wr2a", "Wr2b")[p]
            si, ps, pw = ring.acquire()
            for k in range(32):
                w = pw + ((sem, r2_tot[mg]), ("se", se_h1)) if k == 0 else ()
                pe(lambda e, o=ps[:, 0:512], wt=sb_wr2[p][:, k, :],
                   rh=sb_h1[:, k, :], st=(k == 0), sp=(k == 31):
                   e.matmul(o, wt, rh, start=st, stop=sp),
                   wait=w, inc=((("pe", 1),) if k == 31 else ()))
            mm_done = b.now("pe")
            r2_fin[mg] = mm_done
            se(lambda e, m=m, s=ps[:, 0:512], bi=sb_rb2[:, m:m + 1]:
               e.activation(sb_h2[:, m, :], s, AF.Relu, bias=bi),
               wait=(("pe", mm_done),))
            ring.release(si, "se")
            if mg + 2 < 128:
                issue_r2(mg + 2)
        se_h2 = b.now("se")
        if dbg and blk == 0:
            for m_ in range(32):
                dbg_tot = b.dma("gp", "Dbg", dbg_t["dbh1"][:, m_, :], sb_h1[:, m_, :],
                                wait=(("se", se_h2),))
                dbg_tot = b.dma("gp", "Dbg", dbg_t["dbh2"][:, m_, :], sb_h2[:, m_, :])
        for j in range(4):
            tc = blk * 4 + j
            si, ps, pw = ring.acquire()
            for k in range(32):
                w = pw + (("se", se_h2), ("Wmisc", wmisc)) if k == 0 else ()
                pe(lambda e, o=ps[:, 0:8], wt=sb_h2[:, k, j * 128:(j + 1) * 128],
                   rh=sb_wr3[:, k, :], st=(k == 0), sp=(k == 31):
                   e.matmul(o, wt, rh, start=st, stop=sp),
                   wait=w, inc=((("pe", 1),) if k == 31 else ()))
            mm_done = b.now("pe")
            ve(lambda e, s=ps[:, 0:8]: e.tensor_add(sb_lg[:, :], s, sb_rb3[:, :]),
               wait=(("pe", mm_done), ("gp", g_pro)), inc=(("ve", 1),))
            ring.release(si, "ve")
            ve(lambda e: e.reduce_max(COL[0][:, :], sb_lg[:, :], axis=AXX, negate=True),
               inc=(("ve", 1),))
            v_nm = b.now("ve")
            se(lambda e: e.activation(sb_pp[:, :], sb_lg[:, :], AF.Exp,
                                      bias=COL[0][:, :]),
               wait=(("ve", v_nm), ("Dbg", dbg_tot)))
            se_ex = b.now("se")
            ve(lambda e: e.reduce_sum(COL[1][:, :], sb_pp[:, :], axis=AXX),
               wait=(("se", se_ex),))
            ve(lambda e: e.reciprocal(COL[2][:, :], COL[1][:, :]))
            ve(lambda e: e.tensor_scalar_mul(sb_pp[:, :], sb_pp[:, :], COL[2][:, :]))
            if dbg:
                dbg_tot = b.dma("gp", "Dbg", dbg_t["dbpp"][:, tc, :], sb_pp[:, :],
                                wait=(("ve", b.now("ve")),))
            ve(lambda e, tc=tc: e.reduce_max(sb_s0[:, tc:tc + 1], sb_pp[:, :], axis=AXX))
            ve(lambda e, tc=tc: e.tensor_scalar(ROW8[0][:, :], sb_pp[:, :],
                                                sb_s0[:, tc:tc + 1], None, OP.is_equal))
            ve(lambda e: e.tensor_scalar(ROW8[1][:, :], ROW8[0][:, :], -BIG, BIG,
                                         OP.mult, OP.add))
            ve(lambda e: e.tensor_add(ROW8[1][:, :], ROW8[1][:, :], sb_io8[:, :]))
            ve(lambda e, tc=tc: e.tensor_reduce(sb_e0[:, tc:tc + 1], ROW8[1][:, :],
                                                AXX, OP.min))
            ve(lambda e, tc=tc: e.tensor_scalar(sb_oh0[:, tc, :], sb_io8[:, :],
                                                sb_e0[:, tc:tc + 1], None, OP.is_equal))
            ve(lambda e, tc=tc: e.tensor_scalar(ROW8[1][:, :], sb_oh0[:, tc, :],
                                                -BIG, None, OP.mult))
            ve(lambda e: e.tensor_add(ROW8[1][:, :], ROW8[1][:, :], sb_pp[:, :]))
            ve(lambda e, tc=tc: e.reduce_max(sb_s1[:, tc:tc + 1], ROW8[1][:, :],
                                             axis=AXX))
            ve(lambda e, tc=tc: e.tensor_scalar(ROW8[2][:, :], ROW8[1][:, :],
                                                sb_s1[:, tc:tc + 1], None, OP.is_equal))
            ve(lambda e: e.tensor_scalar(ROW8[2][:, :], ROW8[2][:, :], -BIG, BIG,
                                         OP.mult, OP.add))
            ve(lambda e: e.tensor_add(ROW8[2][:, :], ROW8[2][:, :], sb_io8[:, :]))
            ve(lambda e, tc=tc: e.tensor_reduce(sb_e1[:, tc:tc + 1], ROW8[2][:, :],
                                                AXX, OP.min))
            ve(lambda e, tc=tc: e.tensor_scalar(sb_oh1[:, tc, :], sb_io8[:, :],
                                                sb_e1[:, tc:tc + 1], None, OP.is_equal))
            ve(lambda e, tc=tc: e.tensor_add(ROW8[3][:, :], sb_oh0[:, tc, :],
                                             sb_oh1[:, tc, :]), inc=(("ve", 1),))
            v_m = b.now("ve")
            si2, ps2, pw2 = ring.acquire()
            pe(lambda e, o=ps2[0:8, 0:128], s=ROW8[3][:, :]:
               e.transpose(o, s, sb_idf[:, :]),
               wait=pw2 + (("ve", v_m),), inc=(("pe", 1),))
            pe_mt = b.now("pe")
            ve(lambda e, tc=tc, s=ps2[0:8, 0:128]:
               e.tensor_copy(sb_mem[0:8, tc * 128:(tc + 1) * 128], s),
               wait=(("pe", pe_mt),), inc=(("ve", 1),))
            ring.release(si2, "ve")
        ve_mem = b.now("ve")
        seg = slice(blk * 512, (blk + 1) * 512)
        init = 0.0 if blk == 0 else sb_pin[0:8, 0:1]
        ve(lambda e, seg=seg, init=init: e.tensor_tensor_scan(
            sb_pincl[0:8, seg], sb_o8[0:8, seg], sb_mem[0:8, seg],
            init, OP.mult, OP.add), wait=(("ve", ve_mem), ("gp", g_o8)))
        ve(lambda e, seg=seg: e.tensor_sub(sb_pem[0:8, seg], sb_pincl[0:8, seg],
                                           sb_mem[0:8, seg]))
        ve(lambda e, blk=blk: e.tensor_copy(
            sb_pin[0:8, 0:1], sb_pincl[0:8, (blk + 1) * 512 - 1:(blk + 1) * 512]),
           inc=(("ve", 1),))
        ve_pscan = b.now("ve")
        for j in range(4):
            tc = blk * 4 + j
            si2, ps2, pw2 = ring.acquire()
            pe(lambda e, o=ps2[:, 0:8], s=sb_pem[0:8, tc * 128:(tc + 1) * 128]:
               e.transpose(o, s, sb_idf[0:8, 0:8]),
               wait=pw2 + (("ve", ve_pscan),), inc=(("pe", 1),))
            pe_pt = b.now("pe")
            ve(lambda e, s=ps2[:, 0:8]: e.tensor_copy(sb_ptok[:, :], s),
               wait=(("pe", pe_pt),), inc=(("ve", 1),))
            ring.release(si2, "ve")
            for oh, pssel, esel, slsel, tval in [
                    (sb_oh0, sb_ps0, sb_e0, sb_sl0, float(TRASH)),
                    (sb_oh1, sb_ps1, sb_e1, sb_sl1, float(TRASH + 1))]:
                ve(lambda e, oh=oh, tc=tc: e.tensor_mul(ROW8[0][:, :],
                                                        sb_ptok[:, :], oh[:, tc, :]))
                ve(lambda e, p_=pssel, tc=tc: e.reduce_sum(p_[:, tc:tc + 1],
                                                           ROW8[0][:, :], axis=AXX))
                ve(lambda e, p_=pssel, e_=esel, s_=slsel, tc=tc:
                   e.scalar_tensor_tensor(s_[:, tc:tc + 1], e_[:, tc:tc + 1],
                                          float(CAP), p_[:, tc:tc + 1],
                                          OP.mult, OP.add))
                ve(lambda e, p_=pssel, tc=tc: e.tensor_scalar(
                    COL[4][:, :], p_[:, tc:tc + 1], float(CAP), None, OP.is_lt))
                ve(lambda e, s_=slsel, tc=tc, tv=tval: e.scalar_tensor_tensor(
                    COL[5][:, :], s_[:, tc:tc + 1], tv, COL[4][:, :],
                    OP.subtract, OP.mult))
                ve(lambda e, s_=slsel, tc=tc, tv=tval: e.tensor_scalar(
                    s_[:, tc:tc + 1], COL[5][:, :], tv, None, OP.add))
    ve_route = b.now("ve")
    if dbg:
        dbg_tot = b.dma("gp", "Dbg", dbg_t["d_pem"][:, :], sb_pem[0:8, :],
                        wait=(("ve", ve_route),))
        for j, tile in enumerate([sb_s0, sb_s1, sb_e0, sb_e1, sb_ps0, sb_ps1,
                                  sb_sl0, sb_sl1]):
            dbg_tot = b.dma("gp", "Dbg", dbg_t["d_sel"][:, :, j], tile[:, :],
                            wait=(("ve", ve_route),), slow=True)

    cc3 = b.dma("gp", "CC", cs_cnt.ap()[:], sb_pincl[0:8, T - 1:T],
                wait=(("ve", ve_route),))
    b.emit("gp", lambda e: e.collective_compute(
        "AllGather", OP.bypass, replica_groups=RG_PAIRS,
        ins=[cs_cnt.ap().opt()], outs=[cg_cnt.ap().opt()]),
        wait=(("CC", cc3),), inc=(("CCc", 1),), async_op=True)
    cc4 = b.now("CCc")
    cc5 = b.dma("gp", "CC", sb_cnt[0:8, 0:1], cg_cnt.ap()[0, :], wait=(("CCc", cc4),))
    ve(lambda e: e.tensor_scalar_mul(sb_cnt[0:8, 0:1], sb_cnt[0:8, 0:1],
                                     sb_bf[0:8, 0:1]),
       wait=(("CC", cc5),), inc=(("ve", 1),))
    v_cnt = b.now("ve")
    si2, ps2, pw2 = ring.acquire()
    pe(lambda e, o=ps2[0:1, 0:8]: e.transpose(o, sb_cnt[0:8, 0:1], sb_idf[0:8, 0:8]),
       wait=pw2 + (("ve", v_cnt),), inc=(("pe", 1),))
    pe_ct = b.now("pe")
    ve(lambda e, s=ps2[0:1, 0:8]: e.tensor_copy(sb_crow[0:1, :], s),
       wait=(("pe", pe_ct),), inc=(("ve", 1),))
    v_cr = b.now("ve")
    ring.release(si2, "ve")
    si2, ps2, pw2 = ring.acquire()
    pe(lambda e, o=ps2[:, 0:8]: e.matmul(o, sb_ones[0:1, :], sb_crow[0:1, :],
                                         start=True, stop=True),
       wait=pw2 + (("ve", v_cr), ("gp", g_pro)), inc=(("pe", 1),))
    pe_cb = b.now("pe")
    ve(lambda e, s=ps2[:, 0:8]: e.tensor_copy(sb_cntbc[:, :], s),
       wait=(("pe", pe_cb),), inc=(("ve", 1),))
    ring.release(si2, "ve")
    v_cbc = b.now("ve")


    def build_idx(slsel, i16t, region, base_wait):
        si_, ps_, pw_ = ring.acquire()
        pe(lambda e, o=ps_[0:16, 0:128], s=slsel[:, :]:
           e.transpose(o, s, sb_idf[:, :]),
           wait=pw_ + base_wait, inc=(("pe", 1),))
        pe_t = b.now("pe")
        ve(lambda e, s=ps_[0:16, 0:128]: e.tensor_copy(sb_t16[:, :], s),
           wait=(("pe", pe_t), ("Idx", b.now("Idx"))))
        v_t = b.now("ve")
        ring.release(si_, "ve")
        dst = idx_d[region, :, :].rearrange("(a s) (r q) -> a s r q", s=8, q=16)
        t_ = 0
        for r in range(8):
            t_ = b.dma("sy", "Idx",
                       dst[:, :, r, :].rearrange("a s q -> a s q") if False else
                       idx_d[region, :, :].rearrange("(a s) j -> a s j", s=8)[:, :, 16 * r:16 * (r + 1)],
                       sb_t16[:, :].rearrange("a (s q) -> a s q", q=16),
                       wait=(("ve", v_t),) if r == 0 else ())
        b.emit("sy", lambda e, d=i16t[:, :], s=idx_d[region, :, :]:
               e.dma_start_transpose(d, s),
               wait=(("Idx", t_),), inc=(("Idx", 16),), async_op=True)
        return b.now("Idx")

    idx_tot = 0
    for sl, i16t, reg in [(sb_sl0, sb_i16a, 0), (sb_sl1, sb_i16b, 1)]:
        idx_tot = build_idx(sl, i16t, reg,
                            (("ve", ve_route), ("Dbg", dbg_tot)))
    for i16t in (sb_i16a, sb_i16b):
        b.emit("gp", lambda e, t=i16t: e.dma_scatter_add(disp_d[:, :], sb_xtmb[:, :, :],
                                                         t[:, :], T, T, D),
               wait=(("Idx", idx_tot), ("Zer", zer_tot), ("ve", ve_xln)),
               inc=(("Sca", 16),), async_op=True)
    sca_tot = b.now("Sca")
    pe_selidx = b.now("pe")
    for esel, pssel, ssel, slsel, wsel in [
            (sb_e0, sb_ps0, sb_s0, sb_sl0, sb_w0),
            (sb_e1, sb_ps1, sb_s1, sb_sl1, sb_w1)]:
        ve(lambda e, es=esel: e.tensor_scalar(sb_m16a[:, :], es[:, :], 0.0,
                                              None, OP.is_equal),
           wait=(("ve", v_cbc), ("pe", pe_selidx)))
        ve(lambda e: e.tensor_scalar_mul(sb_m16b[:, :], sb_m16a[:, :],
                                         sb_cntbc[:, 0:1]))
        for ee in range(1, 8):
            ve(lambda e, es=esel, ee=ee: e.tensor_scalar(
                sb_m16a[:, :], es[:, :], float(ee), None, OP.is_equal))
            ve(lambda e, ee=ee: e.scalar_tensor_tensor(
                sb_m16b[:, :], sb_m16a[:, :], sb_cntbc[:, ee:ee + 1],
                sb_m16b[:, :], OP.mult, OP.add))
        ve(lambda e, p_=pssel: e.tensor_add(sb_m16b[:, :], sb_m16b[:, :], p_[:, :]))
        ve(lambda e: e.tensor_scalar(sb_m16a[:, :], sb_m16b[:, :], float(CAP),
                                     None, OP.is_lt))
        ve(lambda e, s_=ssel, w_=wsel: e.tensor_mul(w_[:, :], s_[:, :],
                                                    sb_m16a[:, :]))
        ve(lambda e, sl=slsel: e.scalar_tensor_tensor(
            sb_m16b[:, :], sl[:, :], float(TRASH), sb_m16a[:, :],
            OP.subtract, OP.mult))
        ve(lambda e, sl=slsel: e.tensor_scalar(
            sl[:, :], sb_m16b[:, :], float(TRASH), None, OP.add))
    v_keep = b.now("ve")
    if upto == 'F0':
        return _finish_early()
    gidx_tot = 0
    for sl, i16t, reg in [(sb_sl0, sb_i16c, 2), (sb_sl1, sb_i16d, 3)]:
        gidx_tot = build_idx(sl, i16t, reg,
                             (("ve", v_keep), ("Sca", sca_tot)))
    if dbg:
        dbg_tot = b.dma("gp", "Dbg", dbg_t["d_disp"][:, :], disp_d[:, :],
                        wait=(("Sca", sca_tot),))
    pe_router_done = b.now("pe")
    _stacks.remove(sD)
    sD.close()
    _stacks.remove(sCD)
    sCD.close()
    if upto == 'D':
        return _finish_early()

    sE = ExitStack()
    _stacks.append(sE)
    sb_dtm = [sE.enter_context(nc.sbuf_tensor(f"e_dt{i}", [128, 5, D], BF16))
              for i in range(2)]
    sb_dsp = [sE.enter_context(nc.sbuf_tensor(f"e_d{i}", [128, 8, CAP], BF16))
              for i in range(2)]
    sb_we1 = [sE.enter_context(nc.sbuf_tensor(f"e_w1{i}", [128, 4, 8, 128], BF16))
              for i in range(2)]
    sb_ew2 = sE.enter_context(nc.sbuf_tensor("e_w2", [128, 32, D], BF16))
    sb_eh = sE.enter_context(nc.sbuf_tensor("e_eh", [128, 32, CAP], BF16))
    sb_eb2c = sE.enter_context(nc.sbuf_tensor("e_b2", [128, D], F32))
    sb_eob = [sE.enter_context(nc.sbuf_tensor(f"e_o{i}", [128, D], BF16))
              for i in range(2)]

    dl_tot = {}
    dl_free = {-2: 0, -1: 0}

    def issue_disp_t(e_):
        p = e_ % 2
        w = [("Sca", sca_tot)]
        if e_ >= 2:
            w.append(("pe", dl_free[e_ - 2]))
        dl_tot[e_] = b.dma(
            "sy", SEM_DL[p], sb_dtm[p][:, :, :],
            disp_d[e_ * CAP:(e_ + 1) * CAP, :].rearrange("(sc p) c -> p sc c", p=128),
            wait=tuple(w))

    fm_free = {-2: 0, -1: 0}

    def disp_transpose(e_):
        p = e_ % 2
        dsem = SEM_DL[p]
        for sc in range(5):
            tp_ = (pT1, pT2)[sc % 2]
            for c in range(8):
                w = ((dsem, dl_tot[e_]), ("ve", ve_pt_free[sc % 2]),
                     ("pe", 0)) if c == 0 else ()
                pe(lambda e, o=tp_[:, c * 128:(c + 1) * 128],
                   s=sb_dtm[p][:, sc, c * 128:(c + 1) * 128]:
                   e.transpose(o, s, sb_idb[:, :]),
                   wait=w, inc=((("pe", 1),) if c == 7 else ()))
            pe_tr = b.now("pe")
            ve(lambda e, sc=sc, tp_=tp_, p=p: e.tensor_copy(
                sb_dsp[p][:, :, sc * 128:(sc + 1) * 128],
                tp_[:, :].rearrange("q (c t) -> q c t", c=8)),
               wait=(("pe", pe_tr), ("pe", fm_free[e_ - 2])))
            ve_pt_free[sc % 2] = b.now("ve")
        dl_free[e_] = b.now("pe")

    we1_tot = {}

    we1_fin = {}

    def issue_we1(g):
        p = g % 2
        w = (("pe", we1_fin[g - 2]),) if g >= 2 else (("pe", pe_router_done),)
        we1_tot[g] = b.dma("gp", ("We1a", "We1b")[p], sb_we1[p][:, :, :, :],
                           we1[g // 8, g % 8, :, :, :, :], wait=w)

    issue_disp_t(0)
    issue_we1(0)
    issue_we1(1)
    we2_tot = {}
    eo_pe_done = {-1: pe_router_done}
    eo_ve_last = {-1: 0}
    eh_free = {0: 0, 1: 0}
    eo_ve_free = 0
    eob_free = {0: 0, 1: 0}
    ring_drain = ring.drain_waits()
    eo_tot = 0
    for e_ in range(NE):
        p = e_ % 2
        dsem = SEM_DL[p]
        we2_tot[e_] = b.dma("gp", "We2", sb_ew2[:, :, :], we2[e_, :, :, :],
                            wait=(("pe", eo_pe_done[e_ - 1]),))
        eb2_tot = b.dma("sy", "Web2", sb_eb2c[:, :], eb2[e_, :, :],
                        wait=(("pe", eo_pe_done[e_ - 1]), ("ve", eo_ve_last[e_ - 1])))
        if e_ + 1 < NE:
            issue_disp_t(e_ + 1)
        disp_transpose(e_)
        ve_dsp = b.now("ve")
        for mb in range(8):
            g = e_ * 8 + mb
            wp = g % 2
            wsem = ("We1a", "We1b")[wp]
            for m4 in range(4):
                m = mb * 4 + m4
                pp = (pA, pB)[m % 2]
                first_w = ((wsem, we1_tot[g]), ("ve", ve_dsp),
                           ("se", eh_free[m % 2])) + \
                    (ring_drain if (e_ == 0 and mb == 0 and m4 == 0) else ())
                first = True
                for k in range(8):
                    for (lo, wdt) in ((0, 512), (512, 128)):
                        pe(lambda e, o=pp[:, lo:lo + wdt], wt=sb_we1[wp][:, m4, k, :],
                           rh=sb_dsp[p][:, k, lo:lo + wdt], st=(k == 0), sp=(k == 7):
                           e.matmul(o, wt, rh, start=st, stop=sp),
                           wait=first_w if first else (),
                           inc=((("pe", 1),) if (k == 7 and wdt == 128) else ()))
                        first = False
                mm_done = b.now("pe")
                se(lambda e, m=m, ee=e_, s=pp[:, 0:CAP]:
                   e.activation(sb_eh[:, m, :], s, AF.Relu,
                                bias=sb_eb1[:, ee, m:m + 1]),
                   wait=(("pe", mm_done),))
                eh_free[m % 2] = b.now("se")
            we1_fin[g] = b.now("pe")
            if g + 2 < NE * 8:
                issue_we1(g + 2)
        se_eh = b.now("se")
        fm_free[e_] = b.now("pe")
        for n_c in range(5):
            first = True
            for m in range(32):
                for (lo, wdt) in ((0, 512), (512, 512)):
                    w = (("We2", we2_tot[e_]), ("se", se_eh),
                         ("ve", eo_ve_free)) if first else ()
                    pe(lambda e, o=pC[:, lo:lo + wdt],
                       wt=sb_eh[:, m, n_c * 128:(n_c + 1) * 128],
                       rh=sb_ew2[:, m, lo:lo + wdt], st=(m == 0), sp=(m == 31):
                       e.matmul(o, wt, rh, start=st, stop=sp),
                       wait=w,
                       inc=((("pe", 1),) if (m == 31 and lo == 512) else ()))
                    first = False
            mm_done = b.now("pe")
            op = n_c % 2
            ve(lambda e, o=sb_eob[op][:, :]: e.tensor_add(o, pC[:, :], sb_eb2c[:, :]),
               wait=(("pe", mm_done), ("Web2", eb2_tot), (SEM_EO[op], eob_free[op])),
               inc=(("ve", 1),))
            eo_ve_free = b.now("ve")
            eo_tot = b.dma("sy", SEM_EO[op],
                           eo_d[e_ * CAP + n_c * 128:e_ * CAP + (n_c + 1) * 128, :],
                           sb_eob[op][:, :], wait=(("ve", eo_ve_free),))
            eob_free[op] = eo_tot
        eo_pe_done[e_] = b.now("pe")
        eo_ve_last[e_] = b.now("ve")
    eo_all = (("Eo0", b.now("Eo0")), ("Eo1", b.now("Eo1")))
    if dbg:
        dbg_tot = b.dma("gp", "Dbg", dbg_t["d_eo"][:, :], eo_d[:, :], wait=eo_all)
    _stacks.remove(sE)
    sE.close()
    if upto == 'E':
        return _finish_early()

    sF = ExitStack()
    _stacks.append(sF)
    sb_g0 = sF.enter_context(nc.sbuf_tensor("f_g0", [128, 16, D], BF16))
    sb_g1 = sF.enter_context(nc.sbuf_tensor("f_g1", [128, 16, D], BF16))
    sb_xr = [sF.enter_context(nc.sbuf_tensor(f"f_xr{i}", [128, D], F32))
             for i in range(2)]
    sb_y2 = [sF.enter_context(nc.sbuf_tensor(f"f_y2{i}", [128, D], F32))
             for i in range(2)]
    sb_sq2 = [sF.enter_context(nc.sbuf_tensor(f"f_sq{i}", [128, D], F32))
              for i in range(2)]
    sb_yo = [sF.enter_context(nc.sbuf_tensor(f"f_yo{i}", [128, D], F32))
             for i in range(2)]

    for i16t, gdst in ((sb_i16c, sb_g0), (sb_i16d, sb_g1)):
        for j in range(4):
            b.emit("gp", lambda e, t=i16t[:, 32 * j:32 * (j + 1)],
                   d=gdst[:, 4 * j:4 * (j + 1), :]:
                   e.dma_gather(d, eo_d[:, :], t, 512, 512, D),
                   wait=(("Idx", gidx_tot),) + eo_all +
                        ((("Dbg", dbg_tot),) if dbg else ()),
                   inc=(("Gat", 16),), async_op=True)
    gat_tot = b.now("Gat")
    if dbg:
        for tc in range(16):
            b.dma("gp", "Dbg", dbg_t["d_g01"][0, :, tc, :], sb_g0[:, tc, :],
                  wait=(("Gat", gat_tot),))
            dbg_tot = b.dma("gp", "Dbg", dbg_t["d_g01"][1, :, tc, :], sb_g1[:, tc, :])

    pe_E_done = b.now("pe")
    if upto == 'F1':
        return _finish_early()
    xr_tot = {}

    xr_fin = {}

    def issue_xr(i):
        p = i % 2
        w = (("Spl", spl_tot), ("pe", pe_E_done)) if i < 2 else             (("ve", xr_fin[i - 2]),)
        xr_tot[i] = b.dma("sy", SEM_RL[p], sb_xr[p][:, :],
                          xln_d[i * 128:(i + 1) * 128, :], wait=w)

    issue_xr(0)
    issue_xr(1)
    y2_free = {0: 0, 1: 0}
    se_sumsF = {}

    def f_phase1(i):
        p = i % 2
        y2 = sb_y2[p]
        se(lambda e, i=i, y=y2: e.activation(y[:, :], sb_g1[:, i, :], AF.Copy,
                                             scale=sb_w1[:, i:i + 1]),
           wait=(("Gat", gat_tot), ("ve", v_keep), ("ve", y2_free[p])),
           inc=(("se", 1),))
        s_cp = b.now("se")
        ve(lambda e, i=i, y=y2: e.scalar_tensor_tensor(y[:, :], sb_g0[:, i, :],
                                                       sb_w0[:, i:i + 1], y[:, :],
                                                       OP.mult, OP.add),
           wait=(("se", s_cp),))
        ve(lambda e, p=p, y=y2: e.tensor_add(y[:, :], y[:, :], sb_xr[p][:, :]),
           wait=((SEM_RL[p], xr_tot[i]),), inc=(("ve", 1),))
        xr_fin[i] = b.now("ve")
        se(lambda e, y=y2, sq=sb_sq2[p], c=LN8(p, 0):
           e.activation(sq[:, :], y[:, :], AF.Copy, accum_out=c),
           wait=(("ve", xr_fin[i]),))
        se(lambda e, y=y2, sq=sb_sq2[p], c=LN8(p, 1):
           e.activation(sq[:, :], y[:, :], AF.Square, accum_out=c),
           inc=(("se", 1),))
        se_sumsF[i] = b.now("se")
        if i + 2 < 16:
            issue_xr(i + 2)

    def f_phase2(i):
        p = i % 2
        y2 = sb_y2[p]
        c0, c1, c2, c3 = LN8(p, 0), LN8(p, 1), LN8(p, 2), LN8(p, 3)
        ve(lambda e, c=c0: e.tensor_scalar_mul(c, c, -1.0 / D),
           wait=(("se", se_sumsF[i]),))
        ve(lambda e, c=c1: e.tensor_scalar(c, c, 1.0 / D, EPS, OP.mult, OP.add))
        ve(lambda e, a=c2, c=c0: e.tensor_mul(a, c, c))
        ve(lambda e, a=c1, bq=c2: e.tensor_sub(a, a, bq))
        ve(lambda e, a=c2, c=c1: e.reciprocal(a, c), inc=(("ve", 1),))
        v_s = b.now("ve")
        se(lambda e, a=c3, c=c2: e.activation(a, c, AF.Sqrt),
           wait=(("ve", v_s),), inc=(("se", 1),))
        se_i = b.now("se")
        ve(lambda e, p=p, y=y2, c0=c0, c3=c3: e.tensor_scalar(
            sb_yo[p][:, :], y[:, :], c0, c3, OP.add, OP.mult),
           wait=(("se", se_i), (SEM_OUT[p], b.now(SEM_OUT[p]))), inc=(("ve", 1),))
        v_yo = b.now("ve")
        y2_free[p] = v_yo
        b.dma("sy", SEM_OUT[p], out_t[i * 128:(i + 1) * 128, :], sb_yo[p][:, :],
              wait=(("ve", v_yo),))

    f_phase1(0)
    for i in range(1, 16):
        f_phase1(i)
        f_phase2(i - 1)
    f_phase2(15)
    fw = [("Out0", b.now("Out0")), ("Out1", b.now("Out1"))]
    if dbg:
        fw.append(("Dbg", b.now("Dbg")))
    b.emit("sy", lambda e: e.nop(), wait=tuple(fw))
    _stacks.remove(sF)
    sF.close()

    b.build()
    es.close()
    nc.finalize()
    return nc


def stage_inputs(inputs):
    f32 = np.float32
    g = {k: np.asarray(v, f32) for k, v in inputs.items()}
    x = g["L0_attn_x"]
    cc = np.ascontiguousarray
    chunkcol = lambda v, n: cc(v.reshape(n, 128).T)
    com = {
        "wproj": cc(g["L0_attn_proj_w"].T.reshape(8, 128, 24, 128)
                    .transpose(2, 1, 0, 3)),
        "convw": cc(g["L0_attn_conv_w"][:, 0, :].reshape(12, 128, 4)
                    .transpose(1, 0, 2)),
        "convb": chunkcol(g["L0_attn_conv_b"], 12),
        "fl": chunkcol(g["L0_attn_forget_lambda"], 12),
        "wgates": cc(np.stack([g["L0_attn_gp_w"].T, g["L0_attn_ip_w"].T])
                     .reshape(2, 12, 128, 12, 128).transpose(3, 2, 0, 1, 4)),
        "gpb": chunkcol(g["L0_attn_gp_b"], 12),
        "ipb": chunkcol(g["L0_attn_ip_b"], 12),
        "wout": cc(g["L0_attn_out_w"].T.reshape(12, 128, 8, 128)
                   .transpose(2, 1, 0, 3)),
        "wr1": cc(g["L0_ffn_rW1"].reshape(8, 128, 32, 128).transpose(2, 1, 0, 3)),
        "rb1": chunkcol(g["L0_ffn_rb1"], 32),
        "wr2": cc(g["L0_ffn_rW2"].reshape(32, 128, 32, 128).transpose(2, 1, 0, 3)),
        "rb2": chunkcol(g["L0_ffn_rb2"], 32),
        "wr3": cc(g["L0_ffn_rW3"].reshape(32, 128, 8).transpose(1, 0, 2)),
        "rb3": cc(np.broadcast_to(g["L0_ffn_rb3"], (128, 8))),
        "we1": cc(g["L0_ffn_eW1"].reshape(NE, 8, 128, 8, 4, 128)
                  .transpose(0, 3, 2, 4, 1, 5)),
        "eb1": cc(g["L0_ffn_eb1"].reshape(NE, 32, 128).transpose(0, 2, 1)),
        "we2": cc(g["L0_ffn_eW2"].reshape(NE, 32, 128, D).transpose(0, 2, 1, 3)),
        "eb2": cc(np.broadcast_to(g["L0_ffn_eb2"][:, None, :], (NE, 128, D))),
    }
    in_maps = []
    for c in range(8):
        bb, h = c // 2, c % 2
        t0 = h * 2048
        xf = np.zeros((D, 2051), f32)
        if h == 0:
            xf[:, 3:] = x[bb, 0:2048].T
        else:
            xf[:, :] = x[bb, 2045:4096].T
        m = dict(com)
        m["x_fm"] = cc(xf)
        m["x_tm"] = cc(x[bb, t0:t0 + 2048])
        m["bflag"] = np.full((128, 1), float(h), f32)
        in_maps.append(m)
    return in_maps


def kernel(**inputs):
    in_maps = stage_inputs(inputs)
    nc = build_program(dbg=False)
    res = run_bass_kernel_spmd(nc, in_maps, core_ids=list(range(8)))
    out = np.empty((4, 4096, D), np.float32)
    for c in range(8):
        bb, h = c // 2, c % 2
        out[bb, h * 2048:(h + 1) * 2048, :] = res.results[c]["out"].reshape(T, D)
    return out


# version 05 (final)
# speedup vs baseline: 1.0453x; 1.0031x over previous
import numpy as np
from contextlib import ExitStack

import concourse.bacc as bacc
import concourse.mybir as mybir
from concourse.bass_utils import run_bass_kernel_spmd
from concourse.library_config import mlp as _LIB_MLP

F32 = mybir.dt.float32
BF16 = mybir.dt.bfloat16
I16 = mybir.dt.int16
AF = mybir.ActivationFunctionType
OP = mybir.AluOpType
AXX = mybir.AxisListType.X

D = 1024
T = 2048
NE = 8
CAP = 640
TRASH = NE * CAP
EPS = 1e-5
BIG = 1.0e9
ENGS = ("pe", "ve", "se", "gp", "sy")
RG_PAIRS = [[0, 1], [2, 3], [4, 5], [6, 7]]


class B:
    def __init__(self, nc, es):
        self.nc = nc
        self.es = es
        self.q = {e: [] for e in ENGS}
        self.sems = {}
        self.cnt = {}
        self._lw = {e: {} for e in ENGS}

    def sem(self, name):
        self.sems[name] = self.es.enter_context(self.nc.semaphore(name))
        self.cnt[name] = 0

    AUTO_SER = ("ve", "se", "gp")

    def emit(self, eng, fn, wait=(), inc=(), async_op=False):
        w = []
        lw = self._lw[eng]
        if eng in self.AUTO_SER and self.cnt[eng] > 0:
            wait = ((eng, self.cnt[eng]),) + tuple(wait)
        for s, v in wait:
            if v > 0 and lw.get(s, -1) < v:
                lw[s] = v
                w.append((s, v))
        if eng in self.AUTO_SER and not async_op:
            inc = tuple((s, n) for s, n in inc if s != eng) + ((eng, 1),)
        for s, n in inc:
            self.cnt[s] += n
        self.q[eng].append((tuple(w), fn, tuple(inc)))

    def now(self, s):
        return self.cnt[s]

    def dma(self, eng, sem, out, in_, wait=(), slow=False):
        self.emit(eng, lambda e, o=out, i=in_, sl=slow:
                  e.dma_start(out=o, in_=i, allow_slow_non_contiguous=sl),
                  wait=wait, inc=((sem, 16),), async_op=True)
        return self.cnt[sem]

    def build(self):
        block = self.es.enter_context(self.nc.Block())

        def mk(name):
            def body(e):
                for waits, fn, incs in self.q[name]:
                    for s, v in waits:
                        e.wait_ge(self.sems[s], v)
                    inst = fn(e)
                    for s, n in incs:
                        if inst is not None:
                            inst.then_inc(self.sems[s], n)
            return body

        block.tensor(mk("pe"))
        block.vector(mk("ve"))
        block.scalar(mk("se"))
        block.gpsimd(mk("gp"))
        block.sync(mk("sy"))


class Ring:

    def __init__(self, b, slots):
        self.b = b
        self.slots = slots
        self.free_at = [None] * len(slots)
        self.i = 0

    def acquire(self):
        i = self.i
        self.i = (self.i + 1) % len(self.slots)
        w = (self.free_at[i],) if self.free_at[i] else ()
        return i, self.slots[i], w

    def release(self, i, eng):
        self.free_at[i] = (eng, self.b.now(eng))

    def drain_waits(self):
        return tuple(fa for fa in self.free_at if fa)


SKIP = set()


def build_program(dbg=False, upto='F'):
    nc = bacc.Bacc("TRN2", target_bir_lowering=False, debug=False)
    es = ExitStack()
    b = B(nc, es)
    dbg_t = {}

    def din(name, shape):
        return nc.dram_tensor(name, list(shape), F32, kind="ExternalInput").ap()

    x_fm = din("x_fm", [D, 2051])
    x_tm = din("x_tm", [T, D])
    wproj = din("wproj", [24, 128, 8, 128])
    convw = din("convw", [128, 12, 4])
    convb = din("convb", [128, 12])
    fl_in = din("fl", [128, 12])
    wgates = din("wgates", [12, 128, 2, 12, 128])
    gpb = din("gpb", [128, 12])
    ipb = din("ipb", [128, 12])
    wout = din("wout", [8, 128, 12, 128])
    wr1 = din("wr1", [32, 128, 8, 128])
    rb1 = din("rb1", [128, 32])
    wr2 = din("wr2", [32, 128, 32, 128])
    rb2 = din("rb2", [128, 32])
    wr3 = din("wr3", [128, 32, 8])
    rb3 = din("rb3", [128, 8])
    we1 = din("we1", [NE, 8, 128, 4, 8, 128])
    eb1 = din("eb1", [NE, 128, 32])
    we2 = din("we2", [NE, 128, 32, D])
    eb2 = din("eb2", [NE, 128, D])
    bflag = din("bflag", [128, 1])
    out_t = nc.dram_tensor("out", [T, D], F32, kind="ExternalOutput").ap()

    disp_d = nc.dram_tensor("disp_d", [TRASH + 2, D], BF16).ap()
    eo_d = nc.dram_tensor("eo_d", [TRASH + 1, D], BF16).ap()
    xln_d = nc.dram_tensor("xln_d", [T, D], F32).ap()
    idx_d = nc.dram_tensor("idx_d", [6, 128, 128], I16).ap()
    cs_car = nc.dram_tensor("cs_car", [128, 12], F32)
    cg_car = nc.dram_tensor("cg_car", [2, 128, 12], F32)
    cs_cnt = nc.dram_tensor("cs_cnt", [8], F32)
    cg_cnt = nc.dram_tensor("cg_cnt", [2, 8], F32)
    if dbg:
        for nm, shp in [("d_gate", [128, 12, T]), ("d_xc", [128, 12, T]),
                        ("d_z", [128, 12, T]), ("d_attn", [128, 8, T]),
                        ("d_xln", [T, D]), ("d_sel", [128, 16, 8]),
                        ("d_pem", [8, T]), ("d_disp", [TRASH + 2, D]),
                        ("d_eo", [TRASH + 1, D]), ("d_g01", [2, 128, 16, D]),
                        ("dbh1", [128, 32, 512]), ("dbh2", [128, 32, 512]),
                        ("dbpp", [128, 16, 8])]:
            dbg_t[nm] = nc.dram_tensor(nm, shp, F32, kind="ExternalOutput").ap()

    for s in ["Wmisc", "Wpj0", "Wpj1", "Wpjf", "Wg0", "Wg1", "Wgf", "Wo0", "Wo1", "Wof",
              "Wr1a", "Wr1b", "Wr1c", "Wr1f", "Wr2c", "Wr2a", "Wr2b", "Wr2f", "We1a", "We1b", "We1f", "We2",
              "Web2", "Xf", "Xt0", "Xt1", "Xtf", "Spl", "Rl0", "Rl1", "Rlf",
              "Zer", "Dl0", "Dl1", "Eo0", "Eo1", "Out0", "Out1", "Idx", "Sca",
              "Gat", "CC", "CCc", "Dbg", "pe", "ve", "se", "gp", "sy"]:
        b.sem(s)

    SEM_OUT = ("Out0", "Out1")
    SEM_EO = ("Eo0", "Eo1")
    SEM_XT = ("Xt0", "Xt1")
    SEM_RL = ("Rl0", "Rl1")
    SEM_DL = ("Dl0", "Dl1")

    PA = lambda name, shape, dt=F32: es.enter_context(nc.sbuf_tensor(name, shape, dt))
    sb_convw = PA("c_convw", [128, 12, 4])
    sb_convb = PA("c_convb", [128, 12])
    sb_fl = PA("c_fl", [128, 12])
    sb_sfl = PA("c_sfl", [128, 12])
    sb_gpb = PA("c_gpb", [128, 12])
    sb_ipb = PA("c_ipb", [128, 12])
    sb_rb1 = PA("c_rb1", [128, 32])
    sb_rb2 = PA("c_rb2", [128, 32])
    sb_rb3 = PA("c_rb3", [128, 8])
    sb_eb1 = PA("c_eb1", [128, NE, 32])
    sb_bf = PA("c_bf", [128, 1])
    sb_io8 = PA("c_io8", [128, 8])
    sb_idb = PA("c_idb", [128, 128], BF16)
    sb_idf = PA("c_idf", [128, 128])
    sb_ones = PA("c_ones", [128, 128])
    sb_carry = PA("c_carry", [128, 12])
    sb_carin = PA("c_carin", [128, 12])
    sb_cnt = PA("c_cnt", [8, 1])
    sb_crow = PA("c_crow", [1, 8])
    sb_cntbc = PA("c_cntbc", [128, 8])
    sb_s0 = PA("c_s0", [128, 16])
    sb_s1 = PA("c_s1", [128, 16])
    sb_e0 = PA("c_e0", [128, 16])
    sb_e1 = PA("c_e1", [128, 16])
    sb_ps0 = PA("c_ps0", [128, 16])
    sb_ps1 = PA("c_ps1", [128, 16])
    sb_sl0 = PA("c_sl0", [128, 16])
    sb_sl1 = PA("c_sl1", [128, 16])
    sb_oh0 = PA("c_oh0", [128, 16, 8])
    sb_oh1 = PA("c_oh1", [128, 16, 8])
    sb_w0 = PA("c_w0", [128, 16])
    sb_w1 = PA("c_w1", [128, 16])
    sb_m16a = PA("c_m16a", [128, 16])
    sb_m16b = PA("c_m16b", [128, 16])
    sb_ln8 = PA("c_ln8", [128, 8])
    sb_pin = PA("c_pin", [8, 1])
    sb_trash = PA("c_trash", [128, 1])
    sb_beps = PA("c_beps", [128, 1])
    sb_zero = PA("c_zero", [128, D], BF16)
    sb_t16 = PA("c_t16", [16, 128], I16)
    sb_i16a = PA("c_i16a", [128, 128], I16)
    sb_i16c = PA("c_i16c", [128, 128], I16)
    sb_i16d = PA("c_i16d", [128, 128], I16)
    sb_i16b = PA("c_i16b", [128, 128], I16)
    sb_wr3 = PA("c_wr3", [128, 32, 8], BF16)
    COL = [PA(f"c_col{i}", [128, 1]) for i in range(6)]
    ROW8 = [PA(f"c_row8{i}", [128, 8]) for i in range(4)]

    pA = es.enter_context(nc.psum_tensor("pA", [128, 1024], F32))
    pB = es.enter_context(nc.psum_tensor("pB", [128, 1024], F32))
    pC = es.enter_context(nc.psum_tensor("pC", [128, 1024], F32))
    pT1 = es.enter_context(nc.psum_tensor("pT1", [128, 1024], BF16))
    pT2 = es.enter_context(nc.psum_tensor("pT2", [128, 1024], BF16))
    ring = Ring(b, [pA[:, 0:512], pA[:, 512:1024], pB[:, 0:512],
                    pB[:, 512:1024], pC[:, 0:512], pC[:, 512:1024]])


    _stacks = []

    def _finish_early():
        for s_ in reversed(_stacks):
            s_.close()
        _stacks.clear()
        waits = []
        for snm in ["Wmisc", "Wpj0", "Wpj1", "Wg0", "Wg1", "Wo0", "Wo1", "Wr1c",
                    "Wr2c", "Wr2a", "Wr2b", "Wr1a", "Wr1b", "We1a", "We1b", "We2",
                    "Web2", "Xf", "Xt0", "Xt1", "Spl", "Rl0", "Rl1", "Zer",
                    "Dl0", "Dl1", "Eo0", "Eo1", "Out0", "Out1", "Idx", "Sca",
                    "Gat", "CC", "Dbg"]:
            v = b.now(snm)
            if v > 0:
                waits.append((snm, v))
        b.emit("sy", lambda e: e.nop(), wait=tuple(waits))
        b.build()
        es.close()
        nc.finalize()
        return nc
    def ve(fn, wait=(), inc=()):
        b.emit("ve", fn, wait=wait, inc=inc)

    def se(fn, wait=(), inc=()):
        b.emit("se", fn, wait=wait, inc=inc)

    def pe(fn, wait=(), inc=()):
        b.emit("pe", fn, wait=wait, inc=inc)

    def gp(fn, wait=(), inc=()):
        b.emit("gp", fn, wait=wait, inc=inc)

    if "iota" not in SKIP:
        gp(lambda e: e.iota(sb_io8[:, :], [[1, 8]], base=0, channel_multiplier=0,
                            allow_small_or_imprecise_dtypes=True), inc=(("gp", 1),))
    else:
        gp(lambda e: e.memset(sb_io8[:, :], 0.0), inc=(("gp", 1),))
    gp(lambda e: e.memset(sb_ones[:, :], 1.0), inc=(("gp", 1),))
    gp(lambda e: e.memset(sb_trash[:, :], float(TRASH)), inc=(("gp", 1),))
    gp(lambda e: e.memset(sb_beps[:, :], 1.0 + 1e-6), inc=(("gp", 1),))
    gp(lambda e: e.memset(sb_zero[:, :], 0.0), inc=(("gp", 1),))
    gp(lambda e: e.affine_select(sb_idf[:, :], sb_ones[:, :], [[1, 128]],
                                 OP.is_equal, 0.0, base=0, channel_multiplier=-1),
       wait=(("gp", 5),), inc=(("gp", 1),))
    gp(lambda e: e.tensor_copy(sb_idb[:, :], sb_idf[:, :]),
       wait=(("gp", 6),), inc=(("gp", 1),))
    g_pro = b.now("gp")
    if "lib" not in SKIP:
        b.emit("gp", lambda e: e.load_library(_LIB_MLP), async_op=True)
    if "misc" not in SKIP:
        _miscl = [(sb_convw[:, :, :], convw[:, :, :]), (sb_convb[:, :], convb[:, :]),
                  (sb_fl[:, :], fl_in[:, :]), (sb_gpb[:, :], gpb[:, :]),
                  (sb_ipb[:, :], ipb[:, :]), (sb_rb1[:, :], rb1[:, :]),
                  (sb_rb2[:, :], rb2[:, :]), (sb_rb3[:, :], rb3[:, :]),
                  (sb_bf[:, :], bflag[:, :]), (sb_wr3[:, :, :], wr3[:, :, :])]
        for _mi, (dst, src) in enumerate(_miscl):
            if f"m{_mi}" in SKIP:
                continue
            b.dma("gp", "Wmisc", dst, src)
    if "eb1" not in SKIP:
        for e_ in range(NE):
            b.dma("gp", "Wmisc", sb_eb1[:, e_, :], eb1[e_, :, :])
    wmisc = b.now("Wmisc")
    if "sfl" not in SKIP:
        se(lambda e: e.activation(sb_sfl[:, :], sb_fl[:, :], AF.Exp),
           wait=(("Wmisc", wmisc),))
        se(lambda e: e.activation(sb_sfl[:, :], sb_sfl[:, :], AF.Ln, bias=1.0))
        se(lambda e: e.mul(sb_sfl[:, :], sb_sfl[:, :], -8.0), inc=(("se", 1),))
    se_sfl = b.now("se")
    if upto == 'P0':
        return _finish_early()
    for base in (disp_d, eo_d):
        nr = TRASH + 2 if base is disp_d else TRASH + 1
        for r0 in range(0, nr, 128):
            n = min(128, nr - r0)
            b.dma("sy", "Zer", base[r0:r0 + n, :], sb_zero[0:n, :],
                  wait=(("gp", g_pro),))
    zer_tot = b.now("Zer")
    if upto == 'P1':
        return _finish_early()

    if upto == 'P':
        return _finish_early()

    sGATE = ExitStack()
    _stacks.append(sGATE)
    sXC = ExitStack()
    _stacks.append(sXC)
    sb_gate = sGATE.enter_context(nc.sbuf_tensor("ab_gate", [128, 12, T], BF16))
    sb_xc = sXC.enter_context(nc.sbuf_tensor("ab_xc", [128, 12, T], BF16))
    sA = ExitStack()
    _stacks.append(sA)
    sb_xbf = sA.enter_context(nc.sbuf_tensor("a_xbf", [128, 8, 2051], BF16))
    sb_wpj = [sA.enter_context(nc.sbuf_tensor(f"a_wpj{i}", [128, 8, 128], BF16))
              for i in range(2)]
    sb_xh = sA.enter_context(nc.sbuf_tensor("a_xh", [128, 2051], F32))
    sb_cvt = sA.enter_context(nc.sbuf_tensor("a_cvt", [128, T], F32))

    for k in range(8):
        b.dma("gp", "Xf", sb_xbf[:, k, :], x_fm[k * 128:(k + 1) * 128, :])
    xf_tot = b.now("Xf")

    pj_tot = {}
    pj_fin = {}

    def issue_proj(m):
        p = m % 2
        w = (("pe", pj_fin[m - 2]),) if m >= 2 else ()
        pj_tot[m] = b.dma("gp", ("Wpj0", "Wpj1")[p], sb_wpj[p][:, :, :],
                          wproj[m, :, :, :], wait=w)

    issue_proj(0)
    issue_proj(1)
    for m in range(24):
        p = m % 2
        sem = ("Wpj0", "Wpj1")[p]
        is_gate = m < 12
        cts = [(3 + 512 * j, 512) for j in range(4)] if is_gate else \
              [(512 * j, 512) for j in range(4)] + [(2048, 3)]
        for ci, (c0, cw) in enumerate(cts):
            si, ps, pw = ring.acquire()
            for k in range(8):
                w = pw + ((sem, pj_tot[m]), ("Xf", xf_tot)) if k == 0 else ()
                pe(lambda e, o=ps[:, 0:cw], wt=sb_wpj[p][:, k, :],
                   rh=sb_xbf[:, k, c0:c0 + cw], st=(k == 0), sp=(k == 7):
                   e.matmul(o, wt, rh, start=st, stop=sp),
                   wait=w, inc=((("pe", 1),) if k == 7 else ()))
            mm_done = b.now("pe")
            if is_gate:
                se(lambda e, d=sb_gate[:, m, c0 - 3:c0 - 3 + cw], s=ps[:, 0:cw]:
                   e.activation(d, s, AF.Gelu),
                   wait=(("pe", mm_done),))
                ring.release(si, "se")
            else:
                ve(lambda e, d=sb_xh[:, c0:c0 + cw], s=ps[:, 0:cw]: e.tensor_copy(d, s),
                   wait=(("pe", mm_done),))
                ring.release(si, "ve")
        pj_fin[m] = b.now("pe")
        if m + 2 < 24:
            issue_proj(m + 2)
        if not is_gate:
            f = m - 12
            ve(lambda e, f=f: e.tensor_scalar(
                sb_cvt[:, :], sb_xh[:, 0:T], sb_convw[:, f, 0:1],
                sb_convb[:, f:f + 1], OP.mult, OP.add),
               wait=(("Wmisc", wmisc),))
            for kk in range(1, 4):
                ve(lambda e, f=f, kk=kk: e.scalar_tensor_tensor(
                    sb_cvt[:, :], sb_xh[:, kk:kk + T], sb_convw[:, f, kk:kk + 1],
                    sb_cvt[:, :], OP.mult, OP.add))
            ve(lambda e, f=f: e.tensor_copy(sb_xc[:, f, :], sb_cvt[:, :]))
    ve_xc_done = b.now("ve")
    se_gate_done = b.now("se")
    dbg_tot = 0
    if dbg:
        for f in range(12):
            b.dma("gp", "Dbg", dbg_t["d_gate"][:, f, :], sb_gate[:, f, :],
                  wait=(("se", se_gate_done), ("ve", ve_xc_done)))
            dbg_tot = b.dma("gp", "Dbg", dbg_t["d_xc"][:, f, :], sb_xc[:, f, :])
    _stacks.remove(sA)
    sA.close()
    if upto == 'A':
        return _finish_early()

    sB = ExitStack()
    _stacks.append(sB)
    sb_wg = [sB.enter_context(nc.sbuf_tensor(f"b_wg{i}", [128, 2, 12, 128], BF16))
             for i in range(2)]
    sb_al = sB.enter_context(nc.sbuf_tensor("b_al", [128, T], F32))
    sb_t1 = sB.enter_context(nc.sbuf_tensor("b_t1", [128, T], F32))
    sb_t2 = sB.enter_context(nc.sbuf_tensor("b_t2", [128, T], F32))
    sGAQ = ExitStack()
    _stacks.append(sGAQ)
    sb_gaq = sGAQ.enter_context(nc.sbuf_tensor("bc_gaq", [128, 12, T], BF16,
                                               side="right"))

    wg_tot = {}
    wg_fin = {}

    def issue_gw(f):
        p = f % 2
        w = (("pe", wg_fin[f - 2]),) if f >= 2 else             (("pe", pe_A_done), ("ve", ve_A_done))
        wg_tot[f] = b.dma("gp", ("Wg0", "Wg1")[p], sb_wg[p][:, :, :, :],
                          wgates[f, :, :, :, :], wait=w)

    pe_A_done = b.now("pe")
    ve_A_done = b.now("ve")

    issue_gw(0)
    issue_gw(1)
    ve_chain = 0
    for f in range(12):
        p = f % 2
        sem = ("Wg0", "Wg1")[p]
        for gi in range(2):
            for ct in range(4):
                si, ps, pw = ring.acquire()
                for k in range(12):
                    w = pw + ((sem, wg_tot[f]), ("ve", ve_xc_done),
                              ("Dbg", dbg_tot)) if k == 0 else ()
                    pe(lambda e, o=ps[:, 0:512], wt=sb_wg[p][:, gi, k, :],
                       rh=sb_xc[:, k, ct * 512:(ct + 1) * 512],
                       st=(k == 0), sp=(k == 11): e.matmul(o, wt, rh, start=st, stop=sp),
                       wait=w, inc=((("pe", 1),) if k == 11 else ()))
                mm_done = b.now("pe")
                sl = slice(ct * 512, (ct + 1) * 512)
                if gi == 0:
                    se(lambda e, d=sb_t2[:, sl], s=ps[:, 0:512], bi=sb_gpb[:, f:f + 1]:
                       e.activation(d, s, AF.Sigmoid, bias=bi),
                       wait=(("pe", mm_done), ("se", se_sfl), ("ve", ve_chain)),
                       inc=(("se", 1),))
                    se(lambda e, d=sb_al[:, sl], s=sb_t2[:, sl], sc=sb_sfl[:, f:f + 1]:
                       e.activation(d, s, AF.Exp, scale=sc), inc=(("se", 1),))
                else:
                    se(lambda e, d=sb_t1[:, sl], s=ps[:, 0:512], bi=sb_ipb[:, f:f + 1]:
                       e.activation(d, s, AF.Sigmoid, bias=bi),
                       wait=(("pe", mm_done), ("ve", ve_chain)), inc=(("se", 1),))
                ring.release(si, "se")
        wg_fin[f] = b.now("pe")
        se_g = b.now("se")
        ve(lambda e: e.tensor_mul(sb_t2[:, :], sb_al[:, :], sb_al[:, :]),
           wait=(("se", se_g),), inc=(("ve", 1),))
        v_sq = b.now("ve")
        se(lambda e: e.activation(sb_t2[:, :], sb_t2[:, :], AF.Sqrt,
                                  bias=sb_beps[:, :], scale=-1.0),
           wait=(("ve", v_sq), ("gp", g_pro)), inc=(("se", 1),))
        se_b = b.now("se")
        ve(lambda e: e.tensor_mul(sb_t1[:, :], sb_t1[:, :], sb_t2[:, :]),
           wait=(("se", se_b),))
        ve(lambda e, f=f: e.tensor_mul(sb_t1[:, :], sb_t1[:, :], sb_xc[:, f, :]))
        ve(lambda e: e.tensor_tensor_scan(sb_t2[:, :], sb_al[:, :], sb_t1[:, :],
                                          0.0, OP.mult, OP.add))
        ve(lambda e: e.tensor_tensor_scan(sb_t1[:, :], sb_al[:, :], sb_al[:, :],
                                          1.0, OP.mult, OP.bypass))
        ve(lambda e, f=f: e.tensor_copy(sb_carry[:, f:f + 1], sb_t2[:, T - 1:T]))
        ve(lambda e, f=f: e.tensor_mul(sb_gaq[:, f, :], sb_gate[:, f, :], sb_t1[:, :]),
           wait=(("se", se_gate_done),))
        ve(lambda e, f=f: e.tensor_mul(sb_gate[:, f, :], sb_gate[:, f, :], sb_t2[:, :]))
        ve_chain = b.now("ve")
        if f + 2 < 12:
            issue_gw(f + 2)
    ve_scan_done = b.now("ve")

    cc0 = b.dma("gp", "CC", cs_car.ap()[:, :], sb_carry[:, :],
                wait=(("ve", ve_scan_done),))
    b.emit("gp", lambda e: e.collective_compute(
        "AllGather", OP.bypass, replica_groups=RG_PAIRS,
        ins=[cs_car.ap().opt()], outs=[cg_car.ap().opt()]),
        wait=(("CC", cc0),), inc=(("CCc", 1),), async_op=True)
    cc1 = b.now("CCc")
    cc2 = b.dma("gp", "CC", sb_carin[:, :], cg_car.ap()[0, :, :],
                wait=(("CCc", cc1),))
    ve(lambda e: e.tensor_scalar_mul(sb_carin[:, :], sb_carin[:, :], sb_bf[:, 0:1]),
       wait=(("CC", cc2),), inc=(("ve", 1),))
    ve_car = b.now("ve")
    _stacks.remove(sB)
    sB.close()
    _stacks.remove(sXC)
    sXC.close()
    if upto == 'B':
        return _finish_early()

    for f in range(12):
        ve(lambda e, f=f: e.scalar_tensor_tensor(
            sb_gate[:, f, :], sb_gaq[:, f, :], sb_carin[:, f:f + 1],
            sb_gate[:, f, :], OP.mult, OP.add),
           wait=(("ve", ve_car),), inc=((("ve", 1),) if f == 11 else ()))
    ve_z = b.now("ve")
    if dbg:
        for f in range(12):
            dbg_tot = b.dma("gp", "Dbg", dbg_t["d_z"][:, f, :], sb_gate[:, f, :],
                            wait=(("ve", ve_z),))

    _stacks.remove(sGAQ)
    sGAQ.close()
    sATT = ExitStack()
    _stacks.append(sATT)
    sb_attn = sATT.enter_context(nc.sbuf_tensor("c_attn", [128, 8, T], BF16,
                                                side="right"))
    sWO = ExitStack()
    _stacks.append(sWO)
    sb_wo = [sWO.enter_context(nc.sbuf_tensor(f"cw{i}", [128, 12, 128], BF16))
             for i in range(2)]
    wo_tot = {}

    wo_fin = {}

    def issue_wo(m):
        p = m % 2
        w = (("pe", wo_fin[m - 2]),) if m >= 2 else             (("pe", wg_fin[11]), ("ve", ve_scan_done))
        wo_tot[m] = b.dma("gp", ("Wo0", "Wo1")[p], sb_wo[p][:, :, :],
                          wout[m, :, :, :], wait=w)

    issue_wo(0)
    issue_wo(1)
    for m in range(8):
        p = m % 2
        sem = ("Wo0", "Wo1")[p]
        for ct in range(4):
            si, ps, pw = ring.acquire()
            for k in range(12):
                w = pw + ((sem, wo_tot[m]), ("ve", ve_z)) if k == 0 else ()
                pe(lambda e, o=ps[:, 0:512], wt=sb_wo[p][:, k, :],
                   rh=sb_gate[:, k, ct * 512:(ct + 1) * 512],
                   st=(k == 0), sp=(k == 11): e.matmul(o, wt, rh, start=st, stop=sp),
                   wait=w, inc=((("pe", 1),) if k == 11 else ()))
            mm_done = b.now("pe")
            se(lambda e, d=sb_attn[:, m, ct * 512:(ct + 1) * 512], s=ps[:, 0:512]:
               e.copy(d, s), wait=(("pe", mm_done),))
            ring.release(si, "se")
        wo_fin[m] = b.now("pe")
        if m + 2 < 8:
            issue_wo(m + 2)
    se_attn = b.now("se")
    _stacks.remove(sWO)
    sWO.close()
    _stacks.remove(sGATE)
    sGATE.close()
    sCD = ExitStack()
    _stacks.append(sCD)
    sb_xtmb = sCD.enter_context(nc.sbuf_tensor("cd_xtmb", [128, 16, D], BF16))
    sC = ExitStack()
    _stacks.append(sC)
    sb_xtc = [sC.enter_context(nc.sbuf_tensor(f"c_xt{i}", [128, D], F32))
              for i in range(2)]
    sb_ytc = [sC.enter_context(nc.sbuf_tensor(f"c_ytc{i}", [128, D], F32))
              for i in range(2)]
    sb_sq = [sC.enter_context(nc.sbuf_tensor(f"c_sq{i}", [128, D], F32))
             for i in range(2)]
    sb_xlc = [sC.enter_context(nc.sbuf_tensor(f"c_xl{i}", [128, D], F32))
              for i in range(2)]
    if dbg:
        for m in range(8):
            dbg_tot = b.dma("gp", "Dbg", dbg_t["d_attn"][:, m, :], sb_attn[:, m, :],
                            wait=(("se", se_attn),))

    pe_attn_done = b.now("pe")
    xt_tot = {}
    xt_fin = {}

    def issue_xt(i):
        p = i % 2
        w = (("ve", xt_fin[i - 2]),) if i >= 2 else \
            (("pe", pe_attn_done), ("Dbg", dbg_tot))
        xt_tot[i] = b.dma("sy", SEM_XT[p], sb_xtc[p][:, :],
                          x_tm[i * 128:(i + 1) * 128, :], wait=w)

    issue_xt(0)
    issue_xt(1)
    ve_pt_free = {0: 0, 1: 0}
    spl_prev = {0: 0, 1: 0}
    se_sumsC = {}

    def LN8(p, k):
        return sb_ln8[:, p * 4 + k:p * 4 + k + 1]

    def c_phase1(i):
        p = i % 2
        pt = (pT1, pT2)[p]
        for c in range(8):
            w = (("se", se_attn), ("ve", ve_pt_free[p]), ("gp", g_pro)) if c == 0 else ()
            pe(lambda e, o=pt[:, c * 128:(c + 1) * 128],
               s=sb_attn[:, c, i * 128:(i + 1) * 128]: e.transpose(o, s, sb_idb[:, :]),
               wait=w, inc=((("pe", 1),) if c == 7 else ()))
        pe_tr = b.now("pe")
        yt = sb_ytc[p]
        ve(lambda e, p=p, pt=pt, yt=yt: e.tensor_add(yt[:, :], sb_xtc[p][:, :],
                                                     pt[:, :]),
           wait=(("pe", pe_tr), (SEM_XT[p], xt_tot[i]),
                 ("se", se_sumsC.get(i - 2, 0))), inc=(("ve", 1),))
        ve_pt_free[p] = b.now("ve")
        xt_fin[i] = b.now("ve")
        se(lambda e, yt=yt, sq=sb_sq[p], c=LN8(p, 0):
           e.activation(sq[:, :], yt[:, :], AF.Copy, accum_out=c),
           wait=(("ve", xt_fin[i]),))
        se(lambda e, yt=yt, sq=sb_sq[p], c=LN8(p, 1):
           e.activation(sq[:, :], yt[:, :], AF.Square, accum_out=c),
           inc=(("se", 1),))
        se_sumsC[i] = b.now("se")
        if i + 2 < 16:
            issue_xt(i + 2)

    def c_phase2(i):
        p = i % 2
        yt = sb_ytc[p]
        c0, c1, c2, c3 = LN8(p, 0), LN8(p, 1), LN8(p, 2), LN8(p, 3)
        ve(lambda e, c=c0: e.tensor_scalar_mul(c, c, -1.0 / D),
           wait=(("se", se_sumsC[i]),))
        ve(lambda e, c=c1: e.tensor_scalar(c, c, 1.0 / D, EPS, OP.mult, OP.add))
        ve(lambda e, a=c2, c=c0: e.tensor_mul(a, c, c))
        ve(lambda e, a=c1, bq=c2: e.tensor_sub(a, a, bq))
        ve(lambda e, a=c2, c=c1: e.reciprocal(a, c), inc=(("ve", 1),))
        v_s = b.now("ve")
        se(lambda e, a=c3, c=c2: e.activation(a, c, AF.Sqrt),
           wait=(("ve", v_s),), inc=(("se", 1),))
        se_i = b.now("se")
        ve(lambda e, p=p, yt=yt, c0=c0, c3=c3: e.tensor_scalar(
            sb_xlc[p][:, :], yt[:, :], c0, c3, OP.add, OP.mult),
           wait=(("se", se_i), ("Spl", spl_prev[p])), inc=(("ve", 1),))
        v_xl = b.now("ve")
        spl_prev[p] = b.dma("sy", "Spl", xln_d[i * 128:(i + 1) * 128, :],
                            sb_xlc[p][:, :], wait=(("ve", v_xl),))
        ve(lambda e, i=i, p=p: e.tensor_copy(sb_xtmb[:, i, :], sb_xlc[p][:, :]),
           inc=(("ve", 1),))

    c_phase1(0)
    for i in range(1, 16):
        c_phase1(i)
        c_phase2(i - 1)
    c_phase2(15)
    spl_tot = b.now("Spl")
    ve_xln = b.now("ve")
    if dbg:
        dbg_tot = b.dma("gp", "Dbg", dbg_t["d_xln"][:, :], xln_d[:, :],
                        wait=(("Spl", spl_tot),))
    _stacks.remove(sC)
    sC.close()
    _stacks.remove(sATT)
    sATT.close()
    if upto == 'C':
        return _finish_early()

    sD = ExitStack()
    _stacks.append(sD)
    sb_o8 = sD.enter_context(nc.sbuf_tensor("s_o8", [8, T], F32))
    sb_mem = sD.enter_context(nc.sbuf_tensor("s_mem", [8, T], F32))
    sb_pincl = sD.enter_context(nc.sbuf_tensor("s_pincl", [8, T], F32))
    sb_pem = sD.enter_context(nc.sbuf_tensor("s_pem", [8, T], F32))
    sb_wr1s = [sD.enter_context(nc.sbuf_tensor(f"d_wr1{i}", [128, 8, 128], BF16))
               for i in range(2)]
    sb_xfm = sD.enter_context(nc.sbuf_tensor("d_xfm", [128, 8, 512], BF16))
    sb_h1 = sD.enter_context(nc.sbuf_tensor("d_h1", [128, 32, 512], BF16))
    sb_h2 = sD.enter_context(nc.sbuf_tensor("d_h2", [128, 32, 512], BF16))
    sb_wr2 = [sD.enter_context(nc.sbuf_tensor(f"d_wr2{i}", [128, 32, 128], BF16))
              for i in range(2)]
    sb_lg = sD.enter_context(nc.sbuf_tensor("d_lg", [128, 8], F32))
    sb_pp = sD.enter_context(nc.sbuf_tensor("d_pp", [128, 8], F32))
    sb_ptok = sD.enter_context(nc.sbuf_tensor("d_ptok", [128, 8], F32))

    gp(lambda e: e.memset(sb_o8[:, :], 1.0),
       wait=(("ve", ve_xln), ("Spl", spl_tot)))
    g_o8 = b.now("gp")
    r1_tot = {}
    r1_fin = {}

    def issue_r1(mg):
        p = mg % 2
        w = []
        if mg >= 2:
            w.append(("pe", r1_fin[mg - 2]))
        else:
            w += [("ve", ve_xln), ("Spl", spl_tot)]
        r1_tot[mg] = b.dma("gp", ("Wr1a", "Wr1b")[p], sb_wr1s[p][:, :, :],
                           wr1[mg % 32, :, :, :], wait=tuple(w))

    issue_r1(0)
    issue_r1(1)
    r2_tot = {}
    r2_fin = {}

    def issue_r2(mg):
        p = mg % 2
        w = []
        if mg >= 2:
            w.append(("pe", r2_fin[mg - 2]))
        else:
            w += [("ve", ve_xln), ("Spl", spl_tot)]
        r2_tot[mg] = b.dma("gp", ("Wr2a", "Wr2b")[p], sb_wr2[p][:, :, :],
                           wr2[mg % 32, :, :, :], wait=tuple(w))

    issue_r2(0)
    issue_r2(1)

    def idx_part(slsel, i16t, region, blk, base_wait):
        si_, ps_, pw_ = ring.acquire()
        pe(lambda e, o=ps_[0:16, 0:128], s=slsel[:, :]:
           e.transpose(o, s, sb_idf[:, :]),
           wait=pw_ + base_wait, inc=(("pe", 1),))
        pe_t = b.now("pe")
        ve(lambda e, s=ps_[0:16, 0:128]: e.tensor_copy(sb_t16[:, :], s),
           wait=(("pe", pe_t), ("Idx", b.now("Idx"))))
        v_t = b.now("ve")
        ring.release(si_, "ve")
        dstv = idx_d[region, :, :].rearrange("(a s) j -> a s j", s=8)
        srcv = sb_t16[:, :].rearrange("a (s q) -> a s q", q=16)
        t_ = 0
        for r in range(8):
            t_ = b.dma("sy", "Idx",
                       dstv[blk * 4:(blk + 1) * 4, :, 16 * r:16 * (r + 1)],
                       srcv[blk * 4:(blk + 1) * 4, :, :],
                       wait=(("ve", v_t),) if r == 0 else ())
        b.emit("sy", lambda e, d=i16t[:, blk * 32:(blk + 1) * 32],
               s=idx_d[region, blk * 32:(blk + 1) * 32, :]:
               e.dma_start_transpose(d, s),
               wait=(("Idx", t_),), inc=(("Idx", 16),), async_op=True)
        return b.now("Idx")

    for blk in range(4):
        for j in range(4):
            tc = blk * 4 + j
            p = j % 2
            pt = (pT1, pT2)[p]
            for c in range(8):
                w = (("ve", ve_xln), ("ve", ve_pt_free[p])) if c == 0 else ()
                pe(lambda e, o=pt[:, c * 128:(c + 1) * 128],
                   s=sb_xtmb[:, tc, c * 128:(c + 1) * 128]:
                   e.transpose(o, s, sb_idb[:, :]),
                   wait=w, inc=((("pe", 1),) if c == 7 else ()))
            pe_tr = b.now("pe")
            ve(lambda e, j=j, pt=pt: e.tensor_copy(
                sb_xfm[:, :, j * 128:(j + 1) * 128],
                pt[:, :].rearrange("p (c t) -> p c t", c=8)),
               wait=(("pe", pe_tr),), inc=(("ve", 1),))
            ve_pt_free[p] = b.now("ve")
        ve_xfm = b.now("ve")
        for m in range(32):
            mg1 = blk * 32 + m
            p1 = mg1 % 2
            sem1 = ("Wr1a", "Wr1b")[p1]
            si, ps, pw = ring.acquire()
            for k in range(8):
                w = pw + ((sem1, r1_tot[mg1]), ("ve", ve_xfm),
                          ("Dbg", dbg_tot)) if k == 0 else ()
                pe(lambda e, o=ps[:, 0:512], wt=sb_wr1s[p1][:, k, :],
                   rh=sb_xfm[:, k, :], st=(k == 0), sp=(k == 7):
                   e.matmul(o, wt, rh, start=st, stop=sp),
                   wait=w, inc=((("pe", 1),) if k == 7 else ()))
            mm_done = b.now("pe")
            r1_fin[mg1] = mm_done
            se(lambda e, m=m, s=ps[:, 0:512], bi=sb_rb1[:, m:m + 1]:
               e.activation(sb_h1[:, m, :], s, AF.Relu, bias=bi),
               wait=(("pe", mm_done),))
            ring.release(si, "se")
            if mg1 + 2 < 128:
                issue_r1(mg1 + 2)
        se_h1 = b.now("se")
        for m in range(32):
            mg = blk * 32 + m
            p = mg % 2
            sem = ("Wr2a", "Wr2b")[p]
            si, ps, pw = ring.acquire()
            for k in range(32):
                w = pw + ((sem, r2_tot[mg]), ("se", se_h1)) if k == 0 else ()
                pe(lambda e, o=ps[:, 0:512], wt=sb_wr2[p][:, k, :],
                   rh=sb_h1[:, k, :], st=(k == 0), sp=(k == 31):
                   e.matmul(o, wt, rh, start=st, stop=sp),
                   wait=w, inc=((("pe", 1),) if k == 31 else ()))
            mm_done = b.now("pe")
            r2_fin[mg] = mm_done
            se(lambda e, m=m, s=ps[:, 0:512], bi=sb_rb2[:, m:m + 1]:
               e.activation(sb_h2[:, m, :], s, AF.Relu, bias=bi),
               wait=(("pe", mm_done),))
            ring.release(si, "se")
            if mg + 2 < 128:
                issue_r2(mg + 2)
        se_h2 = b.now("se")
        if dbg and blk == 0:
            for m_ in range(32):
                dbg_tot = b.dma("gp", "Dbg", dbg_t["dbh1"][:, m_, :], sb_h1[:, m_, :],
                                wait=(("se", se_h2),))
                dbg_tot = b.dma("gp", "Dbg", dbg_t["dbh2"][:, m_, :], sb_h2[:, m_, :])
        for j in range(4):
            tc = blk * 4 + j
            si, ps, pw = ring.acquire()
            for k in range(32):
                w = pw + (("se", se_h2), ("Wmisc", wmisc)) if k == 0 else ()
                pe(lambda e, o=ps[:, 0:8], wt=sb_h2[:, k, j * 128:(j + 1) * 128],
                   rh=sb_wr3[:, k, :], st=(k == 0), sp=(k == 31):
                   e.matmul(o, wt, rh, start=st, stop=sp),
                   wait=w, inc=((("pe", 1),) if k == 31 else ()))
            mm_done = b.now("pe")
            ve(lambda e, s=ps[:, 0:8]: e.tensor_add(sb_lg[:, :], s, sb_rb3[:, :]),
               wait=(("pe", mm_done), ("gp", g_pro)), inc=(("ve", 1),))
            ring.release(si, "ve")
            ve(lambda e: e.reduce_max(COL[0][:, :], sb_lg[:, :], axis=AXX, negate=True),
               inc=(("ve", 1),))
            v_nm = b.now("ve")
            se(lambda e: e.activation(sb_pp[:, :], sb_lg[:, :], AF.Exp,
                                      bias=COL[0][:, :]),
               wait=(("ve", v_nm), ("Dbg", dbg_tot)))
            se_ex = b.now("se")
            ve(lambda e: e.reduce_sum(COL[1][:, :], sb_pp[:, :], axis=AXX),
               wait=(("se", se_ex),))
            ve(lambda e: e.reciprocal(COL[2][:, :], COL[1][:, :]))
            ve(lambda e: e.tensor_scalar_mul(sb_pp[:, :], sb_pp[:, :], COL[2][:, :]))
            if dbg:
                dbg_tot = b.dma("gp", "Dbg", dbg_t["dbpp"][:, tc, :], sb_pp[:, :],
                                wait=(("ve", b.now("ve")),))
            ve(lambda e, tc=tc: e.reduce_max(sb_s0[:, tc:tc + 1], sb_pp[:, :], axis=AXX))
            ve(lambda e, tc=tc: e.tensor_scalar(ROW8[0][:, :], sb_pp[:, :],
                                                sb_s0[:, tc:tc + 1], None, OP.is_equal))
            ve(lambda e: e.tensor_scalar(ROW8[1][:, :], ROW8[0][:, :], -BIG, BIG,
                                         OP.mult, OP.add))
            ve(lambda e: e.tensor_add(ROW8[1][:, :], ROW8[1][:, :], sb_io8[:, :]))
            ve(lambda e, tc=tc: e.tensor_reduce(sb_e0[:, tc:tc + 1], ROW8[1][:, :],
                                                AXX, OP.min))
            ve(lambda e, tc=tc: e.tensor_scalar(sb_oh0[:, tc, :], sb_io8[:, :],
                                                sb_e0[:, tc:tc + 1], None, OP.is_equal))
            ve(lambda e, tc=tc: e.tensor_scalar(ROW8[1][:, :], sb_oh0[:, tc, :],
                                                -BIG, None, OP.mult))
            ve(lambda e: e.tensor_add(ROW8[1][:, :], ROW8[1][:, :], sb_pp[:, :]))
            ve(lambda e, tc=tc: e.reduce_max(sb_s1[:, tc:tc + 1], ROW8[1][:, :],
                                             axis=AXX))
            ve(lambda e, tc=tc: e.tensor_scalar(ROW8[2][:, :], ROW8[1][:, :],
                                                sb_s1[:, tc:tc + 1], None, OP.is_equal))
            ve(lambda e: e.tensor_scalar(ROW8[2][:, :], ROW8[2][:, :], -BIG, BIG,
                                         OP.mult, OP.add))
            ve(lambda e: e.tensor_add(ROW8[2][:, :], ROW8[2][:, :], sb_io8[:, :]))
            ve(lambda e, tc=tc: e.tensor_reduce(sb_e1[:, tc:tc + 1], ROW8[2][:, :],
                                                AXX, OP.min))
            ve(lambda e, tc=tc: e.tensor_scalar(sb_oh1[:, tc, :], sb_io8[:, :],
                                                sb_e1[:, tc:tc + 1], None, OP.is_equal))
            ve(lambda e, tc=tc: e.tensor_add(ROW8[3][:, :], sb_oh0[:, tc, :],
                                             sb_oh1[:, tc, :]), inc=(("ve", 1),))
            v_m = b.now("ve")
            si2, ps2, pw2 = ring.acquire()
            pe(lambda e, o=ps2[0:8, 0:128], s=ROW8[3][:, :]:
               e.transpose(o, s, sb_idf[:, :]),
               wait=pw2 + (("ve", v_m),), inc=(("pe", 1),))
            pe_mt = b.now("pe")
            ve(lambda e, tc=tc, s=ps2[0:8, 0:128]:
               e.tensor_copy(sb_mem[0:8, tc * 128:(tc + 1) * 128], s),
               wait=(("pe", pe_mt),), inc=(("ve", 1),))
            ring.release(si2, "ve")
        ve_mem = b.now("ve")
        seg = slice(blk * 512, (blk + 1) * 512)
        init = 0.0 if blk == 0 else sb_pin[0:8, 0:1]
        ve(lambda e, seg=seg, init=init: e.tensor_tensor_scan(
            sb_pincl[0:8, seg], sb_o8[0:8, seg], sb_mem[0:8, seg],
            init, OP.mult, OP.add), wait=(("ve", ve_mem), ("gp", g_o8)))
        ve(lambda e, seg=seg: e.tensor_sub(sb_pem[0:8, seg], sb_pincl[0:8, seg],
                                           sb_mem[0:8, seg]))
        ve(lambda e, blk=blk: e.tensor_copy(
            sb_pin[0:8, 0:1], sb_pincl[0:8, (blk + 1) * 512 - 1:(blk + 1) * 512]),
           inc=(("ve", 1),))
        ve_pscan = b.now("ve")
        for j in range(4):
            tc = blk * 4 + j
            si2, ps2, pw2 = ring.acquire()
            pe(lambda e, o=ps2[:, 0:8], s=sb_pem[0:8, tc * 128:(tc + 1) * 128]:
               e.transpose(o, s, sb_idf[0:8, 0:8]),
               wait=pw2 + (("ve", ve_pscan),), inc=(("pe", 1),))
            pe_pt = b.now("pe")
            ve(lambda e, s=ps2[:, 0:8]: e.tensor_copy(sb_ptok[:, :], s),
               wait=(("pe", pe_pt),), inc=(("ve", 1),))
            ring.release(si2, "ve")
            for oh, pssel, esel, slsel, tval in [
                    (sb_oh0, sb_ps0, sb_e0, sb_sl0, float(TRASH)),
                    (sb_oh1, sb_ps1, sb_e1, sb_sl1, float(TRASH + 1))]:
                ve(lambda e, oh=oh, tc=tc: e.tensor_mul(ROW8[0][:, :],
                                                        sb_ptok[:, :], oh[:, tc, :]))
                ve(lambda e, p_=pssel, tc=tc: e.reduce_sum(p_[:, tc:tc + 1],
                                                           ROW8[0][:, :], axis=AXX))
                ve(lambda e, p_=pssel, e_=esel, s_=slsel, tc=tc:
                   e.scalar_tensor_tensor(s_[:, tc:tc + 1], e_[:, tc:tc + 1],
                                          float(CAP), p_[:, tc:tc + 1],
                                          OP.mult, OP.add))
                ve(lambda e, p_=pssel, tc=tc: e.tensor_scalar(
                    COL[4][:, :], p_[:, tc:tc + 1], float(CAP), None, OP.is_lt))
                ve(lambda e, s_=slsel, tc=tc, tv=tval: e.scalar_tensor_tensor(
                    COL[5][:, :], s_[:, tc:tc + 1], tv, COL[4][:, :],
                    OP.subtract, OP.mult))
                ve(lambda e, s_=slsel, tc=tc, tv=tval: e.tensor_scalar(
                    s_[:, tc:tc + 1], COL[5][:, :], tv, None, OP.add))
        v_blk_sl = b.now("ve")
        for i16t_, sl_, reg_ in ((sb_i16a, sb_sl0, 0), (sb_i16b, sb_sl1, 1)):
            iw = idx_part(sl_, i16t_, reg_, blk, (("ve", v_blk_sl),))
            b.emit("gp", lambda e, t=i16t_[:, blk * 32:(blk + 1) * 32], blk=blk:
                   e.dma_scatter_add(disp_d[:, :],
                                     sb_xtmb[:, blk * 4:(blk + 1) * 4, :],
                                     t, 512, 512, D),
                   wait=(("Idx", iw), ("Zer", zer_tot), ("ve", ve_xln)),
                   inc=(("Sca", 16),), async_op=True)
    ve_route = b.now("ve")
    if dbg:
        dbg_tot = b.dma("gp", "Dbg", dbg_t["d_pem"][:, :], sb_pem[0:8, :],
                        wait=(("ve", ve_route),))
        for j, tile in enumerate([sb_s0, sb_s1, sb_e0, sb_e1, sb_ps0, sb_ps1,
                                  sb_sl0, sb_sl1]):
            dbg_tot = b.dma("gp", "Dbg", dbg_t["d_sel"][:, :, j], tile[:, :],
                            wait=(("ve", ve_route),), slow=True)

    cc3 = b.dma("gp", "CC", cs_cnt.ap()[:], sb_pincl[0:8, T - 1:T],
                wait=(("ve", ve_route),))
    b.emit("gp", lambda e: e.collective_compute(
        "AllGather", OP.bypass, replica_groups=RG_PAIRS,
        ins=[cs_cnt.ap().opt()], outs=[cg_cnt.ap().opt()]),
        wait=(("CC", cc3),), inc=(("CCc", 1),), async_op=True)
    cc4 = b.now("CCc")
    cc5 = b.dma("gp", "CC", sb_cnt[0:8, 0:1], cg_cnt.ap()[0, :], wait=(("CCc", cc4),))
    ve(lambda e: e.tensor_scalar_mul(sb_cnt[0:8, 0:1], sb_cnt[0:8, 0:1],
                                     sb_bf[0:8, 0:1]),
       wait=(("CC", cc5),), inc=(("ve", 1),))
    v_cnt = b.now("ve")
    si2, ps2, pw2 = ring.acquire()
    pe(lambda e, o=ps2[0:1, 0:8]: e.transpose(o, sb_cnt[0:8, 0:1], sb_idf[0:8, 0:8]),
       wait=pw2 + (("ve", v_cnt),), inc=(("pe", 1),))
    pe_ct = b.now("pe")
    ve(lambda e, s=ps2[0:1, 0:8]: e.tensor_copy(sb_crow[0:1, :], s),
       wait=(("pe", pe_ct),), inc=(("ve", 1),))
    v_cr = b.now("ve")
    ring.release(si2, "ve")
    si2, ps2, pw2 = ring.acquire()
    pe(lambda e, o=ps2[:, 0:8]: e.matmul(o, sb_ones[0:1, :], sb_crow[0:1, :],
                                         start=True, stop=True),
       wait=pw2 + (("ve", v_cr), ("gp", g_pro)), inc=(("pe", 1),))
    pe_cb = b.now("pe")
    ve(lambda e, s=ps2[:, 0:8]: e.tensor_copy(sb_cntbc[:, :], s),
       wait=(("pe", pe_cb),), inc=(("ve", 1),))
    ring.release(si2, "ve")
    v_cbc = b.now("ve")


    def build_idx(slsel, i16t, region, base_wait):
        si_, ps_, pw_ = ring.acquire()
        pe(lambda e, o=ps_[0:16, 0:128], s=slsel[:, :]:
           e.transpose(o, s, sb_idf[:, :]),
           wait=pw_ + base_wait, inc=(("pe", 1),))
        pe_t = b.now("pe")
        ve(lambda e, s=ps_[0:16, 0:128]: e.tensor_copy(sb_t16[:, :], s),
           wait=(("pe", pe_t), ("Idx", b.now("Idx"))))
        v_t = b.now("ve")
        ring.release(si_, "ve")
        dst = idx_d[region, :, :].rearrange("(a s) (r q) -> a s r q", s=8, q=16)
        t_ = 0
        for r in range(8):
            t_ = b.dma("sy", "Idx",
                       dst[:, :, r, :].rearrange("a s q -> a s q") if False else
                       idx_d[region, :, :].rearrange("(a s) j -> a s j", s=8)[:, :, 16 * r:16 * (r + 1)],
                       sb_t16[:, :].rearrange("a (s q) -> a s q", q=16),
                       wait=(("ve", v_t),) if r == 0 else ())
        b.emit("sy", lambda e, d=i16t[:, :], s=idx_d[region, :, :]:
               e.dma_start_transpose(d, s),
               wait=(("Idx", t_),), inc=(("Idx", 16),), async_op=True)
        return b.now("Idx")

    sca_tot = b.now("Sca")
    pe_selidx = b.now("pe")
    for esel, pssel, ssel, slsel, wsel in [
            (sb_e0, sb_ps0, sb_s0, sb_sl0, sb_w0),
            (sb_e1, sb_ps1, sb_s1, sb_sl1, sb_w1)]:
        ve(lambda e, es=esel: e.tensor_scalar(sb_m16a[:, :], es[:, :], 0.0,
                                              None, OP.is_equal),
           wait=(("ve", v_cbc), ("pe", pe_selidx)))
        ve(lambda e: e.tensor_scalar_mul(sb_m16b[:, :], sb_m16a[:, :],
                                         sb_cntbc[:, 0:1]))
        for ee in range(1, 8):
            ve(lambda e, es=esel, ee=ee: e.tensor_scalar(
                sb_m16a[:, :], es[:, :], float(ee), None, OP.is_equal))
            ve(lambda e, ee=ee: e.scalar_tensor_tensor(
                sb_m16b[:, :], sb_m16a[:, :], sb_cntbc[:, ee:ee + 1],
                sb_m16b[:, :], OP.mult, OP.add))
        ve(lambda e, p_=pssel: e.tensor_add(sb_m16b[:, :], sb_m16b[:, :], p_[:, :]))
        ve(lambda e: e.tensor_scalar(sb_m16a[:, :], sb_m16b[:, :], float(CAP),
                                     None, OP.is_lt))
        ve(lambda e, s_=ssel, w_=wsel: e.tensor_mul(w_[:, :], s_[:, :],
                                                    sb_m16a[:, :]))
        ve(lambda e, sl=slsel: e.scalar_tensor_tensor(
            sb_m16b[:, :], sl[:, :], float(TRASH), sb_m16a[:, :],
            OP.subtract, OP.mult))
        ve(lambda e, sl=slsel: e.tensor_scalar(
            sl[:, :], sb_m16b[:, :], float(TRASH), None, OP.add))
    v_keep = b.now("ve")
    if upto == 'F0':
        return _finish_early()
    gidx_tot = 0
    for sl, i16t, reg in [(sb_sl0, sb_i16c, 2), (sb_sl1, sb_i16d, 3)]:
        gidx_tot = build_idx(sl, i16t, reg, (("ve", v_keep),))
    if dbg:
        dbg_tot = b.dma("gp", "Dbg", dbg_t["d_disp"][:, :], disp_d[:, :],
                        wait=(("Sca", sca_tot),))
    pe_router_done = b.now("pe")
    _stacks.remove(sD)
    sD.close()
    _stacks.remove(sCD)
    sCD.close()
    if upto == 'D':
        return _finish_early()

    sE = ExitStack()
    _stacks.append(sE)
    sb_dtm = [sE.enter_context(nc.sbuf_tensor(f"e_dt{i}", [128, 5, D], BF16))
              for i in range(2)]
    sb_dsp = [sE.enter_context(nc.sbuf_tensor(f"e_d{i}", [128, 8, CAP], BF16))
              for i in range(2)]
    sb_we1 = [sE.enter_context(nc.sbuf_tensor(f"e_w1{i}", [128, 4, 8, 128], BF16))
              for i in range(2)]
    sb_ew2 = sE.enter_context(nc.sbuf_tensor("e_w2", [128, 32, D], BF16))
    sb_eh = sE.enter_context(nc.sbuf_tensor("e_eh", [128, 32, CAP], BF16))
    sb_eb2c = sE.enter_context(nc.sbuf_tensor("e_b2", [128, D], F32))
    sb_eob = [sE.enter_context(nc.sbuf_tensor(f"e_o{i}", [128, D], BF16))
              for i in range(2)]

    dl_tot = {}
    dl_free = {-2: 0, -1: 0}

    def issue_disp_t(e_):
        p = e_ % 2
        w = [("Sca", sca_tot)]
        if e_ >= 2:
            w.append(("pe", dl_free[e_ - 2]))
        dl_tot[e_] = b.dma(
            "sy", SEM_DL[p], sb_dtm[p][:, :, :],
            disp_d[e_ * CAP:(e_ + 1) * CAP, :].rearrange("(sc p) c -> p sc c", p=128),
            wait=tuple(w))

    fm_free = {-2: 0, -1: 0}

    def disp_transpose(e_):
        p = e_ % 2
        dsem = SEM_DL[p]
        for sc in range(5):
            tp_ = (pT1, pT2)[sc % 2]
            for c in range(8):
                w = ((dsem, dl_tot[e_]), ("ve", ve_pt_free[sc % 2]),
                     ("pe", 0)) if c == 0 else ()
                pe(lambda e, o=tp_[:, c * 128:(c + 1) * 128],
                   s=sb_dtm[p][:, sc, c * 128:(c + 1) * 128]:
                   e.transpose(o, s, sb_idb[:, :]),
                   wait=w, inc=((("pe", 1),) if c == 7 else ()))
            pe_tr = b.now("pe")
            ve(lambda e, sc=sc, tp_=tp_, p=p: e.tensor_copy(
                sb_dsp[p][:, :, sc * 128:(sc + 1) * 128],
                tp_[:, :].rearrange("q (c t) -> q c t", c=8)),
               wait=(("pe", pe_tr), ("pe", fm_free[e_ - 2])))
            ve_pt_free[sc % 2] = b.now("ve")
        dl_free[e_] = b.now("pe")

    we1_tot = {}

    we1_fin = {}

    def issue_we1(g):
        p = g % 2
        w = (("pe", we1_fin[g - 2]),) if g >= 2 else (("pe", pe_router_done),)
        we1_tot[g] = b.dma("gp", ("We1a", "We1b")[p], sb_we1[p][:, :, :, :],
                           we1[g // 8, g % 8, :, :, :, :], wait=w)

    issue_disp_t(0)
    issue_we1(0)
    issue_we1(1)
    we2_tot = {}
    eo_pe_done = {-1: pe_router_done}
    eo_ve_last = {-1: 0}
    eh_free = {0: 0, 1: 0}
    eo_ve_free = 0
    eob_free = {0: 0, 1: 0}
    ring_drain = ring.drain_waits()
    eo_tot = 0
    for e_ in range(NE):
        p = e_ % 2
        dsem = SEM_DL[p]
        we2_tot[e_] = b.dma("gp", "We2", sb_ew2[:, :, :], we2[e_, :, :, :],
                            wait=(("pe", eo_pe_done[e_ - 1]),))
        eb2_tot = b.dma("sy", "Web2", sb_eb2c[:, :], eb2[e_, :, :],
                        wait=(("pe", eo_pe_done[e_ - 1]), ("ve", eo_ve_last[e_ - 1])))
        if e_ + 1 < NE:
            issue_disp_t(e_ + 1)
        disp_transpose(e_)
        ve_dsp = b.now("ve")
        for mb in range(8):
            g = e_ * 8 + mb
            wp = g % 2
            wsem = ("We1a", "We1b")[wp]
            for m4 in range(4):
                m = mb * 4 + m4
                pp = (pA, pB)[m % 2]
                first_w = ((wsem, we1_tot[g]), ("ve", ve_dsp),
                           ("se", eh_free[m % 2])) + \
                    (ring_drain if (e_ == 0 and mb == 0 and m4 == 0) else ())
                first = True
                for k in range(8):
                    for (lo, wdt) in ((0, 512), (512, 128)):
                        pe(lambda e, o=pp[:, lo:lo + wdt], wt=sb_we1[wp][:, m4, k, :],
                           rh=sb_dsp[p][:, k, lo:lo + wdt], st=(k == 0), sp=(k == 7):
                           e.matmul(o, wt, rh, start=st, stop=sp),
                           wait=first_w if first else (),
                           inc=((("pe", 1),) if (k == 7 and wdt == 128) else ()))
                        first = False
                mm_done = b.now("pe")
                se(lambda e, m=m, ee=e_, s=pp[:, 0:CAP]:
                   e.activation(sb_eh[:, m, :], s, AF.Relu,
                                bias=sb_eb1[:, ee, m:m + 1]),
                   wait=(("pe", mm_done),))
                eh_free[m % 2] = b.now("se")
            we1_fin[g] = b.now("pe")
            if g + 2 < NE * 8:
                issue_we1(g + 2)
        se_eh = b.now("se")
        fm_free[e_] = b.now("pe")
        for n_c in range(5):
            first = True
            for m in range(32):
                for (lo, wdt) in ((0, 512), (512, 512)):
                    w = (("We2", we2_tot[e_]), ("se", se_eh),
                         ("ve", eo_ve_free)) if first else ()
                    pe(lambda e, o=pC[:, lo:lo + wdt],
                       wt=sb_eh[:, m, n_c * 128:(n_c + 1) * 128],
                       rh=sb_ew2[:, m, lo:lo + wdt], st=(m == 0), sp=(m == 31):
                       e.matmul(o, wt, rh, start=st, stop=sp),
                       wait=w,
                       inc=((("pe", 1),) if (m == 31 and lo == 512) else ()))
                    first = False
            mm_done = b.now("pe")
            op = n_c % 2
            ve(lambda e, o=sb_eob[op][:, :]: e.tensor_add(o, pC[:, :], sb_eb2c[:, :]),
               wait=(("pe", mm_done), ("Web2", eb2_tot), (SEM_EO[op], eob_free[op])),
               inc=(("ve", 1),))
            eo_ve_free = b.now("ve")
            eo_tot = b.dma("sy", SEM_EO[op],
                           eo_d[e_ * CAP + n_c * 128:e_ * CAP + (n_c + 1) * 128, :],
                           sb_eob[op][:, :], wait=(("ve", eo_ve_free),))
            eob_free[op] = eo_tot
        eo_pe_done[e_] = b.now("pe")
        eo_ve_last[e_] = b.now("ve")
    eo_all = (("Eo0", b.now("Eo0")), ("Eo1", b.now("Eo1")))
    if dbg:
        dbg_tot = b.dma("gp", "Dbg", dbg_t["d_eo"][:, :], eo_d[:, :], wait=eo_all)
    _stacks.remove(sE)
    sE.close()
    if upto == 'E':
        return _finish_early()

    sF = ExitStack()
    _stacks.append(sF)
    sb_g0 = sF.enter_context(nc.sbuf_tensor("f_g0", [128, 16, D], BF16))
    sb_g1 = sF.enter_context(nc.sbuf_tensor("f_g1", [128, 16, D], BF16))
    sb_xr = [sF.enter_context(nc.sbuf_tensor(f"f_xr{i}", [128, D], F32))
             for i in range(2)]
    sb_y2 = [sF.enter_context(nc.sbuf_tensor(f"f_y2{i}", [128, D], F32))
             for i in range(2)]
    sb_sq2 = [sF.enter_context(nc.sbuf_tensor(f"f_sq{i}", [128, D], F32))
              for i in range(2)]
    sb_yo = [sF.enter_context(nc.sbuf_tensor(f"f_yo{i}", [128, D], F32))
             for i in range(2)]

    for i16t, gdst in ((sb_i16c, sb_g0), (sb_i16d, sb_g1)):
        for j in range(4):
            b.emit("gp", lambda e, t=i16t[:, 32 * j:32 * (j + 1)],
                   d=gdst[:, 4 * j:4 * (j + 1), :]:
                   e.dma_gather(d, eo_d[:, :], t, 512, 512, D),
                   wait=(("Idx", gidx_tot),) + eo_all +
                        ((("Dbg", dbg_tot),) if dbg else ()),
                   inc=(("Gat", 16),), async_op=True)
    gat_tot = b.now("Gat")
    if dbg:
        for tc in range(16):
            b.dma("gp", "Dbg", dbg_t["d_g01"][0, :, tc, :], sb_g0[:, tc, :],
                  wait=(("Gat", gat_tot),))
            dbg_tot = b.dma("gp", "Dbg", dbg_t["d_g01"][1, :, tc, :], sb_g1[:, tc, :])

    pe_E_done = b.now("pe")
    if upto == 'F1':
        return _finish_early()
    xr_tot = {}

    xr_fin = {}

    def issue_xr(i):
        p = i % 2
        w = (("Spl", spl_tot), ("pe", pe_E_done)) if i < 2 else             (("ve", xr_fin[i - 2]),)
        xr_tot[i] = b.dma("sy", SEM_RL[p], sb_xr[p][:, :],
                          xln_d[i * 128:(i + 1) * 128, :], wait=w)

    issue_xr(0)
    issue_xr(1)
    y2_free = {0: 0, 1: 0}
    se_sumsF = {}

    def f_phase1(i):
        p = i % 2
        y2 = sb_y2[p]
        se(lambda e, i=i, y=y2: e.activation(y[:, :], sb_g1[:, i, :], AF.Copy,
                                             scale=sb_w1[:, i:i + 1]),
           wait=(("Gat", gat_tot), ("ve", v_keep), ("ve", y2_free[p])),
           inc=(("se", 1),))
        s_cp = b.now("se")
        ve(lambda e, i=i, y=y2: e.scalar_tensor_tensor(y[:, :], sb_g0[:, i, :],
                                                       sb_w0[:, i:i + 1], y[:, :],
                                                       OP.mult, OP.add),
           wait=(("se", s_cp),))
        ve(lambda e, p=p, y=y2: e.tensor_add(y[:, :], y[:, :], sb_xr[p][:, :]),
           wait=((SEM_RL[p], xr_tot[i]),), inc=(("ve", 1),))
        xr_fin[i] = b.now("ve")
        se(lambda e, y=y2, sq=sb_sq2[p], c=LN8(p, 0):
           e.activation(sq[:, :], y[:, :], AF.Copy, accum_out=c),
           wait=(("ve", xr_fin[i]),))
        se(lambda e, y=y2, sq=sb_sq2[p], c=LN8(p, 1):
           e.activation(sq[:, :], y[:, :], AF.Square, accum_out=c),
           inc=(("se", 1),))
        se_sumsF[i] = b.now("se")
        if i + 2 < 16:
            issue_xr(i + 2)

    def f_phase2(i):
        p = i % 2
        y2 = sb_y2[p]
        c0, c1, c2, c3 = LN8(p, 0), LN8(p, 1), LN8(p, 2), LN8(p, 3)
        ve(lambda e, c=c0: e.tensor_scalar_mul(c, c, -1.0 / D),
           wait=(("se", se_sumsF[i]),))
        ve(lambda e, c=c1: e.tensor_scalar(c, c, 1.0 / D, EPS, OP.mult, OP.add))
        ve(lambda e, a=c2, c=c0: e.tensor_mul(a, c, c))
        ve(lambda e, a=c1, bq=c2: e.tensor_sub(a, a, bq))
        ve(lambda e, a=c2, c=c1: e.reciprocal(a, c), inc=(("ve", 1),))
        v_s = b.now("ve")
        se(lambda e, a=c3, c=c2: e.activation(a, c, AF.Sqrt),
           wait=(("ve", v_s),), inc=(("se", 1),))
        se_i = b.now("se")
        ve(lambda e, p=p, y=y2, c0=c0, c3=c3: e.tensor_scalar(
            sb_yo[p][:, :], y[:, :], c0, c3, OP.add, OP.mult),
           wait=(("se", se_i), (SEM_OUT[p], b.now(SEM_OUT[p]))), inc=(("ve", 1),))
        v_yo = b.now("ve")
        y2_free[p] = v_yo
        b.dma("sy", SEM_OUT[p], out_t[i * 128:(i + 1) * 128, :], sb_yo[p][:, :],
              wait=(("ve", v_yo),))

    f_phase1(0)
    for i in range(1, 16):
        f_phase1(i)
        f_phase2(i - 1)
    f_phase2(15)
    fw = [("Out0", b.now("Out0")), ("Out1", b.now("Out1"))]
    if dbg:
        fw.append(("Dbg", b.now("Dbg")))
    b.emit("sy", lambda e: e.nop(), wait=tuple(fw))
    _stacks.remove(sF)
    sF.close()

    b.build()
    es.close()
    nc.finalize()
    return nc


def stage_inputs(inputs):
    f32 = np.float32
    g = {k: np.asarray(v, f32) for k, v in inputs.items()}
    x = g["L0_attn_x"]
    cc = np.ascontiguousarray
    chunkcol = lambda v, n: cc(v.reshape(n, 128).T)
    com = {
        "wproj": cc(g["L0_attn_proj_w"].T.reshape(8, 128, 24, 128)
                    .transpose(2, 1, 0, 3)),
        "convw": cc(g["L0_attn_conv_w"][:, 0, :].reshape(12, 128, 4)
                    .transpose(1, 0, 2)),
        "convb": chunkcol(g["L0_attn_conv_b"], 12),
        "fl": chunkcol(g["L0_attn_forget_lambda"], 12),
        "wgates": cc(np.stack([g["L0_attn_gp_w"].T, g["L0_attn_ip_w"].T])
                     .reshape(2, 12, 128, 12, 128).transpose(3, 2, 0, 1, 4)),
        "gpb": chunkcol(g["L0_attn_gp_b"], 12),
        "ipb": chunkcol(g["L0_attn_ip_b"], 12),
        "wout": cc(g["L0_attn_out_w"].T.reshape(12, 128, 8, 128)
                   .transpose(2, 1, 0, 3)),
        "wr1": cc(g["L0_ffn_rW1"].reshape(8, 128, 32, 128).transpose(2, 1, 0, 3)),
        "rb1": chunkcol(g["L0_ffn_rb1"], 32),
        "wr2": cc(g["L0_ffn_rW2"].reshape(32, 128, 32, 128).transpose(2, 1, 0, 3)),
        "rb2": chunkcol(g["L0_ffn_rb2"], 32),
        "wr3": cc(g["L0_ffn_rW3"].reshape(32, 128, 8).transpose(1, 0, 2)),
        "rb3": cc(np.broadcast_to(g["L0_ffn_rb3"], (128, 8))),
        "we1": cc(g["L0_ffn_eW1"].reshape(NE, 8, 128, 8, 4, 128)
                  .transpose(0, 3, 2, 4, 1, 5)),
        "eb1": cc(g["L0_ffn_eb1"].reshape(NE, 32, 128).transpose(0, 2, 1)),
        "we2": cc(g["L0_ffn_eW2"].reshape(NE, 32, 128, D).transpose(0, 2, 1, 3)),
        "eb2": cc(np.broadcast_to(g["L0_ffn_eb2"][:, None, :], (NE, 128, D))),
    }
    in_maps = []
    for c in range(8):
        bb, h = c // 2, c % 2
        t0 = h * 2048
        xf = np.zeros((D, 2051), f32)
        if h == 0:
            xf[:, 3:] = x[bb, 0:2048].T
        else:
            xf[:, :] = x[bb, 2045:4096].T
        m = dict(com)
        m["x_fm"] = cc(xf)
        m["x_tm"] = cc(x[bb, t0:t0 + 2048])
        m["bflag"] = np.full((128, 1), float(h), f32)
        in_maps.append(m)
    return in_maps


def kernel(**inputs):
    in_maps = stage_inputs(inputs)
    nc = build_program(dbg=False)
    res = run_bass_kernel_spmd(nc, in_maps, core_ids=list(range(8)))
    out = np.empty((4, 4096, D), np.float32)
    for c in range(8):
        bb, h = c // 2, c % 2
        out[bb, h * 2048:(h + 1) * 2048, :] = res.results[c]["out"].reshape(T, D)
    return out
